# version 09
# baseline (speedup 1.0000x reference)
import numpy as np
import ml_dtypes
from contextlib import ExitStack, contextmanager

import concourse.bass as bass
import concourse.mybir as mybir
from concourse.bass_utils import run_bass_kernel_spmd

F32 = mybir.dt.float32
BF16 = mybir.dt.bfloat16
AF = mybir.ActivationFunctionType
ALU = mybir.AluOpType
AX = mybir.AxisListType.X

D = 2048
CI = 16
H = 16
CH = 128
TOK = 2048
HALF = 1024
TT = 512
NCH = 8
FH = 5632
HC = 44
INVD = 1.0 / 2048.0
INVDK = 1.0 / 128.0
EPS_LN = 1e-5
SCALE = 128.0 ** (-0.5)
NW_ELEMS = float(FH * D)
GP = 132


class WStream:

    def __init__(self, kb, S, segments):
        self.kb = kb
        self.wb = [kb.usb(S, "wsb%d" % i, [128, 4096], BF16, side="right")
                   for i in range(2)]
        self.segs = []
        for wst, npairs in segments:
            for j in range(npairs):
                self.segs.append((wst, j))
        self.cons = {}
        self.val = {}
        self.issued = 0
        self.cursor = 0

    def claim(self, npairs):
        base = self.cursor
        self.cursor += npairs
        return base

    def pump(self, g):
        kb = self.kb
        while self.issued < len(self.segs):
            p = self.issued
            gate = self.cons.get(p - 2)
            if p >= 2 and gate is None:
                return
            if gate is not None:
                kb.wt(g, *gate)
            wst, lp = self.segs[p]
            self.val[p] = kb.dtok(g.dma_start(
                out=self.wb[p % 2][:].rearrange("p (o c k) -> p o c k",
                                                o=2, k=128),
                in_=wst[2 * lp:2 * lp + 2].rearrange("o p c k -> p o c k")),
                "Ws%d" % (p % 2))
            self.issued += 1


class _GPump:

    def __init__(self, kb, S):
        self.kb = kb
        self.ring = {
            "s": [kb.usb(S, "gsa%d" % i, [128, 2048], F32, side="left")
                  for i in range(2)],
            "g": [kb.usb(S, "gsb%d" % i, [128, 2048], F32, side="left")
                  for i in range(2)],
        }
        self.junk = kb.usb(S, "gjunk", [128, 2048], BF16, side="left")
        p12 = [kb.g1[i] for i in range(44)] + [kb.g2[i] for i in range(44)]
        p3 = [kb.g3[i] for i in range(44)]
        ALL_SCALAR = True
        if ALL_SCALAR:
            self.pieces = {"s": p12 + p3, "g": []}
            self.cols = {"s": 0, "g": 88}
        else:
            self.pieces = {"s": p12, "g": p3}
            self.cols = {"s": 0, "g": 88}
        self.sem = {"s": ["GA0", "GA1"], "g": ["GB0", "GB1"]}
        self.ni = {"s": 0, "g": 0}
        self.nr = {"s": 0, "g": 0}

    def _issue(self, eng, kind):
        kb = self.kb
        i = self.ni[kind]
        if i >= len(self.pieces[kind]):
            return
        kb.dtok(eng.dma_start(out=self.ring[kind][i % 2][:],
                              in_=self.pieces[kind][i]),
                self.sem[kind][i % 2])
        self.ni[kind] += 1

    def prime(self):
        kb = self.kb
        with kb.block() as blk:
            @blk.scalar
            def _(s):
                self._issue(s, "s")
                self._issue(s, "s")

            @blk.gpsimd
            def _(g):
                self._issue(g, "g")
                self._issue(g, "g")

    def red(self, eng, kind, k=1):
        kb = self.kb
        for _ in range(k):
            i = self.nr[kind]
            if i >= len(self.pieces[kind]):
                return
            kb.wt(eng, self.sem[kind][i % 2], 16 * (i // 2 + 1))
            col = self.cols[kind] + i
            if kind == "s":
                eng.activation(self.junk[:], self.ring[kind][i % 2][:], AF.Abs,
                               accum_out=kb.gacc[:, col:col + 1])
            else:
                eng.tensor_scalar(self.junk[:], self.ring[kind][i % 2][:],
                                  0.0, 0.0, ALU.abs_max, ALU.add,
                                  accum_out=kb.gacc[:, col:col + 1])
            self.nr[kind] += 1
            self._issue(eng, kind)

    @property
    def done(self):
        return (self.nr["s"] + self.nr["g"] == GP)


class KB:
    def __init__(self, dbg=()):
        nc = bass.Bass("TRN2", target_bir_lowering=False, debug=False,
                       detect_race_conditions=False)
        self.nc = nc
        self.es = ExitStack()
        ap = lambda *a, **k: nc.dram_tensor(*a, **k).ap()
        self.xoT = ap("xoT", [D, TOK], F32, kind="ExternalInput")
        self.xpT = ap("xpT", [D, TOK], F32, kind="ExternalInput")
        self.wq = ap("wq", [CI, 128, CI, 128], F32, kind="ExternalInput")
        self.wk = ap("wk", [CI, 128, CI, 128], F32, kind="ExternalInput")
        self.wv = ap("wv", [CI, 128, CI, 128], F32, kind="ExternalInput")
        self.wo = ap("wo", [CI, 128, CI, 128], F32, kind="ExternalInput")
        self.w1s = ap("w1s", [HC, 128, CI, 128], F32, kind="ExternalInput")
        self.w2s = ap("w2s", [HC, 128, CI, 128], F32, kind="ExternalInput")
        self.w3s = ap("w3s", [CI, 128, HC, 128], F32, kind="ExternalInput")
        self.fmw = ap("fmw", [4, 128, 128], F32, kind="ExternalInput")
        self.fmb = ap("fmb", [5, 128], F32, kind="ExternalInput")
        self.cmask4 = ap("cmask4", [128, 512], F32, kind="ExternalInput")
        self.cident = ap("cident", [128, 128], F32, kind="ExternalInput")
        self.outT = ap("outT", [D, TOK], F32, kind="ExternalOutput")
        self.qw1_sp = ap("qw1_sp", [HC, 128, 2048], BF16, kind="Internal")
        self.qw2_sp = ap("qw2_sp", [HC, 128, 2048], BF16, kind="Internal")
        self.qw3_sp = ap("qw3_sp", [CI, 128, HC * 128], BF16, kind="Internal")
        self.dbg_outs = {}
        for i, (tag, shape) in enumerate(dbg):
            self.dbg_outs[tag] = ap(f"dbg{i}", list(shape), F32, kind="ExternalOutput")

        self.xoT_t = self.xoT.rearrange("(c p) t -> c p t", p=128)
        self.xpT_t = self.xpT.rearrange("(c p) t -> c p t", p=128)
        self.g1 = self.w1s.rearrange("n p a b -> n p (a b)")
        self.g2 = self.w2s.rearrange("n p a b -> n p (a b)")
        self.g3 = self.w3s.rearrange("a p c b -> (a p c b)") \
                          .rearrange("(n p c) -> n p c", p=128, c=2048)

        sbl = lambda n, sh, dt: self.es.enter_context(nc.sbuf_tensor(n, sh, dt, side="left"))
        self.ones_col = sbl("ones_col", [128, 1], BF16)
        self.ones_row = sbl("ones_row", [1, 128], BF16)
        self.ones_row_f = sbl("ones_row_f", [1, 128], F32)
        self.ones_col_f = sbl("ones_col_f", [128, 1], F32)
        self.ident = sbl("ident", [128, 128], BF16)
        self.ident_f = sbl("ident_f", [128, 128], F32)
        self.mask4 = sbl("mask4", [128, 512], BF16)
        self.fqk = sbl("fqk", [128, 4 * 128], BF16)
        self.pcol = sbl("pcol", [128, 16], F32)
        self.chalf = sbl("chalf", [128, 2], F32)
        self.rowf = sbl("rowf", [1, 5 * TT], F32)
        self.rowb = sbl("rowb", [1, 4 * TT], BF16)
        self.cst = sbl("cst", [1, 8], F32)
        self.S32 = sbl("S32", [128, H * 128], F32)
        self.Sbf = sbl("Sbf", [128, H * 128], BF16)
        self.gacc = sbl("gacc", [128, 144], F32)

        pst = lambda n, sh, dt: self.es.enter_context(nc.psum_tensor(n, sh, dt))
        self.pA = pst("pA", [128, 512], F32)
        self.pB = pst("pB", [128, 512], F32)
        self.pC = pst("pC", [128, 512], F32)
        self.pD = pst("pD", [128, 512], F32)
        self.pE = pst("pE", [128, 512], F32)
        self.pF = pst("pF", [128, 512], F32)
        self.pT0 = pst("pT0", [128, 1024], BF16)
        self.pT1 = pst("pT1", [128, 1024], BF16)

        self.SEM = {}
        self.C = {}
        self.DMA_SEMS = ["Lg", "Ws0", "Ws1", "Fa0", "Fa1", "Q0", "Q1",
                         "Lx0", "Lx1", "GA0", "GA1", "GB0", "GB1",
                         "Sp", "So", "Db"]
        for name in ["bar", "tE", "vE", "sE", "gR"] + self.DMA_SEMS:
            self.SEM[name] = self.es.enter_context(nc.semaphore(name))
            self.C[name] = 0
        self.bank_tok = {}
        self.bar_n = 0
        self.gp = None

    @contextmanager
    def block(self):
        with self.nc.Block() as b:
            yield b

    def usb(self, S, name, shape, dt, side="right"):
        self._uid = getattr(self, "_uid", 0) + 1
        return S.enter_context(self.nc.sbuf_tensor(f"{name}_{self._uid}", shape, dt, side=side))

    def tok(self, ins, sem, n=1):
        self.C[sem] += n
        ins.then_inc(self.SEM[sem], n)
        return self.C[sem]

    def dtok(self, ins, sem):
        return self.tok(ins, sem, 16)

    def wt(self, eng, sem, val):
        if val > 0:
            eng.wait_ge(self.SEM[sem], val)

    def use_bank(self, eng, bank):
        t = self.bank_tok.get(id(bank))
        if t is not None:
            self.wt(eng, t[0], t[1])

    def rel_bank(self, bank, sem, val):
        self.bank_tok[id(bank)] = (sem, val)

    def barrier(self):
        self.bar_n += 1
        snap = {s: self.C[s] for s in self.DMA_SEMS}
        with self.block() as blk:
            @blk.sync
            def _(e):
                for s, v in snap.items():
                    self.wt(e, s, v)
        self.bank_tok = {}

    def dbg_dump(self, tag, src_ap):
        if tag not in self.dbg_outs:
            return
        with self.block() as blk:
            @blk.gpsimd
            def _(e):
                self.dtok(e.dma_start(out=self.dbg_outs[tag], in_=src_ap), "Db")
                self.wt(e, "Db", self.C["Db"])
        self.barrier()

    def consts_phase(self):
        nc, C = self.nc, self.C
        with ExitStack() as S:
            fst = self.usb(S, "fst", [128, 4 * 128], F32, side="right")
            fmb_s = self.usb(S, "fmb_s", [128, 128], F32, side="right")
            with self.block() as blk:
                @blk.gpsimd
                def _(g):
                    g.wait_ge(self.SEM["vE"], 1)
                    self.dtok(g.dma_start(out=fmb_s[0:5, :], in_=self.fmb), "Lg")
                    self.dtok(g.dma_start(out=self.ident[:], in_=self.cident), "Lg")
                    self.dtok(g.dma_start(out=self.mask4[:], in_=self.cmask4), "Lg")
                    self.dtok(g.dma_start(
                        out=fst[:].rearrange("p (n c) -> p n c", c=128),
                        in_=self.fmw.rearrange("n p c -> p n c")), "Lg")

                @blk.sync
                def _(e):
                    self.dtok(e.dma_start(out=self.ident_f[:], in_=self.cident), "Lx0")

                @blk.vector
                def _(v):
                    self.tok(v.memset(fmb_s[:], 0.0), "vE")
                    v.memset(self.ones_col[:], 1.0)
                    v.memset(self.ones_row[:], 1.0)
                    v.memset(self.ones_row_f[:], 1.0)
                    v.memset(self.ones_col_f[:], 1.0)
                    v.memset(self.cst[0:1, 0:1], EPS_LN)
                    v.memset(self.cst[0:1, 1:2], INVD)
                    v.memset(self.cst[0:1, 2:3], INVDK)
                    v.memset(self.cst[0:1, 3:4], -0.5)
                    v.memset(self.chalf[:, 0:1], 0.5)
                    v.memset(self.chalf[:, 1:2], -0.5)
                    v.memset(self.S32[:], 0.0)
                    v.memset(self.Sbf[:], 0.0)
                    self.wt(v, "Lg", C["Lg"])
                    scl = [SCALE, 1.0, 1.0, 1.0]
                    for i in range(4):
                        ins = v.tensor_scalar_mul(self.fqk[:, i * 128:(i + 1) * 128],
                                                  fst[:, i * 128:(i + 1) * 128], scl[i])
                    self.tok(ins, "vE")

                vready = C["vE"]

                @blk.tensor
                def _(t):
                    self.wt(t, "Lx0", C["Lx0"])
                    self.wt(t, "vE", vready)
                    self.tok(t.transpose(self.pA[:, 0:128], fmb_s[:], self.ident_f[:]), "tE")

                ttok = C["tE"]

                @blk.vector
                def _(v):
                    self.wt(v, "tE", ttok)
                    scl = [SCALE, 1.0, 1.0, 1.0, 1.0]
                    for i in range(5):
                        ins = v.tensor_scalar_mul(self.pcol[:, i:i + 1],
                                                  self.pA[:, i:i + 1], scl[i])
                    self.tok(ins, "vE")
            self.barrier()

    def gamma_start(self, S):
        self.gp = _GPump(self, S)
        self.gp.prime()

    def gamma_tail(self):
        assert self.gp.done, (self.gp.ni, self.gp.nr)
        self.gp = None
        nc, C = self.nc, self.C
        if True:
            with self.block() as blk:
                vred, sred = C["vE"], C["sE"]

                @blk.vector
                def _(v):
                    for m in range(3):
                        ins = v.tensor_reduce(self.gacc[:, 140 + m:141 + m],
                                              self.gacc[:, 44 * m:44 * (m + 1)],
                                              AX, ALU.add)
                    self.tok(ins, "vE")

                vcols = C["vE"]
                t_base = C["tE"]

                @blk.tensor
                def _(t):
                    self.wt(t, "vE", vcols)
                    self.tok(t.matmul(self.pB[0:1, 0:3], self.ones_col_f[:],
                                      self.gacc[:, 140:143],
                                      start=True, stop=True), "tE")

                @blk.vector
                def _(v):
                    self.wt(v, "tE", t_base + 1)
                    v.tensor_copy(self.rowf[0:1, 3 * TT:4 * TT],
                                  self.pB[0:1, 0:512])
                    v.tensor_scalar(self.rowf[0:1, 4 * TT:5 * TT],
                                    self.rowf[0:1, 3 * TT:4 * TT],
                                    0.5 / NW_ELEMS, 0.5e-8, ALU.mult, ALU.add)
                    v.tensor_scalar_mul(self.rowf[0:1, 3 * TT:4 * TT],
                                        self.rowf[0:1, 4 * TT:5 * TT], -1.0)
                    v.reciprocal(self.rowf[0:1, 2 * TT:3 * TT],
                                 self.rowf[0:1, 4 * TT:5 * TT])
                    self.tok(v.tensor_scalar_mul(self.rowf[0:1, 2 * TT:3 * TT],
                                                 self.rowf[0:1, 2 * TT:3 * TT],
                                                 0.5), "vE")

                vthr = C["vE"]

                @blk.tensor
                def _(t):
                    self.wt(t, "vE", vthr)
                    t.matmul(self.pC[:, 0:3], self.ones_row_f[:],
                             self.rowf[0:1, 4 * TT:4 * TT + 3],
                             start=True, stop=True)
                    t.matmul(self.pC[:, 3:6], self.ones_row_f[:],
                             self.rowf[0:1, 3 * TT:3 * TT + 3],
                             start=True, stop=True)
                    mm = t.matmul(self.pC[:, 6:9], self.ones_row_f[:],
                                  self.rowf[0:1, 2 * TT:2 * TT + 3],
                                  start=True, stop=True)
                    self.tok(mm, "tE")

                trep = C["tE"]

                @blk.vector
                def _(v):
                    self.wt(v, "tE", trep)
                    self.tok(v.tensor_copy(self.pcol[:, 5:15], self.pC[:, 0:10]), "vE")
            self.barrier()

    def ln_rows_apply(self, buf, banks, stats_tok, rep_bf, apply=True):
        nc, C = self.nc, self.C
        rm = self.rowf[0:1, 0:TT]
        rs = self.rowf[0:1, TT:2 * TT]
        ra = self.rowf[0:1, 2 * TT:3 * TT]
        bm = self.rowb[0:1, 0:TT]
        bs = self.rowb[0:1, TT:2 * TT]
        toks = []
        for tti in range(2):
            s1, s2 = banks[tti]
            with self.block() as blk:
                @blk.vector
                def _(v):
                    self.wt(v, "tE", stats_tok[tti])
                    v.tensor_scalar_mul(rm, s1[0:1, :], INVD)
                    v.tensor_scalar_mul(rs, s2[0:1, :], INVD)
                    v.tensor_mul(ra, rm, rm)
                    self.tok(v.tensor_sub(rs, rs, ra), "vE")

                vvar = C["vE"]

                @blk.scalar
                def _(s):
                    self.wt(s, "vE", vvar)
                    s.activation(rs, rs, AF.Ln, bias=self.cst[0:1, 0:1])
                    s.activation(rs, rs, AF.Exp, scale=self.cst[0:1, 3:4])
                    s.copy(bm, rm)
                    self.tok(s.copy(bs, rs), "sE")
                    if self.gp:
                        self.gp.red(s, "s", 1)

                srows = C["sE"]

                @blk.tensor
                def _(t):
                    self.wt(t, "sE", srows)
                    self.use_bank(t, self.pC)
                    self.use_bank(t, self.pD)
                    t.matmul(self.pC[:], self.ones_row[:], bm, start=True, stop=True)
                    self.tok(t.matmul(self.pD[:], self.ones_row[:], bs,
                                      start=True, stop=True), "tE")

                trep = C["tE"]

                @blk.vector
                def _(v):
                    self.wt(v, "tE", trep)
                    v.tensor_copy(rep_bf[:, 0:512], self.pC[:])
                    ins = v.tensor_copy(rep_bf[:, 512:1024], self.pD[:])
                    if apply:
                        for ci in range(CI):
                            tile = buf[:, ci * HALF + tti * TT:
                                       ci * HALF + tti * TT + TT]
                            v.tensor_sub(tile, tile, rep_bf[:, 0:512])
                            ins = v.tensor_mul(tile, tile, rep_bf[:, 512:1024])
                    self.tok(ins, "vE")
            self.rel_bank(self.pC, "vE", C["vE"])
            self.rel_bank(self.pD, "vE", C["vE"])
            toks.append(C["vE"])
        return toks

    def x_phase(self, srcT_t, buf, half, ws=None):
        nc, C = self.nc, self.C
        with ExitStack() as S:
            sq16 = self.usb(S, "sq16", [128, CI * TT], BF16, side="right")
            rep = self.usb(S, "repx", [128, 1024], BF16, side="right")
            h0 = half * HALF
            with self.block() as blk:
                @blk.gpsimd
                def _(g):
                    for ci in range(CI):
                        self.dtok(g.dma_start(out=buf[:, ci * HALF:(ci + 1) * HALF],
                                              in_=srcT_t[ci][:, h0:h0 + HALF]), "Lg")
                    if ws is not None:
                        ws.pump(g)
                    if self.gp:
                        self.gp.red(g, "g", 2)

                lg_full = C["Lg"]
                t_base = C["tE"]
                sq_toks = {}

                @blk.vector
                def _(v):
                    self.wt(v, "Lg", lg_full)
                    for tti in range(2):
                        if tti == 1:
                            self.wt(v, "tE", t_base + 1)
                        for ci in range(CI):
                            tile = buf[:, ci * HALF + tti * TT:
                                       ci * HALF + tti * TT + TT]
                            ins = v.tensor_mul(sq16[:, ci * TT:(ci + 1) * TT],
                                               tile, tile)
                            sq_toks[(tti, ci)] = self.tok(ins, "vE")

                stats_tok = {}

                @blk.tensor
                def _(t):
                    bankmap = {0: (self.pA, self.pB), 1: (self.pE, self.pF)}
                    for tti in range(2):
                        s1, s2 = bankmap[tti]
                        self.use_bank(t, s1)
                        self.use_bank(t, s2)
                        for ci in range(CI):
                            self.wt(t, "vE", sq_toks[(tti, ci)])
                            tile = buf[:, ci * HALF + tti * TT:
                                       ci * HALF + tti * TT + TT]
                            t.matmul(s1[0:1, :], self.ones_col[:], tile,
                                     start=(ci == 0), stop=(ci == CI - 1))
                            ins = t.matmul(s2[0:1, :], self.ones_col[:],
                                           sq16[:, ci * TT:(ci + 1) * TT],
                                           start=(ci == 0), stop=(ci == CI - 1))
                        stats_tok[tti] = self.tok(ins, "tE")
                assert stats_tok[0] == t_base + 1

            banks = {0: (self.pA, self.pB), 1: (self.pE, self.pF)}
            vt = self.ln_rows_apply(buf, banks, stats_tok, rep)
            self.rel_bank(self.pA, "vE", vt[0])
            self.rel_bank(self.pB, "vE", vt[0])
            self.rel_bank(self.pE, "vE", vt[1])
            self.rel_bank(self.pF, "vE", vt[1])
            self.barrier()

    def proj_phase(self, ws, xn, dest, kind):
        nc, C = self.nc, self.C
        with ExitStack() as S:
            wb = ws.wb
            fmap = kind in ("fq", "fk")
            if fmap:
                raw = self.usb(S, "rawfm", [128, 2 * TT], BF16, side="right")
                t1s = self.usb(S, "t1fm", [128, 2 * TT], BF16, side="right")
                fw1 = self.fqk[:, 0:128] if kind == "fq" else self.fqk[:, 256:384]
                fw2 = self.fqk[:, 128:256] if kind == "fq" else self.fqk[:, 384:512]
                b1 = self.pcol[:, 0:1] if kind == "fq" else self.pcol[:, 2:3]
                b2 = self.pcol[:, 1:2] if kind == "fq" else self.pcol[:, 3:4]

            t0, v0, s0 = C["tE"], C["vE"], C["sE"]
            base = ws.claim(CI // 2)
            tper = 2 if fmap else 1
            Tproj = {}
            Tf = {}
            n = 0
            for oc in range(CI):
                for tt in range(2):
                    n += 1
                    Tproj[(oc, tt)] = t0 + n
                    if fmap:
                        n += 1
                        Tf[(oc, tt)] = t0 + n
            Vr = {}
            Ve = {}
            n = 0
            for oc in range(CI):
                for tt in range(2):
                    if fmap:
                        n += 1
                        Vr[(oc, tt)] = v0 + n
                        n += 1
                        Ve[(oc, tt)] = v0 + n
                    else:
                        n += 1
                        Ve[(oc, tt)] = v0 + n
            Sf = {}
            if fmap:
                for i, key in enumerate([(oc, tt) for oc in range(CI) for tt in range(2)]):
                    Sf[key] = s0 + i + 1
            for lp in range(CI // 2):
                ws.cons[base + lp] = ("tE", Tproj[(2 * lp + 1, 1)])

            with self.block() as blk:
                @blk.gpsimd
                def _(g):
                    ws.pump(g)
                    if self.gp:
                        self.gp.red(g, "g", 16)

                @blk.tensor
                def _(t):
                    pp = [self.pA, self.pB]
                    pf1 = [self.pC, self.pE]
                    pf2 = [self.pD, self.pF]
                    for oc in range(CI):
                        gp_, osub = base + oc // 2, oc % 2
                        self.wt(t, "Ws%d" % (gp_ % 2), ws.val[gp_])
                        for tt in range(2):
                            par = (oc * 2 + tt) % 2
                            self.use_bank(t, pp[par])
                            for ci in range(CI):
                                ins = t.matmul(pp[par][:],
                                               wb[gp_ % 2][:, osub * 2048 + ci * 128:
                                                  osub * 2048 + (ci + 1) * 128],
                                               xn[:, ci * HALF + tt * TT:
                                                  ci * HALF + tt * TT + TT],
                                               start=(ci == 0), stop=(ci == CI - 1))
                            assert self.tok(ins, "tE") == Tproj[(oc, tt)]
                            self.rel_bank(pp[par], "vE",
                                          Vr[(oc, tt)] if fmap else Ve[(oc, tt)])
                            if fmap:
                                self.use_bank(t, pf1[par])
                                self.use_bank(t, pf2[par])
                                self.wt(t, "vE", Vr[(oc, tt)])
                                t.matmul(pf1[par][:], fw1,
                                         raw[:, par * TT:(par + 1) * TT],
                                         start=True, stop=True)
                                ins = t.matmul(pf2[par][:], fw2,
                                               raw[:, par * TT:(par + 1) * TT],
                                               start=True, stop=True)
                                assert self.tok(ins, "tE") == Tf[(oc, tt)]
                                self.rel_bank(pf1[par], "sE", Sf[(oc, tt)])
                                self.rel_bank(pf2[par], "vE", Ve[(oc, tt)])

                @blk.vector
                def _(v):
                    pp = [self.pA, self.pB]
                    pf2 = [self.pD, self.pF]
                    for oc in range(CI):
                        for tt in range(2):
                            par = (oc * 2 + tt) % 2
                            dst = dest[:, oc * HALF + tt * TT:
                                       oc * HALF + tt * TT + TT]
                            if fmap:
                                self.wt(v, "tE", Tproj[(oc, tt)])
                                assert self.tok(
                                    v.tensor_copy(raw[:, par * TT:(par + 1) * TT],
                                                  pp[par][:]), "vE") == Vr[(oc, tt)]
                                self.wt(v, "tE", Tf[(oc, tt)])
                                self.wt(v, "sE", Sf[(oc, tt)])
                                ins = v.scalar_tensor_tensor(
                                    dst, pf2[par][:], b2,
                                    t1s[:, par * TT:(par + 1) * TT],
                                    ALU.add, ALU.mult)
                                assert self.tok(ins, "vE") == Ve[(oc, tt)]
                            else:
                                self.wt(v, "tE", Tproj[(oc, tt)])
                                assert self.tok(v.tensor_copy(dst, pp[par][:]),
                                                "vE") == Ve[(oc, tt)]

                if not fmap and self.gp:
                    @blk.scalar
                    def _(s):
                        for _ in range(CI):
                            self.gp.red(s, "s", 2)

                if fmap:
                    @blk.scalar
                    def _(s):
                        pf1 = [self.pC, self.pE]
                        keysf = [(oc, tt) for oc in range(CI) for tt in range(2)]
                        for i, (oc, tt) in enumerate(keysf):
                            par = (oc * 2 + tt) % 2
                            self.wt(s, "tE", Tf[(oc, tt)])
                            if i >= 2:
                                self.wt(s, "vE", Ve[keysf[i - 2]])
                            ins = s.add(t1s[:, par * TT:(par + 1) * TT],
                                        pf1[par][:], b1)
                            assert self.tok(ins, "sE") == Sf[(oc, tt)]
                            if self.gp:
                                self.gp.red(s, "s", 1)
            self.barrier()

    def attn_half(self, half, q_sb, k_sb, v_sb, o_sb, full=True):
        nc, C = self.nc, self.C
        with ExitStack() as S:
            tm = self.usb(S, "tm", [128, 2048], BF16, side="right")
            sscr = self.usb(S, "sscr", [128, 1024], BF16, side="right")
            o2scr = self.usb(S, "o2scr", [128, 1024], BF16, side="right")

            t0, v0, s0 = C["tE"], C["vE"], C["sE"]
            tn, vn, sn = [t0], [v0], [s0]
            Ttp, Ts, To, Td = {}, {}, {}, {}
            Vtm, Vmask, Vo, Vs, Sc = {}, {}, {}, {}, {}
            Vo2, Tst, Sr, Trp, Vap = {}, {}, {}, {}, {}
            for c in range(NCH):
                for g in range(4):
                    tn[0] += 1; Ttp[(c, g)] = tn[0]
                    if full:
                        tn[0] += 1; Ts[(c, g)] = tn[0]
                        tn[0] += 1; To[(c, g)] = tn[0]
                    tn[0] += 1; Td[(c, g)] = tn[0]
                    vn[0] += 1; Vtm[(c, g)] = vn[0]
                    if full:
                        vn[0] += 1; Vmask[(c, g)] = vn[0]
                        vn[0] += 1; Vo[(c, g)] = vn[0]
                    vn[0] += 1; Vs[(c, g)] = vn[0]
                    sn[0] += 1; Sc[(c, g)] = sn[0]
                if full and c % 4 == 3:
                    tti = c // 4
                    for h in range(H):
                        vn[0] += 1; Vo2[(tti, h)] = vn[0]
                        tn[0] += 1; Tst[(tti, h)] = tn[0]
                        sn[0] += 1; Sr[(tti, h)] = sn[0]
                        tn[0] += 1; Trp[(tti, h)] = tn[0]
                        vn[0] += 1; Vap[(tti, h)] = vn[0]

            def prev2(c, g):
                i = c * 4 + g - 2
                return None if i < 0 else (i // 4, i % 4)

            def prevh(tti, h):
                return None if h < 2 else (tti, h - 2)

            qcol = lambda hd, c: hd * HALF + c * CH
            rfr = [self.rowf[0:1, 3 * TT:4 * TT], self.rowf[0:1, 4 * TT:5 * TT]]
            rbr = [self.rowb[0:1, 2 * TT:3 * TT], self.rowb[0:1, 3 * TT:4 * TT]]

            with self.block() as blk:
                @blk.tensor
                def _(t):
                    pT = [self.pT0, self.pT1]
                    pS = [self.pA, self.pB]
                    pO = [self.pC, self.pD]
                    pDl = [self.pE, self.pF]
                    for c in range(NCH):
                        for g in range(4):
                            par = g % 2
                            p2 = prev2(c, g)
                            if p2:
                                self.wt(t, "vE", Vtm[p2])
                            for j in range(4):
                                hd = 4 * g + j
                                t.transpose(pT[par][:, j * 128:(j + 1) * 128],
                                            v_sb[:, qcol(hd, c):qcol(hd, c) + CH],
                                            self.ident[:])
                            for j in range(4):
                                hd = 4 * g + j
                                ins = t.transpose(pT[par][:, 512 + j * 128:512 + (j + 1) * 128],
                                                  k_sb[:, qcol(hd, c):qcol(hd, c) + CH],
                                                  self.ident[:])
                            assert self.tok(ins, "tE") == Ttp[(c, g)]
                            if full:
                                if p2:
                                    self.wt(t, "vE", Vmask[p2])
                                if c == 4:
                                    self.wt(t, "sE", Sr[(0, 14 + par)])
                                for j in range(4):
                                    hd = 4 * g + j
                                    ins = t.matmul(pS[par][:, j * 128:(j + 1) * 128],
                                                   k_sb[:, qcol(hd, c):qcol(hd, c) + CH],
                                                   q_sb[:, qcol(hd, c):qcol(hd, c) + CH],
                                                   start=True, stop=True)
                                assert self.tok(ins, "tE") == Ts[(c, g)]
                                if p2:
                                    self.wt(t, "vE", Vo[p2])
                                if c == 4:
                                    self.wt(t, "vE", Vap[(0, 14 + par)])
                                if c > 0:
                                    self.wt(t, "sE", Sc[(c - 1, g)])
                                self.wt(t, "vE", Vmask[(c, g)])
                                self.wt(t, "vE", Vtm[(c, g)])
                                for j in range(4):
                                    hd = 4 * g + j
                                    t.matmul(pO[par][:, j * 128:(j + 1) * 128],
                                             self.Sbf[:, hd * 128:(hd + 1) * 128],
                                             q_sb[:, qcol(hd, c):qcol(hd, c) + CH],
                                             start=True, stop=False)
                                    ins = t.matmul(pO[par][:, j * 128:(j + 1) * 128],
                                                   tm[:, par * 1024 + j * 128:
                                                      par * 1024 + (j + 1) * 128],
                                                   sscr[:, par * 512 + j * 128:
                                                        par * 512 + (j + 1) * 128],
                                                   start=False, stop=True)
                                assert self.tok(ins, "tE") == To[(c, g)]
                            if p2:
                                self.wt(t, "vE", Vs[p2])
                            if not full:
                                self.wt(t, "vE", Vtm[(c, g)])
                            for j in range(4):
                                hd = 4 * g + j
                                ins = t.matmul(pDl[par][:, j * 128:(j + 1) * 128],
                                               tm[:, par * 1024 + 512 + j * 128:
                                                  par * 1024 + 512 + (j + 1) * 128],
                                               tm[:, par * 1024 + j * 128:
                                                  par * 1024 + (j + 1) * 128],
                                               start=True, stop=True)
                            assert self.tok(ins, "tE") == Td[(c, g)]
                        if full and c % 4 == 3:
                            tti = c // 4
                            pSt = [self.pA, self.pB]
                            pRp = [self.pC, self.pD]
                            for h in range(H):
                                par = h % 2
                                ph = prevh(tti, h)
                                self.wt(t, "vE", Vo2[(tti, h)])
                                if ph:
                                    self.wt(t, "sE", Sr[ph])
                                else:
                                    self.wt(t, "vE", Vmask[(c, 2 + par)])
                                ins = t.matmul(pSt[par][0:1, :], self.ones_col[:],
                                               o2scr[:, par * TT:(par + 1) * TT],
                                               start=True, stop=True)
                                assert self.tok(ins, "tE") == Tst[(tti, h)]
                                self.wt(t, "sE", Sr[(tti, h)])
                                if ph:
                                    self.wt(t, "vE", Vap[ph])
                                else:
                                    self.wt(t, "vE", Vo[(c, 2 + par)])
                                ins = t.matmul(pRp[par][:], self.ones_row[:],
                                               rbr[par], start=True, stop=True)
                                assert self.tok(ins, "tE") == Trp[(tti, h)]

                @blk.vector
                def _(v):
                    pT = [self.pT0, self.pT1]
                    pS = [self.pA, self.pB]
                    pO = [self.pC, self.pD]
                    pDl = [self.pE, self.pF]
                    for c in range(NCH):
                        for g in range(4):
                            par = g % 2
                            p2 = prev2(c, g)
                            self.wt(v, "tE", Ttp[(c, g)])
                            if p2:
                                self.wt(v, "tE", Td[p2])
                            ins = v.tensor_copy(tm[:, par * 1024:(par + 1) * 1024],
                                                pT[par][:])
                            assert self.tok(ins, "vE") == Vtm[(c, g)]
                            if full:
                                self.wt(v, "tE", Ts[(c, g)])
                                if p2:
                                    self.wt(v, "tE", To[p2])
                                ins = v.tensor_mul(sscr[:, par * 512:(par + 1) * 512],
                                                   pS[par][:], self.mask4[:])
                                assert self.tok(ins, "vE") == Vmask[(c, g)]
                                self.wt(v, "tE", To[(c, g)])
                                dst = o_sb[:].rearrange("p (h t) -> p h t", t=HALF)[
                                    :, 4 * g:4 * g + 4, c * CH:(c + 1) * CH]
                                src = pO[par][:].rearrange("p (h t) -> p h t", t=CH)
                                ins = v.tensor_copy(dst, src)
                                assert self.tok(ins, "vE") == Vo[(c, g)]
                            self.wt(v, "tE", Td[(c, g)])
                            if c > 0:
                                self.wt(v, "sE", Sc[(c - 1, g)])
                            sl = self.S32[:, g * 512:(g + 1) * 512]
                            ins = v.tensor_add(sl, sl, pDl[par][:])
                            assert self.tok(ins, "vE") == Vs[(c, g)]
                        if full and c % 4 == 3:
                            tti = c // 4
                            pRp = [self.pC, self.pD]
                            for h in range(H):
                                par = h % 2
                                ph = prevh(tti, h)
                                otile = o_sb[:, h * HALF + tti * TT:
                                             h * HALF + tti * TT + TT]
                                if ph:
                                    self.wt(v, "tE", Tst[ph])
                                ins = v.tensor_mul(o2scr[:, par * TT:(par + 1) * TT],
                                                   otile, otile)
                                assert self.tok(ins, "vE") == Vo2[(tti, h)]
                                self.wt(v, "tE", Trp[(tti, h)])
                                ins = v.scalar_tensor_tensor(
                                    otile, otile, self.pcol[:, 4:5], pRp[par][:],
                                    ALU.mult, ALU.mult)
                                assert self.tok(ins, "vE") == Vap[(tti, h)]

                @blk.scalar
                def _(s):
                    pSt = [self.pA, self.pB]
                    for c in range(NCH):
                        for g in range(4):
                            self.wt(s, "vE", Vs[(c, g)])
                            if full:
                                self.wt(s, "tE", To[(c, g)])
                            ins = s.copy(self.Sbf[:, g * 512:(g + 1) * 512],
                                         self.S32[:, g * 512:(g + 1) * 512])
                            assert self.tok(ins, "sE") == Sc[(c, g)]
                            if self.gp:
                                self.gp.red(s, "s", 1)
                        if full and c % 4 == 3:
                            tti = c // 4
                            for h in range(H):
                                par = h % 2
                                self.wt(s, "tE", Tst[(tti, h)])
                                s.activation(rfr[par], pSt[par][0:1, :], AF.Ln,
                                             bias=self.cst[0:1, 0:1],
                                             scale=self.cst[0:1, 2:3])
                                s.activation(rfr[par], rfr[par], AF.Exp,
                                             scale=self.cst[0:1, 3:4])
                                ins = s.copy(rbr[par], rfr[par])
                                assert self.tok(ins, "sE") == Sr[(tti, h)]

            for b in [self.pA, self.pB]:
                self.rel_bank(b, "sE", C["sE"])
            for b in [self.pC, self.pD]:
                self.rel_bank(b, "vE", C["vE"])
            for b in [self.pE, self.pF]:
                self.rel_bank(b, "vE", C["vE"])
            self.barrier()

    def wo_phase(self, ws, o_sb, lnattn):
        nc, C = self.nc, self.C
        with ExitStack() as S:
            wb = ws.wb
            sq4 = self.usb(S, "sq4", [128, 4 * TT], BF16, side="right")
            rep = self.usb(S, "repw", [128, 1024], BF16, side="right")

            t0, v0 = C["tE"], C["vE"]
            base = ws.claim(CI // 2)
            Tproj = {}
            Tst = {}
            n = 0
            for oc in range(CI):
                for tt in range(2):
                    n += 1
                    Tproj[(oc, tt)] = t0 + n
                    n += 1
                    Tst[(oc, tt)] = t0 + n
            Vev = {}
            n = 0
            for oc in range(CI):
                for tt in range(2):
                    n += 1
                    Vev[(oc, tt)] = v0 + n
            for lp in range(CI // 2):
                ws.cons[base + lp] = ("tE", Tproj[(2 * lp + 1, 1)])

            with self.block() as blk:
                @blk.gpsimd
                def _(g):
                    ws.pump(g)

                @blk.tensor
                def _(t):
                    pp = [self.pA, self.pB]
                    stats = {0: (self.pC, self.pD), 1: (self.pE, self.pF)}
                    for b in [self.pC, self.pD, self.pE, self.pF]:
                        self.use_bank(t, b)
                    for oc in range(CI):
                        gp_, osub = base + oc // 2, oc % 2
                        self.wt(t, "Ws%d" % (gp_ % 2), ws.val[gp_])
                        for tt in range(2):
                            par = (oc * 2 + tt) % 2
                            self.use_bank(t, pp[par])
                            for ci in range(CI):
                                ins = t.matmul(pp[par][:],
                                               wb[gp_ % 2][:, osub * 2048 + ci * 128:
                                                  osub * 2048 + (ci + 1) * 128],
                                               o_sb[:, ci * HALF + tt * TT:
                                                    ci * HALF + tt * TT + TT],
                                               start=(ci == 0), stop=(ci == CI - 1))
                            assert self.tok(ins, "tE") == Tproj[(oc, tt)]
                            self.rel_bank(pp[par], "vE", Vev[(oc, tt)])
                            s1, s2 = stats[tt]
                            self.wt(t, "vE", Vev[(oc, tt)])
                            t.matmul(s1[0:1, :], self.ones_col[:],
                                     lnattn[:, oc * HALF + tt * TT:
                                            oc * HALF + tt * TT + TT],
                                     start=(oc == 0), stop=(oc == CI - 1))
                            ins = t.matmul(s2[0:1, :], self.ones_col[:],
                                           sq4[:, ((oc % 2) * 2 + tt) * TT:
                                               ((oc % 2) * 2 + tt) * TT + TT],
                                           start=(oc == 0), stop=(oc == CI - 1))
                            assert self.tok(ins, "tE") == Tst[(oc, tt)]

                @blk.vector
                def _(v):
                    pp = [self.pA, self.pB]
                    for oc in range(CI):
                        for tt in range(2):
                            par = (oc * 2 + tt) % 2
                            self.wt(v, "tE", Tproj[(oc, tt)])
                            if oc >= 2:
                                self.wt(v, "tE", Tst[(oc - 2, tt)])
                            dst = lnattn[:, oc * HALF + tt * TT:
                                         oc * HALF + tt * TT + TT]
                            v.tensor_copy(dst, pp[par][:])
                            ins = v.tensor_mul(sq4[:, ((oc % 2) * 2 + tt) * TT:
                                                   ((oc % 2) * 2 + tt) * TT + TT],
                                               dst, dst)
                            assert self.tok(ins, "vE") == Vev[(oc, tt)]

            stats_tok = {0: Tst[(CI - 1, 0)], 1: Tst[(CI - 1, 1)]}
            banks = {0: (self.pC, self.pD), 1: (self.pE, self.pF)}
            vt = self.ln_rows_apply(lnattn, banks, stats_tok, rep)
            for b in [self.pC, self.pD, self.pE, self.pF]:
                self.rel_bank(b, "vE", vt[1])
            self.barrier()

    def y_phase(self, half, lnattn, y_sb):
        nc, C = self.nc, self.C
        with ExitStack() as S:
            xstg0 = self.usb(S, "xstg0", [128, TT], F32, side="right")
            xstg1 = self.usb(S, "xstg1", [128, TT], F32, side="right")
            sq2 = self.usb(S, "sq2", [128, 2 * TT], BF16, side="right")
            rep = self.usb(S, "repy", [128, 1024], BF16, side="right")
            xstg = [xstg0, xstg1]
            h0 = half * HALF

            t0, v0 = C["tE"], C["vE"]
            lx0_0, lx1_0 = C["Lx0"], C["Lx1"]
            keys = [(tt, ci) for tt in range(2) for ci in range(CI)]
            Vy = {k: v0 + 2 * i + 1 for i, k in enumerate(keys)}
            Vsq = {k: v0 + 2 * i + 2 for i, k in enumerate(keys)}
            Tst = {k: t0 + i + 1 for i, k in enumerate(keys)}

            with self.block() as blk:
                @blk.sync
                def _(e):
                    for i, (tt, ci) in enumerate(keys):
                        if i >= 2:
                            self.wt(e, "vE", Vsq[keys[i - 2]])
                        self.dtok(e.dma_start(
                            out=xstg[i % 2][:],
                            in_=self.xoT_t[ci][:, h0 + tt * TT:h0 + tt * TT + TT]),
                            "Lx%d" % (i % 2))

                @blk.vector
                def _(v):
                    for i, (tt, ci) in enumerate(keys):
                        self.wt(v, "Lx%d" % (i % 2),
                                (lx0_0 if i % 2 == 0 else lx1_0) + 16 * (i // 2 + 1))
                        if i >= 2:
                            self.wt(v, "tE", Tst[keys[i - 2]])
                        ytile = y_sb[:, ci * HALF + tt * TT: ci * HALF + tt * TT + TT]
                        la = lnattn[:, ci * HALF + tt * TT: ci * HALF + tt * TT + TT]
                        ins = v.tensor_add(ytile, xstg[i % 2][:], la)
                        assert self.tok(ins, "vE") == Vy[(tt, ci)]
                        ins = v.tensor_mul(sq2[:, (i % 2) * TT:(i % 2 + 1) * TT],
                                           ytile, ytile)
                        assert self.tok(ins, "vE") == Vsq[(tt, ci)]

                @blk.tensor
                def _(t):
                    bankmap = {0: (self.pA, self.pB), 1: (self.pE, self.pF)}
                    for b in [self.pA, self.pB, self.pE, self.pF]:
                        self.use_bank(t, b)
                    for i, (tt, ci) in enumerate(keys):
                        s1, s2 = bankmap[tt]
                        self.wt(t, "vE", Vsq[(tt, ci)])
                        t.matmul(s1[0:1, :], self.ones_col[:],
                                 y_sb[:, ci * HALF + tt * TT: ci * HALF + tt * TT + TT],
                                 start=(ci == 0), stop=(ci == CI - 1))
                        ins = t.matmul(s2[0:1, :], self.ones_col[:],
                                       sq2[:, (i % 2) * TT:(i % 2 + 1) * TT],
                                       start=(ci == 0), stop=(ci == CI - 1))
                        assert self.tok(ins, "tE") == Tst[(tt, ci)]

            stats_tok = {0: Tst[(0, CI - 1)], 1: Tst[(1, CI - 1)]}
            banks = {0: (self.pA, self.pB), 1: (self.pE, self.pF)}
            vt = self.ln_rows_apply(y_sb, banks, stats_tok, rep)
            for b in [self.pA, self.pB, self.pE, self.pF]:
                self.rel_bank(b, "vE", vt[1])
            self.barrier()

    def ffn12_q(self, q, y_sb, hid_sb, first):
        nc, C = self.nc, self.C
        half, tt = q // 2, q % 2
        with ExitStack() as S:
            stg0 = self.usb(S, "stg0", [128, 2048], F32, side="right")
            stg1 = self.usb(S, "stg1", [128, 2048], F32, side="right")
            qb10 = self.usb(S, "qb10", [128, 2048], BF16, side="right")
            qb11 = self.usb(S, "qb11", [128, 2048], BF16, side="right")
            qb20 = self.usb(S, "qb20", [128, 2048], BF16, side="right")
            qb21 = self.usb(S, "qb21", [128, 2048], BF16, side="right")
            tsc = self.usb(S, "tsc", [128, 6144], BF16, side="right")
            sig = self.usb(S, "sig", [128, 1024], BF16, side="right")
            stg = [stg0, stg1]
            qb1 = [qb10, qb11]
            qb2 = [qb20, qb21]

            t0, v0, s0 = C["tE"], C["vE"], C["sE"]
            sp0 = C["Sp"]
            fa0_0, fa1_0 = C["Fa0"], C["Fa1"]
            q0_0, q1_0 = C["Q0"], C["Q1"]
            T1 = {hc: t0 + 2 * hc + 1 for hc in range(HC)}
            T2 = {hc: t0 + 2 * hc + 2 for hc in range(HC)}
            Ssig = {hc: s0 + hc + 1 for hc in range(HC)}
            if first:
                Vq1 = {hc: v0 + 3 * hc + 1 for hc in range(HC)}
                Vq2 = {hc: v0 + 3 * hc + 2 for hc in range(HC)}
                Vh = {hc: v0 + 3 * hc + 3 for hc in range(HC)}
            else:
                Vh = {hc: v0 + hc + 1 for hc in range(HC)}

            with self.block() as blk:
                if first:
                    Sa = {i: s0 + (i // 2) * 5 + (i % 2) * 2 + 1 for i in range(2 * HC)}
                    Sb = {i: s0 + (i // 2) * 5 + (i % 2) * 2 + 2 for i in range(2 * HC)}
                    Ssig = {hc: s0 + 5 * hc + 5 for hc in range(HC)}

                    @blk.gpsimd
                    def _(g):
                        for i in range(2 * HC):
                            hc, which = i // 2, i % 2
                            if i >= 2:
                                self.wt(g, "sE", Sb[i - 2])
                            src = (self.w1s if which == 0 else self.w2s)[hc]
                            self.dtok(g.dma_start(
                                out=stg[i % 2][:].rearrange("p (c o) -> p c o", o=128),
                                in_=src), "Fa%d" % (i % 2))

                    @blk.scalar
                    def _(s):
                        for hc in range(HC):
                            for which in range(2):
                                i = 2 * hc + which
                                self.wt(s, "Fa%d" % (i % 2),
                                        (fa0_0 if i % 2 == 0 else fa1_0) + 16 * (i // 2 + 1))
                                inv = self.pcol[:, 11:12] if which == 0 else self.pcol[:, 12:13]
                                qb = (qb1 if which == 0 else qb2)[hc % 2]
                                if hc >= 2:
                                    self.wt(s, "tE", T2[hc - 2])
                                    self.wt(s, "Sp", sp0 + 16 * 2 * (hc - 1))
                                ins = s.activation(qb[:], stg[i % 2][:], AF.Sign,
                                                   scale=inv, bias=self.chalf[:, 0:1])
                                assert self.tok(ins, "sE") == Sa[i]
                                if i >= 2:
                                    self.wt(s, "vE", (Vq1 if i % 2 == 0 else Vq2)[(i - 2) // 2])
                                ins = s.activation(tsc[:, (i % 2) * 2048:(i % 2 + 1) * 2048],
                                                   stg[i % 2][:], AF.Sign,
                                                   scale=inv, bias=self.chalf[:, 1:2])
                                assert self.tok(ins, "sE") == Sb[i]
                            self.wt(s, "tE", T1[hc])
                            if hc >= 2:
                                self.wt(s, "vE", Vh[hc - 2])
                            ins = s.activation(sig[:, (hc % 2) * TT:(hc % 2 + 1) * TT],
                                               self.pA[:] if hc % 2 == 0 else self.pC[:],
                                               AF.Sigmoid, scale=self.chalf[:, 0:1])
                            assert self.tok(ins, "sE") == Ssig[hc]

                    @blk.sync
                    def _(e):
                        for hc in range(HC):
                            self.wt(e, "vE", Vq1[hc])
                            self.dtok(e.dma_start(out=self.qw1_sp[hc],
                                                  in_=qb1[hc % 2][:]), "Sp")
                            self.wt(e, "vE", Vq2[hc])
                            self.dtok(e.dma_start(out=self.qw2_sp[hc],
                                                  in_=qb2[hc % 2][:]), "Sp")

                    @blk.vector
                    def _(v):
                        for hc in range(HC):
                            for which in range(2):
                                i = 2 * hc + which
                                self.wt(v, "sE", Sb[i])
                                qb = (qb1 if which == 0 else qb2)[hc % 2]
                                ins = v.tensor_add(qb[:], qb[:],
                                                   tsc[:, (i % 2) * 2048:(i % 2 + 1) * 2048])
                                got = self.tok(ins, "vE")
                                assert got == (Vq1 if which == 0 else Vq2)[hc]
                            self.wt(v, "sE", Ssig[hc])
                            self.wt(v, "tE", T2[hc])
                            ins = v.scalar_tensor_tensor(
                                hid_sb[:, hc * TT:(hc + 1) * TT],
                                self.pD[:] if hc % 2 == 0 else self.pB[:],
                                self.chalf[:, 0:1],
                                sig[:, (hc % 2) * TT:(hc % 2 + 1) * TT],
                                ALU.mult, ALU.mult)
                            assert self.tok(ins, "vE") == Vh[hc]
                else:
                    @blk.sync
                    def _(e):
                        for hc in range(HC):
                            if hc >= 2:
                                self.wt(e, "tE", T2[hc - 2])
                            self.dtok(e.dma_start(out=qb1[hc % 2][:],
                                                  in_=self.qw1_sp[hc]), "Q%d" % (hc % 2))
                            self.dtok(e.dma_start(out=qb2[hc % 2][:],
                                                  in_=self.qw2_sp[hc]), "Q%d" % (hc % 2))

                    @blk.vector
                    def _(v):
                        for hc in range(HC):
                            self.wt(v, "sE", Ssig[hc])
                            self.wt(v, "tE", T2[hc])
                            ins = v.scalar_tensor_tensor(
                                hid_sb[:, hc * TT:(hc + 1) * TT],
                                self.pD[:] if hc % 2 == 0 else self.pB[:],
                                self.chalf[:, 0:1],
                                sig[:, (hc % 2) * TT:(hc % 2 + 1) * TT],
                                ALU.mult, ALU.mult)
                            assert self.tok(ins, "vE") == Vh[hc]

                    @blk.scalar
                    def _(s):
                        for hc in range(HC):
                            self.wt(s, "tE", T1[hc])
                            if hc >= 2:
                                self.wt(s, "vE", Vh[hc - 2])
                            ins = s.activation(sig[:, (hc % 2) * TT:(hc % 2 + 1) * TT],
                                               self.pA[:] if hc % 2 == 0 else self.pC[:],
                                               AF.Sigmoid, scale=self.chalf[:, 0:1])
                            assert self.tok(ins, "sE") == Ssig[hc]

                @blk.tensor
                def _(t):
                    for hc in range(HC):
                        if first:
                            self.wt(t, "vE", Vq2[hc])
                        else:
                            self.wt(t, "Q%d" % (hc % 2),
                                    (q0_0 if hc % 2 == 0 else q1_0) + 32 * (hc // 2 + 1))
                        b1 = self.pA if hc % 2 == 0 else self.pC
                        b2 = self.pD if hc % 2 == 0 else self.pB
                        if hc >= 2:
                            self.wt(t, "sE", Ssig[hc - 2])
                            self.wt(t, "vE", Vh[hc - 2])
                        else:
                            self.use_bank(t, b1)
                            self.use_bank(t, b2)
                        for ci in range(CI):
                            ins = t.matmul(b1[:], qb1[hc % 2][:, ci * 128:(ci + 1) * 128],
                                           y_sb[:, ci * HALF + tt * TT:
                                                ci * HALF + tt * TT + TT],
                                           start=(ci == 0), stop=(ci == CI - 1))
                        assert self.tok(ins, "tE") == T1[hc]
                        for ci in range(CI):
                            ins = t.matmul(b2[:], qb2[hc % 2][:, ci * 128:(ci + 1) * 128],
                                           y_sb[:, ci * HALF + tt * TT:
                                                ci * HALF + tt * TT + TT],
                                           start=(ci == 0), stop=(ci == CI - 1))
                        assert self.tok(ins, "tE") == T2[hc]

            for b in [self.pA, self.pC]:
                self.rel_bank(b, "sE", C["sE"])
            for b in [self.pB, self.pD]:
                self.rel_bank(b, "vE", C["vE"])
            self.barrier()

    def ffn3_q(self, q, hid_sb, f_sb, first):
        nc, C = self.nc, self.C
        NP = 4
        with ExitStack() as S:
            stg0 = self.usb(S, "s3g0", [128, 1408], F32, side="right")
            stg1 = self.usb(S, "s3g1", [128, 1408], F32, side="right")
            qb0 = self.usb(S, "q3b0", [128, 1408], BF16, side="right")
            qb1_ = self.usb(S, "q3b1", [128, 1408], BF16, side="right")
            tsc = self.usb(S, "t3sc", [128, 4224], BF16, side="right")
            sqf = self.usb(S, "sqf", [128, 1024], BF16, side="right")
            stg = [stg0, stg1]
            qb = [qb0, qb1_]

            t0, v0 = C["tE"], C["vE"]
            s0_3 = C["sE"]
            sp3 = C["Sp"]
            fa30, fa31 = C["Fa0"], C["Fa1"]
            q30, q31 = C["Q0"], C["Q1"]
            Tg = {}
            n = 0
            for oc in range(CI):
                for pc in range(NP):
                    n += 1
                    Tg[(oc, pc)] = t0 + n
                n += 1
                Tg[("st", oc)] = t0 + n
            if first:
                Vq = {}
                Vf = {}
                n = 0
                for oc in range(CI):
                    for pc in range(NP):
                        n += 1
                        Vq[(oc, pc)] = v0 + n
                    n += 2
                    Vf[oc] = v0 + n
            else:
                Vf = {oc: v0 + 2 * (oc + 1) for oc in range(CI)}
            del v0

            with self.block() as blk:
                if first:
                    Sa3 = {i: s0_3 + 2 * i + 1 for i in range(CI * NP)}
                    Sb3 = {i: s0_3 + 2 * i + 2 for i in range(CI * NP)}

                    @blk.gpsimd
                    def _(g):
                        for oc in range(CI):
                            for pc in range(NP):
                                i = oc * NP + pc
                                if i >= 2:
                                    self.wt(g, "sE", Sb3[i - 2])
                                self.dtok(g.dma_start(
                                    out=stg[i % 2][:].rearrange("p (c o) -> p c o", o=128),
                                    in_=self.w3s[oc][:, pc * 11:(pc + 1) * 11, :]),
                                    "Fa%d" % (i % 2))

                    @blk.scalar
                    def _(s):
                        for oc in range(CI):
                            for pc in range(NP):
                                i = oc * NP + pc
                                self.wt(s, "Fa%d" % (i % 2),
                                        (fa30 if i % 2 == 0 else fa31) + 16 * (i // 2 + 1))
                                if i >= 2:
                                    self.wt(s, "tE", Tg[((i - 2) // NP, (i - 2) % NP)])
                                    self.wt(s, "Sp", sp3 + 16 * (i - 1))
                                ins = s.activation(qb[i % 2][:], stg[i % 2][:], AF.Sign,
                                                   scale=self.pcol[:, 13:14],
                                                   bias=self.chalf[:, 0:1])
                                assert self.tok(ins, "sE") == Sa3[i]
                                if i >= 2:
                                    self.wt(s, "vE", Vq[((i - 2) // NP, (i - 2) % NP)])
                                ins = s.activation(tsc[:, (i % 2) * 1408:(i % 2 + 1) * 1408],
                                                   stg[i % 2][:], AF.Sign,
                                                   scale=self.pcol[:, 13:14],
                                                   bias=self.chalf[:, 1:2])
                                assert self.tok(ins, "sE") == Sb3[i]

                    @blk.sync
                    def _(e):
                        for oc in range(CI):
                            for pc in range(NP):
                                i = oc * NP + pc
                                self.wt(e, "vE", Vq[(oc, pc)])
                                self.dtok(e.dma_start(
                                    out=self.qw3_sp[oc][:, pc * 1408:(pc + 1) * 1408],
                                    in_=qb[i % 2][:]), "Sp")

                    @blk.vector
                    def _(v):
                        for oc in range(CI):
                            for pc in range(NP):
                                i = oc * NP + pc
                                self.wt(v, "sE", Sb3[i])
                                ins = v.tensor_add(qb[i % 2][:], qb[i % 2][:],
                                                   tsc[:, (i % 2) * 1408:(i % 2 + 1) * 1408])
                                assert self.tok(ins, "vE") == Vq[(oc, pc)]
                            self.wt(v, "tE", Tg[(oc, NP - 1)])
                            ftile = f_sb[:, oc * TT:(oc + 1) * TT]
                            self.tok(v.tensor_copy(ftile,
                                     self.pE[:] if oc % 2 == 0 else self.pF[:]), "vE")
                            if oc >= 2:
                                self.wt(v, "tE", Tg[("st", oc - 2)])
                            ins = v.tensor_mul(sqf[:, (oc % 2) * TT:(oc % 2 + 1) * TT],
                                               ftile, ftile)
                            assert self.tok(ins, "vE") == Vf[oc]
                else:
                    @blk.sync
                    def _(e):
                        for oc in range(CI):
                            for pc in range(NP):
                                i = oc * NP + pc
                                if i >= 2:
                                    self.wt(e, "tE", Tg[((i - 2) // NP, (i - 2) % NP)])
                                self.dtok(e.dma_start(
                                    out=qb[i % 2][:],
                                    in_=self.qw3_sp[oc][:, pc * 1408:(pc + 1) * 1408]),
                                    "Q%d" % (i % 2))

                    @blk.vector
                    def _(v):
                        for oc in range(CI):
                            self.wt(v, "tE", Tg[(oc, NP - 1)])
                            ftile = f_sb[:, oc * TT:(oc + 1) * TT]
                            self.tok(v.tensor_copy(ftile,
                                     self.pE[:] if oc % 2 == 0 else self.pF[:]), "vE")
                            if oc >= 2:
                                self.wt(v, "tE", Tg[("st", oc - 2)])
                            ins = v.tensor_mul(sqf[:, (oc % 2) * TT:(oc % 2 + 1) * TT],
                                               ftile, ftile)
                            assert self.tok(ins, "vE") == Vf[oc]

                @blk.tensor
                def _(t):
                    for b in [self.pA, self.pB, self.pE, self.pF]:
                        self.use_bank(t, b)
                    for oc in range(CI):
                        fb = self.pE if oc % 2 == 0 else self.pF
                        if oc >= 2:
                            self.wt(t, "vE", Vf[oc - 2])
                        for pc in range(NP):
                            i = oc * NP + pc
                            if first:
                                self.wt(t, "vE", Vq[(oc, pc)])
                            else:
                                self.wt(t, "Q%d" % (i % 2),
                                        (q30 if i % 2 == 0 else q31) + 16 * (i // 2 + 1))
                            for j in range(11):
                                hc = pc * 11 + j
                                ins = t.matmul(fb[:],
                                               qb[i % 2][:, j * 128:(j + 1) * 128],
                                               hid_sb[:, hc * TT:(hc + 1) * TT],
                                               start=(hc == 0), stop=(hc == HC - 1))
                            assert self.tok(ins, "tE") == Tg[(oc, pc)]
                        self.wt(t, "vE", Vf[oc])
                        t.matmul(self.pA[0:1, :], self.ones_col[:],
                                 f_sb[:, oc * TT:(oc + 1) * TT],
                                 start=(oc == 0), stop=(oc == CI - 1))
                        ins = t.matmul(self.pB[0:1, :], self.ones_col[:],
                                       sqf[:, (oc % 2) * TT:(oc % 2 + 1) * TT],
                                       start=(oc == 0), stop=(oc == CI - 1))
                        assert self.tok(ins, "tE") == Tg[("st", oc)]

            self.f_stats_tok = Tg[("st", CI - 1)]
            for b in [self.pE, self.pF]:
                self.rel_bank(b, "vE", C["vE"])
            self.barrier()

    def final_q(self, q, lnattn, f_sb):
        nc, C = self.nc, self.C
        half, tt = q // 2, q % 2
        with ExitStack() as S:
            xstg0 = self.usb(S, "xf0", [128, TT], F32, side="right")
            xstg1 = self.usb(S, "xf1", [128, TT], F32, side="right")
            repf = self.usb(S, "repf", [128, 1024], F32, side="right")
            ya = self.usb(S, "ya", [128, TT], F32, side="right")
            yb = self.usb(S, "yb", [128, TT], F32, side="right")
            ost0 = self.usb(S, "ost0", [128, TT], F32, side="right")
            ost1 = self.usb(S, "ost1", [128, TT], F32, side="right")
            xstg = [xstg0, xstg1]
            ost = [ost0, ost1]
            h0 = half * HALF
            so0 = C["So"]
            lx0_0, lx1_0 = C["Lx0"], C["Lx1"]

            rm = self.rowf[0:1, 0:TT]
            rs = self.rowf[0:1, TT:2 * TT]
            ra = self.rowf[0:1, 2 * TT:3 * TT]

            with self.block() as blk:
                @blk.vector
                def _(v):
                    self.wt(v, "tE", self.f_stats_tok)
                    v.tensor_scalar_mul(rm, self.pA[0:1, :], INVD * 0.5)
                    v.tensor_scalar_mul(rs, self.pB[0:1, :], INVD * 0.25)
                    v.tensor_mul(ra, rm, rm)
                    self.tok(v.tensor_sub(rs, rs, ra), "vE")

                vvar = C["vE"]

                @blk.scalar
                def _(s):
                    self.wt(s, "vE", vvar)
                    s.activation(rs, rs, AF.Ln, bias=self.cst[0:1, 0:1])
                    self.tok(s.activation(rs, rs, AF.Exp, scale=self.cst[0:1, 3:4]),
                             "sE")

                srow = C["sE"]

                @blk.vector
                def _(v):
                    self.wt(v, "sE", srow)
                    v.tensor_mul(ra, rm, rs)
                    v.tensor_scalar_mul(ra, ra, -1.0)
                    self.tok(v.tensor_scalar_mul(rs, rs, 0.5), "vE")

                vu3 = C["vE"]

                @blk.tensor
                def _(t):
                    self.wt(t, "vE", vu3)
                    self.use_bank(t, self.pC)
                    self.use_bank(t, self.pD)
                    t.matmul(self.pC[:], self.ones_row_f[:], rs, start=True, stop=True)
                    self.tok(t.matmul(self.pD[:], self.ones_row_f[:], ra,
                                      start=True, stop=True), "tE")

                trep = C["tE"]
                v0 = C["vE"]
                Vout = {ci: v0 + ci + 1 for ci in range(CI)}

                @blk.sync
                def _(e):
                    for ci in range(CI):
                        if ci >= 2:
                            self.wt(e, "vE", Vout[ci - 2])
                        self.dtok(e.dma_start(
                            out=xstg[ci % 2][:],
                            in_=self.xoT_t[ci][:, h0 + tt * TT:h0 + tt * TT + TT]),
                            "Lx%d" % (ci % 2))

                @blk.vector
                def _(v):
                    self.wt(v, "tE", trep)
                    v.tensor_copy(repf[:, 0:512], self.pC[:])
                    v.tensor_copy(repf[:, 512:1024], self.pD[:])
                    for ci in range(CI):
                        self.wt(v, "Lx%d" % (ci % 2),
                                (lx0_0 if ci % 2 == 0 else lx1_0) + 16 * (ci // 2 + 1))
                        if ci >= 2:
                            self.wt(v, "So", so0 + 16 * (ci - 1))
                        la = lnattn[:, ci * HALF + tt * TT: ci * HALF + tt * TT + TT]
                        ftile = f_sb[:, ci * TT:(ci + 1) * TT]
                        v.tensor_add(ya[:], la, repf[:, 512:1024])
                        v.tensor_mul(yb[:], ftile, repf[:, 0:512])
                        v.tensor_add(ya[:], ya[:], yb[:])
                        ins = v.tensor_add(ost[ci % 2][:], ya[:], xstg[ci % 2][:])
                        assert self.tok(ins, "vE") == Vout[ci]

                @blk.scalar
                def _(s):
                    for ci in range(CI):
                        self.wt(s, "vE", Vout[ci])
                        self.dtok(s.dma_start(
                            out=self.outT.rearrange("(c p) t -> c p t", p=128)[ci][
                                :, h0 + tt * TT:h0 + tt * TT + TT],
                            in_=ost[ci % 2][:]), "So")

            self.rel_bank(self.pA, "vE", C["vE"])
            self.rel_bank(self.pB, "vE", C["vE"])
            self.rel_bank(self.pC, "vE", C["vE"])
            self.rel_bank(self.pD, "vE", C["vE"])
            self.barrier()

    def build(self):
        nc = self.nc
        self._build_body()
        rem = nc.sbuf_bytes_remaining
        self.es.close()
        return nc, rem

    def _build_body(self):
        nc = self.nc
        self.consts_phase()

        with ExitStack() as SGP:
            self.gamma_start(SGP)
            for half in range(2):
                with ExitStack() as SL:
                    SWS = ExitStack()
                    ws = WStream(self, SWS, [(self.wk, 8), (self.wv, 8)])
                    SRx = ExitStack()
                    xn = self.usb(SRx, "xn", [128, CI * HALF], BF16, side="right")
                    self.x_phase(self.xpT_t, xn, half, ws=ws)
                    k_sb = self.usb(SL, "k_sb", [128, H * HALF], BF16, side="left")
                    v_sb = self.usb(SL, "v_sb", [128, H * HALF], BF16, side="left")
                    self.proj_phase(ws, xn, k_sb, "fk")
                    self.proj_phase(ws, xn, v_sb, "plain")
                    SRx.close()
                    SWS.close()
                    self.attn_half(half, None, k_sb, v_sb, None, full=False)
            self.gamma_tail()
        self.dbg_dump("gacc", self.gacc[:])
        self.dbg_dump("pcol", self.pcol[:])
        self.dbg_dump("rowf", self.rowf[:])

        first_ffn = True
        for half in range(2):
            SRL = ExitStack()
            SWS = ExitStack()
            ws = WStream(self, SWS,
                         [(self.wq, 8), (self.wk, 8), (self.wv, 8), (self.wo, 8)])
            SRo = ExitStack()
            with ExitStack() as SL3:
                SRx = ExitStack()
                xn = self.usb(SRx, "xn", [128, CI * HALF], BF16, side="right")
                self.x_phase(self.xoT_t, xn, half, ws=ws)
                self.dbg_dump("xn%d" % half, xn[:])
                q_sb = self.usb(SL3, "q_sb", [128, H * HALF], BF16, side="left")
                k_sb = self.usb(SL3, "k_sb", [128, H * HALF], BF16, side="left")
                v_sb = self.usb(SL3, "v_sb", [128, H * HALF], BF16, side="left")
                self.proj_phase(ws, xn, q_sb, "fq")
                self.proj_phase(ws, xn, k_sb, "fk")
                self.proj_phase(ws, xn, v_sb, "plain")
                SRx.close()
                o_sb = self.usb(SRo, "o_sb", [128, H * HALF], BF16, side="right")
                self.dbg_dump("qp%d" % half, q_sb[:])
                self.dbg_dump("kp%d" % half, k_sb[:])
                self.attn_half(half, q_sb, k_sb, v_sb, o_sb, full=True)
                self.dbg_dump("on%d" % half, o_sb[:])
            lnattn = self.usb(SRL, "lnattn", [128, CI * HALF], BF16, side="left")
            self.wo_phase(ws, o_sb, lnattn)
            SRo.close()
            SWS.close()
            self.dbg_dump("lnattn%d" % half, lnattn[:])
            with ExitStack() as SF:
                y_sb = self.usb(SF, "y_sb", [128, CI * HALF], BF16, side="left")
                hid_sb = self.usb(SF, "hid_sb", [128, HC * TT], BF16, side="left")
                f_sb = self.usb(SF, "f_sb", [128, CI * TT], BF16, side="left")
                self.y_phase(half, lnattn, y_sb)
                self.dbg_dump("yn%d" % half, y_sb[:])
                for tt in range(2):
                    q = half * 2 + tt
                    self.ffn12_q(q, y_sb, hid_sb, first_ffn)
                    self.dbg_dump("hid%d" % q, hid_sb[:])
                    self.ffn3_q(q, hid_sb, f_sb, first_ffn)
                    self.dbg_dump("f%d" % q, f_sb[:])
                    self.final_q(q, lnattn, f_sb)
                    first_ffn = False
            SRL.close()

        with self.block() as blk:
            @blk.scalar
            def _(s):
                self.wt(s, "So", self.C["So"])


def build_nc(dbg=()):
    kb = KB(dbg)
    nc, rem = kb.build()
    return nc


def _stage_weight_qk(w):
    wT = np.ascontiguousarray(np.asarray(w, np.float32).T)
    return np.ascontiguousarray(wT.reshape(CI, 128, CI, 128).transpose(2, 1, 0, 3))


def _stage_w12(w):
    wT = np.ascontiguousarray(np.asarray(w, np.float32).T)
    return np.ascontiguousarray(wT.reshape(CI, 128, HC, 128).transpose(2, 1, 0, 3))


def _stage_w3(w):
    wT = np.ascontiguousarray(np.asarray(w, np.float32).T)
    return np.ascontiguousarray(wT.reshape(HC, 128, CI, 128).transpose(2, 1, 0, 3))


def kernel(**inputs):
    x = np.asarray(inputs["L0_attn_x"], np.float32)
    B, L, _ = x.shape
    n_cores = 8

    wq = _stage_weight_qk(inputs["L0_attn_wq"])
    wk = _stage_weight_qk(inputs["L0_attn_wk"])
    wv = _stage_weight_qk(inputs["L0_attn_wv"])
    wo = _stage_weight_qk(inputs["L0_attn_wo"])
    w1s = _stage_w12(inputs["L0_ffn_w1"])
    w2s = _stage_w12(inputs["L0_ffn_w2"])
    w3s = _stage_w3(inputs["L0_ffn_w3"])
    fmw = np.ascontiguousarray(np.stack([
        np.asarray(inputs["L0_attn_fq1_w"], np.float32).T,
        np.asarray(inputs["L0_attn_fq2_w"], np.float32).T,
        np.asarray(inputs["L0_attn_fk1_w"], np.float32).T,
        np.asarray(inputs["L0_attn_fk2_w"], np.float32).T]))
    fmb = np.ascontiguousarray(np.stack([
        np.asarray(inputs["L0_attn_fq1_b"], np.float32),
        np.asarray(inputs["L0_attn_fq2_b"], np.float32),
        np.asarray(inputs["L0_attn_fk1_b"], np.float32),
        np.asarray(inputs["L0_attn_fk2_b"], np.float32),
        np.asarray(inputs["L0_attn_norm_w"], np.float32)]))
    cmask4 = np.ascontiguousarray(
        np.tile(np.triu(np.ones((128, 128), np.float32)), (1, 4)))
    cident = np.eye(128, dtype=np.float32)
    zeros_x = np.zeros((D, TOK), np.float32)

    shared = dict(wq=wq, wk=wk, wv=wv, wo=wo, w1s=w1s, w2s=w2s, w3s=w3s,
                  fmw=fmw, fmb=fmb, cmask4=cmask4, cident=cident)
    in_maps = []
    for c in range(n_cores):
        b, half = c // 2, c % 2
        xo = np.ascontiguousarray(x[b, half * TOK:(half + 1) * TOK, :].T)
        xp = np.ascontiguousarray(x[b, 0:TOK, :].T) if half == 1 else zeros_x
        m = dict(shared)
        m["xoT"] = xo
        m["xpT"] = xp
        in_maps.append(m)

    nc = build_nc()
    res = run_bass_kernel_spmd(nc, in_maps, core_ids=list(range(n_cores)))
    out = np.empty((B, L, D), np.float32)
    for c in range(n_cores):
        b, half = c // 2, c % 2
        out[b, half * TOK:(half + 1) * TOK, :] = res.results[c]["outT"].T
    return out


# version 10
# speedup vs baseline: 1.1567x; 1.1567x over previous
import numpy as np
import ml_dtypes
from contextlib import ExitStack, contextmanager

import concourse.bass as bass
import concourse.mybir as mybir
from concourse.bass_utils import run_bass_kernel_spmd

F32 = mybir.dt.float32
BF16 = mybir.dt.bfloat16
AF = mybir.ActivationFunctionType
ALU = mybir.AluOpType
AX = mybir.AxisListType.X

D = 2048
CI = 16
H = 16
CH = 128
TOK = 2048
HALF = 1024
TT = 512
NCH = 8
FH = 5632
HC = 44
INVD = 1.0 / 2048.0
INVDK = 1.0 / 128.0
EPS_LN = 1e-5
SCALE = 128.0 ** (-0.5)
NW_ELEMS = float(FH * D)
GP = 132


class WStream:

    def __init__(self, kb, S, segments):
        self.kb = kb
        self.wb = [kb.usb(S, "wsb%d" % i, [128, 4096], BF16, side="right")
                   for i in range(2)]
        self.segs = []
        for wst, npairs in segments:
            for j in range(npairs):
                self.segs.append((wst, j))
        self.cons = {}
        self.val = {}
        self.issued = 0
        self.cursor = 0

    def claim(self, npairs):
        base = self.cursor
        self.cursor += npairs
        return base

    def pump(self, g):
        kb = self.kb
        while self.issued < len(self.segs):
            p = self.issued
            gate = self.cons.get(p - 2)
            if p >= 2 and gate is None:
                return
            if gate is not None:
                kb.wt(g, *gate)
            wst, lp = self.segs[p]
            self.val[p] = kb.dtok(g.dma_start(
                out=self.wb[p % 2][:].rearrange("p (o c k) -> p o c k",
                                                o=2, k=128),
                in_=wst[2 * lp:2 * lp + 2].rearrange("o p c k -> p o c k")),
                "Ws%d" % (p % 2))
            self.issued += 1


class _GPump:

    def __init__(self, kb, S):
        self.kb = kb
        self.ring = {
            "s": [kb.usb(S, "gsa%d" % i, [128, 2048], F32, side="left")
                  for i in range(2)],
            "g": [kb.usb(S, "gsb%d" % i, [128, 2048], F32, side="left")
                  for i in range(2)],
        }
        self.junk = kb.usb(S, "gjunk", [128, 2048], BF16, side="left")
        p12 = [kb.g1[i] for i in range(44)] + [kb.g2[i] for i in range(44)]
        p3 = [kb.g3[i] for i in range(44)]
        ALL_SCALAR = True
        if ALL_SCALAR:
            self.pieces = {"s": p12 + p3, "g": []}
            self.cols = {"s": 0, "g": 88}
        else:
            self.pieces = {"s": p12, "g": p3}
            self.cols = {"s": 0, "g": 88}
        self.sem = {"s": ["GA0", "GA1"], "g": ["GB0", "GB1"]}
        self.ni = {"s": 0, "g": 0}
        self.nr = {"s": 0, "g": 0}

    def _issue(self, eng, kind):
        kb = self.kb
        i = self.ni[kind]
        if i >= len(self.pieces[kind]):
            return
        kb.dtok(eng.dma_start(out=self.ring[kind][i % 2][:],
                              in_=self.pieces[kind][i]),
                self.sem[kind][i % 2])
        self.ni[kind] += 1

    def prime(self):
        kb = self.kb
        with kb.block() as blk:
            @blk.scalar
            def _(s):
                self._issue(s, "s")
                self._issue(s, "s")

            @blk.gpsimd
            def _(g):
                self._issue(g, "g")
                self._issue(g, "g")

    def red(self, eng, kind, k=1):
        kb = self.kb
        for _ in range(k):
            i = self.nr[kind]
            if i >= len(self.pieces[kind]):
                return
            kb.wt(eng, self.sem[kind][i % 2], 16 * (i // 2 + 1))
            col = self.cols[kind] + i
            if kind == "s":
                eng.activation(self.junk[:], self.ring[kind][i % 2][:], AF.Abs,
                               accum_out=kb.gacc[:, col:col + 1])
            else:
                eng.tensor_scalar(self.junk[:], self.ring[kind][i % 2][:],
                                  0.0, 0.0, ALU.abs_max, ALU.add,
                                  accum_out=kb.gacc[:, col:col + 1])
            self.nr[kind] += 1
            self._issue(eng, kind)

    @property
    def done(self):
        return (self.nr["s"] + self.nr["g"] == GP)


class KB:
    def __init__(self, dbg=()):
        nc = bass.Bass("TRN2", target_bir_lowering=False, debug=False,
                       detect_race_conditions=False)
        self.nc = nc
        self.es = ExitStack()
        ap = lambda *a, **k: nc.dram_tensor(*a, **k).ap()
        self.xoT = ap("xoT", [D, TOK], F32, kind="ExternalInput")
        self.xpT = ap("xpT", [D, TOK], F32, kind="ExternalInput")
        self.wq = ap("wq", [CI, 128, CI, 128], F32, kind="ExternalInput")
        self.wk = ap("wk", [CI, 128, CI, 128], F32, kind="ExternalInput")
        self.wv = ap("wv", [CI, 128, CI, 128], F32, kind="ExternalInput")
        self.wo = ap("wo", [CI, 128, CI, 128], F32, kind="ExternalInput")
        self.w1s = ap("w1s", [HC, 128, CI, 128], F32, kind="ExternalInput")
        self.w2s = ap("w2s", [HC, 128, CI, 128], F32, kind="ExternalInput")
        self.w3s = ap("w3s", [CI, 128, HC, 128], F32, kind="ExternalInput")
        self.fmw = ap("fmw", [4, 128, 128], F32, kind="ExternalInput")
        self.fmb = ap("fmb", [5, 128], F32, kind="ExternalInput")
        self.cmask4 = ap("cmask4", [128, 512], F32, kind="ExternalInput")
        self.cident = ap("cident", [128, 128], F32, kind="ExternalInput")
        self.outT = ap("outT", [D, TOK], F32, kind="ExternalOutput")
        self.qw1_sp = ap("qw1_sp", [HC, 128, 2048], BF16, kind="Internal")
        self.qw2_sp = ap("qw2_sp", [HC, 128, 2048], BF16, kind="Internal")
        self.qw3_sp = ap("qw3_sp", [CI, 128, HC * 128], BF16, kind="Internal")
        self.dbg_outs = {}
        for i, (tag, shape) in enumerate(dbg):
            self.dbg_outs[tag] = ap(f"dbg{i}", list(shape), F32, kind="ExternalOutput")

        self.xoT_t = self.xoT.rearrange("(c p) t -> c p t", p=128)
        self.xpT_t = self.xpT.rearrange("(c p) t -> c p t", p=128)
        self.g1 = self.w1s.rearrange("n p a b -> n p (a b)")
        self.g2 = self.w2s.rearrange("n p a b -> n p (a b)")
        self.g3 = self.w3s.rearrange("a p c b -> (a p c b)") \
                          .rearrange("(n p c) -> n p c", p=128, c=2048)

        sbl = lambda n, sh, dt: self.es.enter_context(nc.sbuf_tensor(n, sh, dt, side="left"))
        self.ones_col = sbl("ones_col", [128, 1], BF16)
        self.ones_row = sbl("ones_row", [1, 128], BF16)
        self.ones_row_f = sbl("ones_row_f", [1, 128], F32)
        self.ones_col_f = sbl("ones_col_f", [128, 1], F32)
        self.ident = sbl("ident", [128, 128], BF16)
        self.ident_f = sbl("ident_f", [128, 128], F32)
        self.mask4 = sbl("mask4", [128, 512], BF16)
        self.fqk = sbl("fqk", [128, 4 * 128], BF16)
        self.pcol = sbl("pcol", [128, 16], F32)
        self.chalf = sbl("chalf", [128, 2], F32)
        self.rowf = sbl("rowf", [1, 5 * TT], F32)
        self.rowb = sbl("rowb", [1, 4 * TT], BF16)
        self.cst = sbl("cst", [1, 8], F32)
        self.S32 = sbl("S32", [128, H * 128], F32)
        self.Sbf = sbl("Sbf", [128, H * 128], BF16)
        self.gacc = sbl("gacc", [128, 144], F32)

        pst = lambda n, sh, dt: self.es.enter_context(nc.psum_tensor(n, sh, dt))
        self.pA = pst("pA", [128, 512], F32)
        self.pB = pst("pB", [128, 512], F32)
        self.pC = pst("pC", [128, 512], F32)
        self.pD = pst("pD", [128, 512], F32)
        self.pE = pst("pE", [128, 512], F32)
        self.pF = pst("pF", [128, 512], F32)
        self.pT0 = pst("pT0", [128, 1024], BF16)
        self.pT1 = pst("pT1", [128, 1024], BF16)

        self.SEM = {}
        self.C = {}
        self.DMA_SEMS = ["Lg", "Ws0", "Ws1", "Fa0", "Fa1", "Q0", "Q1",
                         "Lx0", "Lx1", "GA0", "GA1", "GB0", "GB1",
                         "Sp", "So", "Db"]
        for name in ["bar", "tE", "vE", "sE", "gR"] + self.DMA_SEMS:
            self.SEM[name] = self.es.enter_context(nc.semaphore(name))
            self.C[name] = 0
        self.bank_tok = {}
        self.bar_n = 0
        self.gp = None

    @contextmanager
    def block(self):
        with self.nc.Block() as b:
            yield b

    def usb(self, S, name, shape, dt, side="right"):
        self._uid = getattr(self, "_uid", 0) + 1
        return S.enter_context(self.nc.sbuf_tensor(f"{name}_{self._uid}", shape, dt, side=side))

    def tok(self, ins, sem, n=1):
        self.C[sem] += n
        ins.then_inc(self.SEM[sem], n)
        return self.C[sem]

    def dtok(self, ins, sem):
        return self.tok(ins, sem, 16)

    def wt(self, eng, sem, val):
        if val > 0:
            eng.wait_ge(self.SEM[sem], val)

    def use_bank(self, eng, bank):
        t = self.bank_tok.get(id(bank))
        if t is not None:
            self.wt(eng, t[0], t[1])

    def rel_bank(self, bank, sem, val):
        self.bank_tok[id(bank)] = (sem, val)

    def barrier(self):
        self.bar_n += 1
        snap = {s: self.C[s] for s in self.DMA_SEMS}
        with self.block() as blk:
            @blk.sync
            def _(e):
                for s, v in snap.items():
                    self.wt(e, s, v)
        self.bank_tok = {}

    def dbg_dump(self, tag, src_ap):
        if tag not in self.dbg_outs:
            return
        with self.block() as blk:
            @blk.gpsimd
            def _(e):
                self.dtok(e.dma_start(out=self.dbg_outs[tag], in_=src_ap), "Db")
                self.wt(e, "Db", self.C["Db"])
        self.barrier()

    def consts_phase(self):
        nc, C = self.nc, self.C
        with ExitStack() as S:
            fst = self.usb(S, "fst", [128, 4 * 128], F32, side="right")
            fmb_s = self.usb(S, "fmb_s", [128, 128], F32, side="right")
            with self.block() as blk:
                @blk.gpsimd
                def _(g):
                    g.wait_ge(self.SEM["vE"], 1)
                    self.dtok(g.dma_start(out=fmb_s[0:5, :], in_=self.fmb), "Lg")
                    self.dtok(g.dma_start(out=self.ident[:], in_=self.cident), "Lg")
                    self.dtok(g.dma_start(out=self.mask4[:], in_=self.cmask4), "Lg")
                    self.dtok(g.dma_start(
                        out=fst[:].rearrange("p (n c) -> p n c", c=128),
                        in_=self.fmw.rearrange("n p c -> p n c")), "Lg")

                @blk.sync
                def _(e):
                    self.dtok(e.dma_start(out=self.ident_f[:], in_=self.cident), "Lx0")

                @blk.vector
                def _(v):
                    self.tok(v.memset(fmb_s[:], 0.0), "vE")
                    v.memset(self.ones_col[:], 1.0)
                    v.memset(self.ones_row[:], 1.0)
                    v.memset(self.ones_row_f[:], 1.0)
                    v.memset(self.ones_col_f[:], 1.0)
                    v.memset(self.cst[0:1, 0:1], EPS_LN)
                    v.memset(self.cst[0:1, 1:2], INVD)
                    v.memset(self.cst[0:1, 2:3], INVDK)
                    v.memset(self.cst[0:1, 3:4], -0.5)
                    v.memset(self.chalf[:, 0:1], 0.5)
                    v.memset(self.chalf[:, 1:2], -0.5)
                    v.memset(self.S32[:], 0.0)
                    v.memset(self.Sbf[:], 0.0)
                    self.wt(v, "Lg", C["Lg"])
                    scl = [SCALE, 1.0, 1.0, 1.0]
                    for i in range(4):
                        ins = v.tensor_scalar_mul(self.fqk[:, i * 128:(i + 1) * 128],
                                                  fst[:, i * 128:(i + 1) * 128], scl[i])
                    self.tok(ins, "vE")

                vready = C["vE"]

                @blk.tensor
                def _(t):
                    self.wt(t, "Lx0", C["Lx0"])
                    self.wt(t, "vE", vready)
                    self.tok(t.transpose(self.pA[:, 0:128], fmb_s[:], self.ident_f[:]), "tE")

                ttok = C["tE"]

                @blk.vector
                def _(v):
                    self.wt(v, "tE", ttok)
                    scl = [SCALE, 1.0, 1.0, 1.0, 1.0]
                    for i in range(5):
                        ins = v.tensor_scalar_mul(self.pcol[:, i:i + 1],
                                                  self.pA[:, i:i + 1], scl[i])
                    self.tok(ins, "vE")
            self.barrier()

    def gamma_start(self, S):
        self.gp = _GPump(self, S)
        self.gp.prime()

    def gamma_tail(self):
        assert self.gp.done, (self.gp.ni, self.gp.nr)
        self.gp = None
        nc, C = self.nc, self.C
        if True:
            with self.block() as blk:
                vred, sred = C["vE"], C["sE"]

                @blk.vector
                def _(v):
                    for m in range(3):
                        ins = v.tensor_reduce(self.gacc[:, 140 + m:141 + m],
                                              self.gacc[:, 44 * m:44 * (m + 1)],
                                              AX, ALU.add)
                    self.tok(ins, "vE")

                vcols = C["vE"]
                t_base = C["tE"]

                @blk.tensor
                def _(t):
                    self.wt(t, "vE", vcols)
                    self.tok(t.matmul(self.pB[0:1, 0:3], self.ones_col_f[:],
                                      self.gacc[:, 140:143],
                                      start=True, stop=True), "tE")

                @blk.vector
                def _(v):
                    self.wt(v, "tE", t_base + 1)
                    v.tensor_copy(self.rowf[0:1, 3 * TT:4 * TT],
                                  self.pB[0:1, 0:512])
                    v.tensor_scalar(self.rowf[0:1, 4 * TT:5 * TT],
                                    self.rowf[0:1, 3 * TT:4 * TT],
                                    0.5 / NW_ELEMS, 0.5e-8, ALU.mult, ALU.add)
                    v.tensor_scalar_mul(self.rowf[0:1, 3 * TT:4 * TT],
                                        self.rowf[0:1, 4 * TT:5 * TT], -1.0)
                    v.reciprocal(self.rowf[0:1, 2 * TT:3 * TT],
                                 self.rowf[0:1, 4 * TT:5 * TT])
                    self.tok(v.tensor_scalar_mul(self.rowf[0:1, 2 * TT:3 * TT],
                                                 self.rowf[0:1, 2 * TT:3 * TT],
                                                 0.5), "vE")

                vthr = C["vE"]

                @blk.tensor
                def _(t):
                    self.wt(t, "vE", vthr)
                    t.matmul(self.pC[:, 0:3], self.ones_row_f[:],
                             self.rowf[0:1, 4 * TT:4 * TT + 3],
                             start=True, stop=True)
                    t.matmul(self.pC[:, 3:6], self.ones_row_f[:],
                             self.rowf[0:1, 3 * TT:3 * TT + 3],
                             start=True, stop=True)
                    mm = t.matmul(self.pC[:, 6:9], self.ones_row_f[:],
                                  self.rowf[0:1, 2 * TT:2 * TT + 3],
                                  start=True, stop=True)
                    self.tok(mm, "tE")

                trep = C["tE"]

                @blk.vector
                def _(v):
                    self.wt(v, "tE", trep)
                    self.tok(v.tensor_copy(self.pcol[:, 5:15], self.pC[:, 0:10]), "vE")
            self.barrier()

    def ln_rows_apply(self, buf, banks, stats_tok, rep_bf, apply=True):
        nc, C = self.nc, self.C
        rm = self.rowf[0:1, 0:TT]
        rs = self.rowf[0:1, TT:2 * TT]
        ra = self.rowf[0:1, 2 * TT:3 * TT]
        bm = self.rowb[0:1, 0:TT]
        bs = self.rowb[0:1, TT:2 * TT]
        toks = []
        for tti in range(2):
            s1, s2 = banks[tti]
            with self.block() as blk:
                @blk.vector
                def _(v):
                    self.wt(v, "tE", stats_tok[tti])
                    v.tensor_scalar_mul(rm, s1[0:1, :], INVD)
                    v.tensor_scalar_mul(rs, s2[0:1, :], INVD)
                    v.tensor_mul(ra, rm, rm)
                    self.tok(v.tensor_sub(rs, rs, ra), "vE")

                vvar = C["vE"]

                @blk.scalar
                def _(s):
                    self.wt(s, "vE", vvar)
                    s.activation(rs, rs, AF.Ln, bias=self.cst[0:1, 0:1])
                    s.activation(rs, rs, AF.Exp, scale=self.cst[0:1, 3:4])
                    s.copy(bm, rm)
                    self.tok(s.copy(bs, rs), "sE")
                    if self.gp:
                        self.gp.red(s, "s", 1)

                srows = C["sE"]

                @blk.tensor
                def _(t):
                    self.wt(t, "sE", srows)
                    self.use_bank(t, self.pC)
                    self.use_bank(t, self.pD)
                    t.matmul(self.pC[:], self.ones_row[:], bm, start=True, stop=True)
                    self.tok(t.matmul(self.pD[:], self.ones_row[:], bs,
                                      start=True, stop=True), "tE")

                trep = C["tE"]

                @blk.vector
                def _(v):
                    self.wt(v, "tE", trep)
                    v.tensor_copy(rep_bf[:, 0:512], self.pC[:])
                    ins = v.tensor_copy(rep_bf[:, 512:1024], self.pD[:])
                    if apply:
                        for ci in range(CI):
                            tile = buf[:, ci * HALF + tti * TT:
                                       ci * HALF + tti * TT + TT]
                            v.tensor_sub(tile, tile, rep_bf[:, 0:512])
                            ins = v.tensor_mul(tile, tile, rep_bf[:, 512:1024])
                    self.tok(ins, "vE")
            self.rel_bank(self.pC, "vE", C["vE"])
            self.rel_bank(self.pD, "vE", C["vE"])
            toks.append(C["vE"])
        return toks

    def x_phase(self, srcT_t, buf, half, ws=None):
        nc, C = self.nc, self.C
        with ExitStack() as S:
            sq16 = self.usb(S, "sq16", [128, CI * TT], BF16, side="right")
            rep = self.usb(S, "repx", [128, 1024], BF16, side="right")
            h0 = half * HALF
            with self.block() as blk:
                @blk.gpsimd
                def _(g):
                    for ci in range(CI):
                        self.dtok(g.dma_start(out=buf[:, ci * HALF:(ci + 1) * HALF],
                                              in_=srcT_t[ci][:, h0:h0 + HALF]), "Lg")
                    if ws is not None:
                        ws.pump(g)
                    if self.gp:
                        self.gp.red(g, "g", 2)

                lg_full = C["Lg"]
                t_base = C["tE"]
                sq_toks = {}

                @blk.vector
                def _(v):
                    self.wt(v, "Lg", lg_full)
                    for tti in range(2):
                        if tti == 1:
                            self.wt(v, "tE", t_base + 1)
                        for ci in range(CI):
                            tile = buf[:, ci * HALF + tti * TT:
                                       ci * HALF + tti * TT + TT]
                            ins = v.tensor_mul(sq16[:, ci * TT:(ci + 1) * TT],
                                               tile, tile)
                            sq_toks[(tti, ci)] = self.tok(ins, "vE")

                stats_tok = {}

                @blk.tensor
                def _(t):
                    bankmap = {0: (self.pA, self.pB), 1: (self.pE, self.pF)}
                    for tti in range(2):
                        s1, s2 = bankmap[tti]
                        self.use_bank(t, s1)
                        self.use_bank(t, s2)
                        for ci in range(CI):
                            self.wt(t, "vE", sq_toks[(tti, ci)])
                            tile = buf[:, ci * HALF + tti * TT:
                                       ci * HALF + tti * TT + TT]
                            t.matmul(s1[0:1, :], self.ones_col[:], tile,
                                     start=(ci == 0), stop=(ci == CI - 1))
                            ins = t.matmul(s2[0:1, :], self.ones_col[:],
                                           sq16[:, ci * TT:(ci + 1) * TT],
                                           start=(ci == 0), stop=(ci == CI - 1))
                        stats_tok[tti] = self.tok(ins, "tE")
                assert stats_tok[0] == t_base + 1

            banks = {0: (self.pA, self.pB), 1: (self.pE, self.pF)}
            vt = self.ln_rows_apply(buf, banks, stats_tok, rep)
            self.rel_bank(self.pA, "vE", vt[0])
            self.rel_bank(self.pB, "vE", vt[0])
            self.rel_bank(self.pE, "vE", vt[1])
            self.rel_bank(self.pF, "vE", vt[1])
            self.barrier()

    def proj_phase(self, ws, xn, dest, kind):
        nc, C = self.nc, self.C
        with ExitStack() as S:
            wb = ws.wb
            fmap = kind in ("fq", "fk")
            if fmap:
                raw = self.usb(S, "rawfm", [128, 2 * TT], BF16, side="right")
                t1s = self.usb(S, "t1fm", [128, 2 * TT], BF16, side="right")
                fw1 = self.fqk[:, 0:128] if kind == "fq" else self.fqk[:, 256:384]
                fw2 = self.fqk[:, 128:256] if kind == "fq" else self.fqk[:, 384:512]
                b1 = self.pcol[:, 0:1] if kind == "fq" else self.pcol[:, 2:3]
                b2 = self.pcol[:, 1:2] if kind == "fq" else self.pcol[:, 3:4]

            t0, v0, s0 = C["tE"], C["vE"], C["sE"]
            base = ws.claim(CI // 2)
            tper = 2 if fmap else 1
            Tproj = {}
            Tf = {}
            n = 0
            for oc in range(CI):
                for tt in range(2):
                    n += 1
                    Tproj[(oc, tt)] = t0 + n
                    if fmap:
                        n += 1
                        Tf[(oc, tt)] = t0 + n
            Vr = {}
            Ve = {}
            n = 0
            for oc in range(CI):
                for tt in range(2):
                    if fmap:
                        n += 1
                        Vr[(oc, tt)] = v0 + n
                        n += 1
                        Ve[(oc, tt)] = v0 + n
                    else:
                        n += 1
                        Ve[(oc, tt)] = v0 + n
            Sf = {}
            if fmap:
                for i, key in enumerate([(oc, tt) for oc in range(CI) for tt in range(2)]):
                    Sf[key] = s0 + i + 1
            for lp in range(CI // 2):
                ws.cons[base + lp] = ("tE", Tproj[(2 * lp + 1, 1)])

            with self.block() as blk:
                @blk.gpsimd
                def _(g):
                    ws.pump(g)
                    if self.gp:
                        self.gp.red(g, "g", 16)

                @blk.tensor
                def _(t):
                    pp = [self.pA, self.pB]
                    pf1 = [self.pC, self.pE]
                    pf2 = [self.pD, self.pF]
                    for oc in range(CI):
                        gp_, osub = base + oc // 2, oc % 2
                        self.wt(t, "Ws%d" % (gp_ % 2), ws.val[gp_])
                        for tt in range(2):
                            par = (oc * 2 + tt) % 2
                            self.use_bank(t, pp[par])
                            for ci in range(CI):
                                ins = t.matmul(pp[par][:],
                                               wb[gp_ % 2][:, osub * 2048 + ci * 128:
                                                  osub * 2048 + (ci + 1) * 128],
                                               xn[:, ci * HALF + tt * TT:
                                                  ci * HALF + tt * TT + TT],
                                               start=(ci == 0), stop=(ci == CI - 1))
                            assert self.tok(ins, "tE") == Tproj[(oc, tt)]
                            self.rel_bank(pp[par], "vE",
                                          Vr[(oc, tt)] if fmap else Ve[(oc, tt)])
                            if fmap:
                                self.use_bank(t, pf1[par])
                                self.use_bank(t, pf2[par])
                                self.wt(t, "vE", Vr[(oc, tt)])
                                t.matmul(pf1[par][:], fw1,
                                         raw[:, par * TT:(par + 1) * TT],
                                         start=True, stop=True)
                                ins = t.matmul(pf2[par][:], fw2,
                                               raw[:, par * TT:(par + 1) * TT],
                                               start=True, stop=True)
                                assert self.tok(ins, "tE") == Tf[(oc, tt)]
                                self.rel_bank(pf1[par], "sE", Sf[(oc, tt)])
                                self.rel_bank(pf2[par], "vE", Ve[(oc, tt)])

                @blk.vector
                def _(v):
                    pp = [self.pA, self.pB]
                    pf2 = [self.pD, self.pF]
                    for oc in range(CI):
                        for tt in range(2):
                            par = (oc * 2 + tt) % 2
                            dst = dest[:, oc * HALF + tt * TT:
                                       oc * HALF + tt * TT + TT]
                            if fmap:
                                self.wt(v, "tE", Tproj[(oc, tt)])
                                assert self.tok(
                                    v.tensor_copy(raw[:, par * TT:(par + 1) * TT],
                                                  pp[par][:]), "vE") == Vr[(oc, tt)]
                                self.wt(v, "tE", Tf[(oc, tt)])
                                self.wt(v, "sE", Sf[(oc, tt)])
                                ins = v.scalar_tensor_tensor(
                                    dst, pf2[par][:], b2,
                                    t1s[:, par * TT:(par + 1) * TT],
                                    ALU.add, ALU.mult)
                                assert self.tok(ins, "vE") == Ve[(oc, tt)]
                            else:
                                self.wt(v, "tE", Tproj[(oc, tt)])
                                assert self.tok(v.tensor_copy(dst, pp[par][:]),
                                                "vE") == Ve[(oc, tt)]

                if not fmap and self.gp:
                    @blk.scalar
                    def _(s):
                        for _ in range(CI):
                            self.gp.red(s, "s", 2)

                if fmap:
                    @blk.scalar
                    def _(s):
                        pf1 = [self.pC, self.pE]
                        keysf = [(oc, tt) for oc in range(CI) for tt in range(2)]
                        for i, (oc, tt) in enumerate(keysf):
                            par = (oc * 2 + tt) % 2
                            self.wt(s, "tE", Tf[(oc, tt)])
                            if i >= 2:
                                self.wt(s, "vE", Ve[keysf[i - 2]])
                            ins = s.add(t1s[:, par * TT:(par + 1) * TT],
                                        pf1[par][:], b1)
                            assert self.tok(ins, "sE") == Sf[(oc, tt)]
                            if self.gp:
                                self.gp.red(s, "s", 1)
            self.barrier()

    def attn_half(self, half, q_sb, k_sb, v_sb, o_sb, full=True):
        nc, C = self.nc, self.C
        with ExitStack() as S:
            tm = self.usb(S, "tm", [128, 2048], BF16, side="right")
            sscr = self.usb(S, "sscr", [128, 1024], BF16, side="right")
            o2scr = self.usb(S, "o2scr", [128, 1024], BF16, side="right")

            t0, v0, s0 = C["tE"], C["vE"], C["sE"]
            tn, vn, sn = [t0], [v0], [s0]
            Ttp, Ts, To, Td = {}, {}, {}, {}
            Vtm, Vmask, Vo, Vs, Sc = {}, {}, {}, {}, {}
            Vo2, Tst, Sr, Trp, Vap = {}, {}, {}, {}, {}
            for c in range(NCH):
                for g in range(4):
                    tn[0] += 1; Ttp[(c, g)] = tn[0]
                    if full:
                        tn[0] += 1; Ts[(c, g)] = tn[0]
                        tn[0] += 1; To[(c, g)] = tn[0]
                    tn[0] += 1; Td[(c, g)] = tn[0]
                    vn[0] += 1; Vtm[(c, g)] = vn[0]
                    if full:
                        vn[0] += 1; Vmask[(c, g)] = vn[0]
                        vn[0] += 1; Vo[(c, g)] = vn[0]
                    vn[0] += 1; Vs[(c, g)] = vn[0]
                    sn[0] += 1; Sc[(c, g)] = sn[0]
                if full and c % 4 == 3:
                    tti = c // 4
                    for h in range(H):
                        vn[0] += 1; Vo2[(tti, h)] = vn[0]
                        tn[0] += 1; Tst[(tti, h)] = tn[0]
                        sn[0] += 1; Sr[(tti, h)] = sn[0]
                        tn[0] += 1; Trp[(tti, h)] = tn[0]
                        vn[0] += 1; Vap[(tti, h)] = vn[0]

            def prev2(c, g):
                i = c * 4 + g - 2
                return None if i < 0 else (i // 4, i % 4)

            def prevh(tti, h):
                return None if h < 2 else (tti, h - 2)

            qcol = lambda hd, c: hd * HALF + c * CH
            rfr = [self.rowf[0:1, 3 * TT:4 * TT], self.rowf[0:1, 4 * TT:5 * TT]]
            rbr = [self.rowb[0:1, 2 * TT:3 * TT], self.rowb[0:1, 3 * TT:4 * TT]]

            with self.block() as blk:
                @blk.tensor
                def _(t):
                    pT = [self.pT0, self.pT1]
                    pS = [self.pA, self.pB]
                    pO = [self.pC, self.pD]
                    pDl = [self.pE, self.pF]
                    for c in range(NCH):
                        for g in range(4):
                            par = g % 2
                            p2 = prev2(c, g)
                            if p2:
                                self.wt(t, "vE", Vtm[p2])
                            for j in range(4):
                                hd = 4 * g + j
                                t.transpose(pT[par][:, j * 128:(j + 1) * 128],
                                            v_sb[:, qcol(hd, c):qcol(hd, c) + CH],
                                            self.ident[:])
                            for j in range(4):
                                hd = 4 * g + j
                                ins = t.transpose(pT[par][:, 512 + j * 128:512 + (j + 1) * 128],
                                                  k_sb[:, qcol(hd, c):qcol(hd, c) + CH],
                                                  self.ident[:])
                            assert self.tok(ins, "tE") == Ttp[(c, g)]
                            if full:
                                if p2:
                                    self.wt(t, "vE", Vmask[p2])
                                if c == 4:
                                    self.wt(t, "sE", Sr[(0, 14 + par)])
                                for j in range(4):
                                    hd = 4 * g + j
                                    ins = t.matmul(pS[par][:, j * 128:(j + 1) * 128],
                                                   k_sb[:, qcol(hd, c):qcol(hd, c) + CH],
                                                   q_sb[:, qcol(hd, c):qcol(hd, c) + CH],
                                                   start=True, stop=True)
                                assert self.tok(ins, "tE") == Ts[(c, g)]
                                if p2:
                                    self.wt(t, "vE", Vo[p2])
                                if c == 4:
                                    self.wt(t, "vE", Vap[(0, 14 + par)])
                                if c > 0:
                                    self.wt(t, "sE", Sc[(c - 1, g)])
                                self.wt(t, "vE", Vmask[(c, g)])
                                self.wt(t, "vE", Vtm[(c, g)])
                                for j in range(4):
                                    hd = 4 * g + j
                                    t.matmul(pO[par][:, j * 128:(j + 1) * 128],
                                             self.Sbf[:, hd * 128:(hd + 1) * 128],
                                             q_sb[:, qcol(hd, c):qcol(hd, c) + CH],
                                             start=True, stop=False)
                                    ins = t.matmul(pO[par][:, j * 128:(j + 1) * 128],
                                                   tm[:, par * 1024 + j * 128:
                                                      par * 1024 + (j + 1) * 128],
                                                   sscr[:, par * 512 + j * 128:
                                                        par * 512 + (j + 1) * 128],
                                                   start=False, stop=True)
                                assert self.tok(ins, "tE") == To[(c, g)]
                            if p2:
                                self.wt(t, "vE", Vs[p2])
                            if not full:
                                self.wt(t, "vE", Vtm[(c, g)])
                            for j in range(4):
                                hd = 4 * g + j
                                ins = t.matmul(pDl[par][:, j * 128:(j + 1) * 128],
                                               tm[:, par * 1024 + 512 + j * 128:
                                                  par * 1024 + 512 + (j + 1) * 128],
                                               tm[:, par * 1024 + j * 128:
                                                  par * 1024 + (j + 1) * 128],
                                               start=True, stop=True)
                            assert self.tok(ins, "tE") == Td[(c, g)]
                        if full and c % 4 == 3:
                            tti = c // 4
                            pSt = [self.pA, self.pB]
                            pRp = [self.pC, self.pD]
                            for h in range(H):
                                par = h % 2
                                ph = prevh(tti, h)
                                self.wt(t, "vE", Vo2[(tti, h)])
                                if ph:
                                    self.wt(t, "sE", Sr[ph])
                                else:
                                    self.wt(t, "vE", Vmask[(c, 2 + par)])
                                ins = t.matmul(pSt[par][0:1, :], self.ones_col[:],
                                               o2scr[:, par * TT:(par + 1) * TT],
                                               start=True, stop=True)
                                assert self.tok(ins, "tE") == Tst[(tti, h)]
                                self.wt(t, "sE", Sr[(tti, h)])
                                if ph:
                                    self.wt(t, "vE", Vap[ph])
                                else:
                                    self.wt(t, "vE", Vo[(c, 2 + par)])
                                ins = t.matmul(pRp[par][:], self.ones_row[:],
                                               rbr[par], start=True, stop=True)
                                assert self.tok(ins, "tE") == Trp[(tti, h)]

                @blk.vector
                def _(v):
                    pT = [self.pT0, self.pT1]
                    pS = [self.pA, self.pB]
                    pO = [self.pC, self.pD]
                    pDl = [self.pE, self.pF]
                    for c in range(NCH):
                        for g in range(4):
                            par = g % 2
                            p2 = prev2(c, g)
                            self.wt(v, "tE", Ttp[(c, g)])
                            if p2:
                                self.wt(v, "tE", Td[p2])
                            ins = v.tensor_copy(tm[:, par * 1024:(par + 1) * 1024],
                                                pT[par][:])
                            assert self.tok(ins, "vE") == Vtm[(c, g)]
                            if full:
                                self.wt(v, "tE", Ts[(c, g)])
                                if p2:
                                    self.wt(v, "tE", To[p2])
                                ins = v.tensor_mul(sscr[:, par * 512:(par + 1) * 512],
                                                   pS[par][:], self.mask4[:])
                                assert self.tok(ins, "vE") == Vmask[(c, g)]
                                self.wt(v, "tE", To[(c, g)])
                                dst = o_sb[:].rearrange("p (h t) -> p h t", t=HALF)[
                                    :, 4 * g:4 * g + 4, c * CH:(c + 1) * CH]
                                src = pO[par][:].rearrange("p (h t) -> p h t", t=CH)
                                ins = v.tensor_copy(dst, src)
                                assert self.tok(ins, "vE") == Vo[(c, g)]
                            self.wt(v, "tE", Td[(c, g)])
                            if c > 0:
                                self.wt(v, "sE", Sc[(c - 1, g)])
                            sl = self.S32[:, g * 512:(g + 1) * 512]
                            ins = v.tensor_add(sl, sl, pDl[par][:])
                            assert self.tok(ins, "vE") == Vs[(c, g)]
                        if full and c % 4 == 3:
                            tti = c // 4
                            pRp = [self.pC, self.pD]
                            for h in range(H):
                                par = h % 2
                                ph = prevh(tti, h)
                                otile = o_sb[:, h * HALF + tti * TT:
                                             h * HALF + tti * TT + TT]
                                if ph:
                                    self.wt(v, "tE", Tst[ph])
                                ins = v.tensor_mul(o2scr[:, par * TT:(par + 1) * TT],
                                                   otile, otile)
                                assert self.tok(ins, "vE") == Vo2[(tti, h)]
                                self.wt(v, "tE", Trp[(tti, h)])
                                ins = v.scalar_tensor_tensor(
                                    otile, otile, self.pcol[:, 4:5], pRp[par][:],
                                    ALU.mult, ALU.mult)
                                assert self.tok(ins, "vE") == Vap[(tti, h)]

                @blk.scalar
                def _(s):
                    pSt = [self.pA, self.pB]
                    for c in range(NCH):
                        for g in range(4):
                            self.wt(s, "vE", Vs[(c, g)])
                            if full:
                                self.wt(s, "tE", To[(c, g)])
                            ins = s.copy(self.Sbf[:, g * 512:(g + 1) * 512],
                                         self.S32[:, g * 512:(g + 1) * 512])
                            assert self.tok(ins, "sE") == Sc[(c, g)]
                            if self.gp:
                                self.gp.red(s, "s", 1)
                        if full and c % 4 == 3:
                            tti = c // 4
                            for h in range(H):
                                par = h % 2
                                self.wt(s, "tE", Tst[(tti, h)])
                                s.activation(rfr[par], pSt[par][0:1, :], AF.Ln,
                                             bias=self.cst[0:1, 0:1],
                                             scale=self.cst[0:1, 2:3])
                                s.activation(rfr[par], rfr[par], AF.Exp,
                                             scale=self.cst[0:1, 3:4])
                                ins = s.copy(rbr[par], rfr[par])
                                assert self.tok(ins, "sE") == Sr[(tti, h)]

            for b in [self.pA, self.pB]:
                self.rel_bank(b, "sE", C["sE"])
            for b in [self.pC, self.pD]:
                self.rel_bank(b, "vE", C["vE"])
            for b in [self.pE, self.pF]:
                self.rel_bank(b, "vE", C["vE"])
            self.barrier()

    def wo_phase(self, ws, o_sb, lnattn):
        nc, C = self.nc, self.C
        with ExitStack() as S:
            wb = ws.wb
            sq4 = self.usb(S, "sq4", [128, 4 * TT], BF16, side="right")
            rep = self.usb(S, "repw", [128, 1024], BF16, side="right")

            t0, v0 = C["tE"], C["vE"]
            base = ws.claim(CI // 2)
            Tproj = {}
            Tst = {}
            n = 0
            for oc in range(CI):
                for tt in range(2):
                    n += 1
                    Tproj[(oc, tt)] = t0 + n
                    n += 1
                    Tst[(oc, tt)] = t0 + n
            Vev = {}
            n = 0
            for oc in range(CI):
                for tt in range(2):
                    n += 1
                    Vev[(oc, tt)] = v0 + n
            for lp in range(CI // 2):
                ws.cons[base + lp] = ("tE", Tproj[(2 * lp + 1, 1)])

            with self.block() as blk:
                @blk.gpsimd
                def _(g):
                    ws.pump(g)

                @blk.tensor
                def _(t):
                    pp = [self.pA, self.pB]
                    stats = {0: (self.pC, self.pD), 1: (self.pE, self.pF)}
                    for b in [self.pC, self.pD, self.pE, self.pF]:
                        self.use_bank(t, b)
                    for oc in range(CI):
                        gp_, osub = base + oc // 2, oc % 2
                        self.wt(t, "Ws%d" % (gp_ % 2), ws.val[gp_])
                        for tt in range(2):
                            par = (oc * 2 + tt) % 2
                            self.use_bank(t, pp[par])
                            for ci in range(CI):
                                ins = t.matmul(pp[par][:],
                                               wb[gp_ % 2][:, osub * 2048 + ci * 128:
                                                  osub * 2048 + (ci + 1) * 128],
                                               o_sb[:, ci * HALF + tt * TT:
                                                    ci * HALF + tt * TT + TT],
                                               start=(ci == 0), stop=(ci == CI - 1))
                            assert self.tok(ins, "tE") == Tproj[(oc, tt)]
                            self.rel_bank(pp[par], "vE", Vev[(oc, tt)])
                            s1, s2 = stats[tt]
                            self.wt(t, "vE", Vev[(oc, tt)])
                            t.matmul(s1[0:1, :], self.ones_col[:],
                                     lnattn[:, oc * HALF + tt * TT:
                                            oc * HALF + tt * TT + TT],
                                     start=(oc == 0), stop=(oc == CI - 1))
                            ins = t.matmul(s2[0:1, :], self.ones_col[:],
                                           sq4[:, ((oc % 2) * 2 + tt) * TT:
                                               ((oc % 2) * 2 + tt) * TT + TT],
                                           start=(oc == 0), stop=(oc == CI - 1))
                            assert self.tok(ins, "tE") == Tst[(oc, tt)]

                @blk.vector
                def _(v):
                    pp = [self.pA, self.pB]
                    for oc in range(CI):
                        for tt in range(2):
                            par = (oc * 2 + tt) % 2
                            self.wt(v, "tE", Tproj[(oc, tt)])
                            if oc >= 2:
                                self.wt(v, "tE", Tst[(oc - 2, tt)])
                            dst = lnattn[:, oc * HALF + tt * TT:
                                         oc * HALF + tt * TT + TT]
                            v.tensor_copy(dst, pp[par][:])
                            ins = v.tensor_mul(sq4[:, ((oc % 2) * 2 + tt) * TT:
                                                   ((oc % 2) * 2 + tt) * TT + TT],
                                               dst, dst)
                            assert self.tok(ins, "vE") == Vev[(oc, tt)]

            stats_tok = {0: Tst[(CI - 1, 0)], 1: Tst[(CI - 1, 1)]}
            banks = {0: (self.pC, self.pD), 1: (self.pE, self.pF)}
            vt = self.ln_rows_apply(lnattn, banks, stats_tok, rep)
            for b in [self.pC, self.pD, self.pE, self.pF]:
                self.rel_bank(b, "vE", vt[1])
            self.barrier()

    def y_phase(self, half, lnattn, y_sb):
        nc, C = self.nc, self.C
        with ExitStack() as S:
            xstg0 = self.usb(S, "xstg0", [128, TT], F32, side="right")
            xstg1 = self.usb(S, "xstg1", [128, TT], F32, side="right")
            sq2 = self.usb(S, "sq2", [128, 2 * TT], BF16, side="right")
            rep = self.usb(S, "repy", [128, 1024], BF16, side="right")
            xstg = [xstg0, xstg1]
            h0 = half * HALF

            t0, v0 = C["tE"], C["vE"]
            lx0_0, lx1_0 = C["Lx0"], C["Lx1"]
            keys = [(tt, ci) for tt in range(2) for ci in range(CI)]
            Vy = {k: v0 + 2 * i + 1 for i, k in enumerate(keys)}
            Vsq = {k: v0 + 2 * i + 2 for i, k in enumerate(keys)}
            Tst = {k: t0 + i + 1 for i, k in enumerate(keys)}

            with self.block() as blk:
                @blk.sync
                def _(e):
                    for i, (tt, ci) in enumerate(keys):
                        if i >= 2:
                            self.wt(e, "vE", Vsq[keys[i - 2]])
                        self.dtok(e.dma_start(
                            out=xstg[i % 2][:],
                            in_=self.xoT_t[ci][:, h0 + tt * TT:h0 + tt * TT + TT]),
                            "Lx%d" % (i % 2))

                @blk.vector
                def _(v):
                    for i, (tt, ci) in enumerate(keys):
                        self.wt(v, "Lx%d" % (i % 2),
                                (lx0_0 if i % 2 == 0 else lx1_0) + 16 * (i // 2 + 1))
                        if i >= 2:
                            self.wt(v, "tE", Tst[keys[i - 2]])
                        ytile = y_sb[:, ci * HALF + tt * TT: ci * HALF + tt * TT + TT]
                        la = lnattn[:, ci * HALF + tt * TT: ci * HALF + tt * TT + TT]
                        ins = v.tensor_add(ytile, xstg[i % 2][:], la)
                        assert self.tok(ins, "vE") == Vy[(tt, ci)]
                        ins = v.tensor_mul(sq2[:, (i % 2) * TT:(i % 2 + 1) * TT],
                                           ytile, ytile)
                        assert self.tok(ins, "vE") == Vsq[(tt, ci)]

                @blk.tensor
                def _(t):
                    bankmap = {0: (self.pA, self.pB), 1: (self.pE, self.pF)}
                    for b in [self.pA, self.pB, self.pE, self.pF]:
                        self.use_bank(t, b)
                    for i, (tt, ci) in enumerate(keys):
                        s1, s2 = bankmap[tt]
                        self.wt(t, "vE", Vsq[(tt, ci)])
                        t.matmul(s1[0:1, :], self.ones_col[:],
                                 y_sb[:, ci * HALF + tt * TT: ci * HALF + tt * TT + TT],
                                 start=(ci == 0), stop=(ci == CI - 1))
                        ins = t.matmul(s2[0:1, :], self.ones_col[:],
                                       sq2[:, (i % 2) * TT:(i % 2 + 1) * TT],
                                       start=(ci == 0), stop=(ci == CI - 1))
                        assert self.tok(ins, "tE") == Tst[(tt, ci)]

            stats_tok = {0: Tst[(0, CI - 1)], 1: Tst[(1, CI - 1)]}
            banks = {0: (self.pA, self.pB), 1: (self.pE, self.pF)}
            vt = self.ln_rows_apply(y_sb, banks, stats_tok, rep)
            for b in [self.pA, self.pB, self.pE, self.pF]:
                self.rel_bank(b, "vE", vt[1])
            self.barrier()

    def ffn12_q(self, q, y_sb, hid_sb, first):
        nc, C = self.nc, self.C
        half, tt = q // 2, q % 2
        with ExitStack() as S:
            stg0 = self.usb(S, "stg0", [128, 2048], F32, side="right")
            stg1 = self.usb(S, "stg1", [128, 2048], F32, side="right")
            qb10 = self.usb(S, "qb10", [128, 2048], BF16, side="right")
            qb11 = self.usb(S, "qb11", [128, 2048], BF16, side="right")
            qb20 = self.usb(S, "qb20", [128, 2048], BF16, side="right")
            qb21 = self.usb(S, "qb21", [128, 2048], BF16, side="right")
            tsc = self.usb(S, "tsc", [128, 6144], BF16, side="right")
            sig = self.usb(S, "sig", [128, 1024], BF16, side="right")
            stg = [stg0, stg1]
            qb1 = [qb10, qb11]
            qb2 = [qb20, qb21]

            t0, v0, s0 = C["tE"], C["vE"], C["sE"]
            sp0 = C["Sp"]
            fa0_0, fa1_0 = C["Fa0"], C["Fa1"]
            q0_0, q1_0 = C["Q0"], C["Q1"]
            T1 = {hc: t0 + 2 * hc + 1 for hc in range(HC)}
            T2 = {hc: t0 + 2 * hc + 2 for hc in range(HC)}
            Ssig = {hc: s0 + hc + 1 for hc in range(HC)}
            if first:
                Vq1 = {hc: v0 + 3 * hc + 1 for hc in range(HC)}
                Vq2 = {hc: v0 + 3 * hc + 2 for hc in range(HC)}
                Vh = {hc: v0 + 3 * hc + 3 for hc in range(HC)}
            else:
                Vh = {hc: v0 + hc + 1 for hc in range(HC)}

            with self.block() as blk:
                if first:
                    Sa, Sb, Ssig = {}, {}, {}
                    n = s0
                    for hc in range(HC):
                        for which in range(2):
                            i = 2 * hc + which
                            n += 1
                            Sa[i] = n
                            n += 1
                            Sb[i] = n
                        if hc >= 1:
                            n += 1
                            Ssig[hc - 1] = n
                    n += 1
                    Ssig[HC - 1] = n

                    @blk.gpsimd
                    def _(g):
                        for i in range(2 * HC):
                            hc, which = i // 2, i % 2
                            if i >= 2:
                                self.wt(g, "sE", Sb[i - 2])
                            src = (self.w1s if which == 0 else self.w2s)[hc]
                            self.dtok(g.dma_start(
                                out=stg[i % 2][:].rearrange("p (c o) -> p c o", o=128),
                                in_=src), "Fa%d" % (i % 2))

                    @blk.scalar
                    def _(s):
                        for hc in range(HC):
                            for which in range(2):
                                i = 2 * hc + which
                                self.wt(s, "Fa%d" % (i % 2),
                                        (fa0_0 if i % 2 == 0 else fa1_0) + 16 * (i // 2 + 1))
                                inv = self.pcol[:, 11:12] if which == 0 else self.pcol[:, 12:13]
                                qb = (qb1 if which == 0 else qb2)[hc % 2]
                                if hc >= 2:
                                    self.wt(s, "tE", T2[hc - 2])
                                    self.wt(s, "Sp", sp0 + 16 * 2 * (hc - 1))
                                ins = s.activation(qb[:], stg[i % 2][:], AF.Sign,
                                                   scale=inv, bias=self.chalf[:, 0:1])
                                assert self.tok(ins, "sE") == Sa[i]
                                if i >= 2:
                                    self.wt(s, "vE", (Vq1 if i % 2 == 0 else Vq2)[(i - 2) // 2])
                                ins = s.activation(tsc[:, (i % 2) * 2048:(i % 2 + 1) * 2048],
                                                   stg[i % 2][:], AF.Sign,
                                                   scale=inv, bias=self.chalf[:, 1:2])
                                assert self.tok(ins, "sE") == Sb[i]
                            if hc >= 1:
                                h = hc - 1
                                self.wt(s, "tE", T1[h])
                                if h >= 2:
                                    self.wt(s, "vE", Vh[h - 2])
                                ins = s.activation(sig[:, (h % 2) * TT:(h % 2 + 1) * TT],
                                                   self.pA[:] if h % 2 == 0 else self.pC[:],
                                                   AF.Sigmoid, scale=self.chalf[:, 0:1])
                                assert self.tok(ins, "sE") == Ssig[h]
                        h = HC - 1
                        self.wt(s, "tE", T1[h])
                        self.wt(s, "vE", Vh[h - 2])
                        ins = s.activation(sig[:, (h % 2) * TT:(h % 2 + 1) * TT],
                                           self.pA[:] if h % 2 == 0 else self.pC[:],
                                           AF.Sigmoid, scale=self.chalf[:, 0:1])
                        assert self.tok(ins, "sE") == Ssig[h]

                    @blk.sync
                    def _(e):
                        for hc in range(HC):
                            self.wt(e, "vE", Vq1[hc])
                            self.dtok(e.dma_start(out=self.qw1_sp[hc],
                                                  in_=qb1[hc % 2][:]), "Sp")
                            self.wt(e, "vE", Vq2[hc])
                            self.dtok(e.dma_start(out=self.qw2_sp[hc],
                                                  in_=qb2[hc % 2][:]), "Sp")

                    @blk.vector
                    def _(v):
                        for hc in range(HC):
                            for which in range(2):
                                i = 2 * hc + which
                                self.wt(v, "sE", Sb[i])
                                qb = (qb1 if which == 0 else qb2)[hc % 2]
                                ins = v.tensor_add(qb[:], qb[:],
                                                   tsc[:, (i % 2) * 2048:(i % 2 + 1) * 2048])
                                got = self.tok(ins, "vE")
                                assert got == (Vq1 if which == 0 else Vq2)[hc]
                            self.wt(v, "sE", Ssig[hc])
                            self.wt(v, "tE", T2[hc])
                            ins = v.scalar_tensor_tensor(
                                hid_sb[:, hc * TT:(hc + 1) * TT],
                                self.pD[:] if hc % 2 == 0 else self.pB[:],
                                self.chalf[:, 0:1],
                                sig[:, (hc % 2) * TT:(hc % 2 + 1) * TT],
                                ALU.mult, ALU.mult)
                            assert self.tok(ins, "vE") == Vh[hc]
                else:
                    @blk.sync
                    def _(e):
                        for hc in range(HC):
                            if hc >= 2:
                                self.wt(e, "tE", T2[hc - 2])
                            self.dtok(e.dma_start(out=qb1[hc % 2][:],
                                                  in_=self.qw1_sp[hc]), "Q%d" % (hc % 2))
                            self.dtok(e.dma_start(out=qb2[hc % 2][:],
                                                  in_=self.qw2_sp[hc]), "Q%d" % (hc % 2))

                    @blk.vector
                    def _(v):
                        for hc in range(HC):
                            self.wt(v, "sE", Ssig[hc])
                            self.wt(v, "tE", T2[hc])
                            ins = v.scalar_tensor_tensor(
                                hid_sb[:, hc * TT:(hc + 1) * TT],
                                self.pD[:] if hc % 2 == 0 else self.pB[:],
                                self.chalf[:, 0:1],
                                sig[:, (hc % 2) * TT:(hc % 2 + 1) * TT],
                                ALU.mult, ALU.mult)
                            assert self.tok(ins, "vE") == Vh[hc]

                    @blk.scalar
                    def _(s):
                        for hc in range(HC):
                            self.wt(s, "tE", T1[hc])
                            if hc >= 2:
                                self.wt(s, "vE", Vh[hc - 2])
                            ins = s.activation(sig[:, (hc % 2) * TT:(hc % 2 + 1) * TT],
                                               self.pA[:] if hc % 2 == 0 else self.pC[:],
                                               AF.Sigmoid, scale=self.chalf[:, 0:1])
                            assert self.tok(ins, "sE") == Ssig[hc]

                @blk.tensor
                def _(t):
                    for hc in range(HC):
                        if first:
                            self.wt(t, "vE", Vq2[hc])
                        else:
                            self.wt(t, "Q%d" % (hc % 2),
                                    (q0_0 if hc % 2 == 0 else q1_0) + 32 * (hc // 2 + 1))
                        b1 = self.pA if hc % 2 == 0 else self.pC
                        b2 = self.pD if hc % 2 == 0 else self.pB
                        if hc >= 2:
                            self.wt(t, "sE", Ssig[hc - 2])
                            self.wt(t, "vE", Vh[hc - 2])
                        else:
                            self.use_bank(t, b1)
                            self.use_bank(t, b2)
                        for ci in range(CI):
                            ins = t.matmul(b1[:], qb1[hc % 2][:, ci * 128:(ci + 1) * 128],
                                           y_sb[:, ci * HALF + tt * TT:
                                                ci * HALF + tt * TT + TT],
                                           start=(ci == 0), stop=(ci == CI - 1))
                        assert self.tok(ins, "tE") == T1[hc]
                        for ci in range(CI):
                            ins = t.matmul(b2[:], qb2[hc % 2][:, ci * 128:(ci + 1) * 128],
                                           y_sb[:, ci * HALF + tt * TT:
                                                ci * HALF + tt * TT + TT],
                                           start=(ci == 0), stop=(ci == CI - 1))
                        assert self.tok(ins, "tE") == T2[hc]

            for b in [self.pA, self.pC]:
                self.rel_bank(b, "sE", C["sE"])
            for b in [self.pB, self.pD]:
                self.rel_bank(b, "vE", C["vE"])
            self.barrier()

    def ffn3_q(self, q, hid_sb, f_sb, first):
        nc, C = self.nc, self.C
        NP = 4
        with ExitStack() as S:
            stg0 = self.usb(S, "s3g0", [128, 1408], F32, side="right")
            stg1 = self.usb(S, "s3g1", [128, 1408], F32, side="right")
            qb0 = self.usb(S, "q3b0", [128, 1408], BF16, side="right")
            qb1_ = self.usb(S, "q3b1", [128, 1408], BF16, side="right")
            tsc = self.usb(S, "t3sc", [128, 4224], BF16, side="right")
            sqf = self.usb(S, "sqf", [128, 1024], BF16, side="right")
            stg = [stg0, stg1]
            qb = [qb0, qb1_]

            t0, v0 = C["tE"], C["vE"]
            s0_3 = C["sE"]
            sp3 = C["Sp"]
            fa30, fa31 = C["Fa0"], C["Fa1"]
            q30, q31 = C["Q0"], C["Q1"]
            Tg = {}
            n = 0
            for oc in range(CI):
                for pc in range(NP):
                    n += 1
                    Tg[(oc, pc)] = t0 + n
                n += 1
                Tg[("st", oc)] = t0 + n
            if first:
                Vq = {}
                Vf = {}
                n = 0
                for oc in range(CI):
                    for pc in range(NP):
                        n += 1
                        Vq[(oc, pc)] = v0 + n
                    n += 2
                    Vf[oc] = v0 + n
            else:
                Vf = {oc: v0 + 2 * (oc + 1) for oc in range(CI)}
            del v0

            with self.block() as blk:
                if first:
                    Sa3 = {i: s0_3 + 2 * i + 1 for i in range(CI * NP)}
                    Sb3 = {i: s0_3 + 2 * i + 2 for i in range(CI * NP)}

                    @blk.gpsimd
                    def _(g):
                        for oc in range(CI):
                            for pc in range(NP):
                                i = oc * NP + pc
                                if i >= 2:
                                    self.wt(g, "sE", Sb3[i - 2])
                                self.dtok(g.dma_start(
                                    out=stg[i % 2][:].rearrange("p (c o) -> p c o", o=128),
                                    in_=self.w3s[oc][:, pc * 11:(pc + 1) * 11, :]),
                                    "Fa%d" % (i % 2))

                    @blk.scalar
                    def _(s):
                        for oc in range(CI):
                            for pc in range(NP):
                                i = oc * NP + pc
                                self.wt(s, "Fa%d" % (i % 2),
                                        (fa30 if i % 2 == 0 else fa31) + 16 * (i // 2 + 1))
                                if i >= 2:
                                    self.wt(s, "tE", Tg[((i - 2) // NP, (i - 2) % NP)])
                                    self.wt(s, "Sp", sp3 + 16 * (i - 1))
                                ins = s.activation(qb[i % 2][:], stg[i % 2][:], AF.Sign,
                                                   scale=self.pcol[:, 13:14],
                                                   bias=self.chalf[:, 0:1])
                                assert self.tok(ins, "sE") == Sa3[i]
                                if i >= 2:
                                    self.wt(s, "vE", Vq[((i - 2) // NP, (i - 2) % NP)])
                                ins = s.activation(tsc[:, (i % 2) * 1408:(i % 2 + 1) * 1408],
                                                   stg[i % 2][:], AF.Sign,
                                                   scale=self.pcol[:, 13:14],
                                                   bias=self.chalf[:, 1:2])
                                assert self.tok(ins, "sE") == Sb3[i]

                    @blk.sync
                    def _(e):
                        for oc in range(CI):
                            for pc in range(NP):
                                i = oc * NP + pc
                                self.wt(e, "vE", Vq[(oc, pc)])
                                self.dtok(e.dma_start(
                                    out=self.qw3_sp[oc][:, pc * 1408:(pc + 1) * 1408],
                                    in_=qb[i % 2][:]), "Sp")

                    @blk.vector
                    def _(v):
                        for oc in range(CI):
                            for pc in range(NP):
                                i = oc * NP + pc
                                self.wt(v, "sE", Sb3[i])
                                ins = v.tensor_add(qb[i % 2][:], qb[i % 2][:],
                                                   tsc[:, (i % 2) * 1408:(i % 2 + 1) * 1408])
                                assert self.tok(ins, "vE") == Vq[(oc, pc)]
                            self.wt(v, "tE", Tg[(oc, NP - 1)])
                            ftile = f_sb[:, oc * TT:(oc + 1) * TT]
                            self.tok(v.tensor_copy(ftile,
                                     self.pE[:] if oc % 2 == 0 else self.pF[:]), "vE")
                            if oc >= 2:
                                self.wt(v, "tE", Tg[("st", oc - 2)])
                            ins = v.tensor_mul(sqf[:, (oc % 2) * TT:(oc % 2 + 1) * TT],
                                               ftile, ftile)
                            assert self.tok(ins, "vE") == Vf[oc]
                else:
                    @blk.sync
                    def _(e):
                        for oc in range(CI):
                            for pc in range(NP):
                                i = oc * NP + pc
                                if i >= 2:
                                    self.wt(e, "tE", Tg[((i - 2) // NP, (i - 2) % NP)])
                                self.dtok(e.dma_start(
                                    out=qb[i % 2][:],
                                    in_=self.qw3_sp[oc][:, pc * 1408:(pc + 1) * 1408]),
                                    "Q%d" % (i % 2))

                    @blk.vector
                    def _(v):
                        for oc in range(CI):
                            self.wt(v, "tE", Tg[(oc, NP - 1)])
                            ftile = f_sb[:, oc * TT:(oc + 1) * TT]
                            self.tok(v.tensor_copy(ftile,
                                     self.pE[:] if oc % 2 == 0 else self.pF[:]), "vE")
                            if oc >= 2:
                                self.wt(v, "tE", Tg[("st", oc - 2)])
                            ins = v.tensor_mul(sqf[:, (oc % 2) * TT:(oc % 2 + 1) * TT],
                                               ftile, ftile)
                            assert self.tok(ins, "vE") == Vf[oc]

                @blk.tensor
                def _(t):
                    for b in [self.pA, self.pB, self.pE, self.pF]:
                        self.use_bank(t, b)
                    for oc in range(CI):
                        fb = self.pE if oc % 2 == 0 else self.pF
                        if oc >= 2:
                            self.wt(t, "vE", Vf[oc - 2])
                        for pc in range(NP):
                            i = oc * NP + pc
                            if first:
                                self.wt(t, "vE", Vq[(oc, pc)])
                            else:
                                self.wt(t, "Q%d" % (i % 2),
                                        (q30 if i % 2 == 0 else q31) + 16 * (i // 2 + 1))
                            for j in range(11):
                                hc = pc * 11 + j
                                ins = t.matmul(fb[:],
                                               qb[i % 2][:, j * 128:(j + 1) * 128],
                                               hid_sb[:, hc * TT:(hc + 1) * TT],
                                               start=(hc == 0), stop=(hc == HC - 1))
                            assert self.tok(ins, "tE") == Tg[(oc, pc)]
                        self.wt(t, "vE", Vf[oc])
                        t.matmul(self.pA[0:1, :], self.ones_col[:],
                                 f_sb[:, oc * TT:(oc + 1) * TT],
                                 start=(oc == 0), stop=(oc == CI - 1))
                        ins = t.matmul(self.pB[0:1, :], self.ones_col[:],
                                       sqf[:, (oc % 2) * TT:(oc % 2 + 1) * TT],
                                       start=(oc == 0), stop=(oc == CI - 1))
                        assert self.tok(ins, "tE") == Tg[("st", oc)]

            self.f_stats_tok = Tg[("st", CI - 1)]
            for b in [self.pE, self.pF]:
                self.rel_bank(b, "vE", C["vE"])
            self.barrier()

    def final_q(self, q, lnattn, f_sb):
        nc, C = self.nc, self.C
        half, tt = q // 2, q % 2
        with ExitStack() as S:
            xstg0 = self.usb(S, "xf0", [128, TT], F32, side="right")
            xstg1 = self.usb(S, "xf1", [128, TT], F32, side="right")
            repf = self.usb(S, "repf", [128, 1024], F32, side="right")
            ya = self.usb(S, "ya", [128, TT], F32, side="right")
            yb = self.usb(S, "yb", [128, TT], F32, side="right")
            ost0 = self.usb(S, "ost0", [128, TT], F32, side="right")
            ost1 = self.usb(S, "ost1", [128, TT], F32, side="right")
            xstg = [xstg0, xstg1]
            ost = [ost0, ost1]
            h0 = half * HALF
            so0 = C["So"]
            lx0_0, lx1_0 = C["Lx0"], C["Lx1"]

            rm = self.rowf[0:1, 0:TT]
            rs = self.rowf[0:1, TT:2 * TT]
            ra = self.rowf[0:1, 2 * TT:3 * TT]

            with self.block() as blk:
                @blk.vector
                def _(v):
                    self.wt(v, "tE", self.f_stats_tok)
                    v.tensor_scalar_mul(rm, self.pA[0:1, :], INVD * 0.5)
                    v.tensor_scalar_mul(rs, self.pB[0:1, :], INVD * 0.25)
                    v.tensor_mul(ra, rm, rm)
                    self.tok(v.tensor_sub(rs, rs, ra), "vE")

                vvar = C["vE"]

                @blk.scalar
                def _(s):
                    self.wt(s, "vE", vvar)
                    s.activation(rs, rs, AF.Ln, bias=self.cst[0:1, 0:1])
                    self.tok(s.activation(rs, rs, AF.Exp, scale=self.cst[0:1, 3:4]),
                             "sE")

                srow = C["sE"]

                @blk.vector
                def _(v):
                    self.wt(v, "sE", srow)
                    v.tensor_mul(ra, rm, rs)
                    v.tensor_scalar_mul(ra, ra, -1.0)
                    self.tok(v.tensor_scalar_mul(rs, rs, 0.5), "vE")

                vu3 = C["vE"]

                @blk.tensor
                def _(t):
                    self.wt(t, "vE", vu3)
                    self.use_bank(t, self.pC)
                    self.use_bank(t, self.pD)
                    t.matmul(self.pC[:], self.ones_row_f[:], rs, start=True, stop=True)
                    self.tok(t.matmul(self.pD[:], self.ones_row_f[:], ra,
                                      start=True, stop=True), "tE")

                trep = C["tE"]
                v0 = C["vE"]
                Vout = {ci: v0 + ci + 1 for ci in range(CI)}

                @blk.sync
                def _(e):
                    for ci in range(CI):
                        if ci >= 2:
                            self.wt(e, "vE", Vout[ci - 2])
                        self.dtok(e.dma_start(
                            out=xstg[ci % 2][:],
                            in_=self.xoT_t[ci][:, h0 + tt * TT:h0 + tt * TT + TT]),
                            "Lx%d" % (ci % 2))

                @blk.vector
                def _(v):
                    self.wt(v, "tE", trep)
                    v.tensor_copy(repf[:, 0:512], self.pC[:])
                    v.tensor_copy(repf[:, 512:1024], self.pD[:])
                    for ci in range(CI):
                        self.wt(v, "Lx%d" % (ci % 2),
                                (lx0_0 if ci % 2 == 0 else lx1_0) + 16 * (ci // 2 + 1))
                        if ci >= 2:
                            self.wt(v, "So", so0 + 16 * (ci - 1))
                        la = lnattn[:, ci * HALF + tt * TT: ci * HALF + tt * TT + TT]
                        ftile = f_sb[:, ci * TT:(ci + 1) * TT]
                        v.tensor_add(ya[:], la, repf[:, 512:1024])
                        v.tensor_mul(yb[:], ftile, repf[:, 0:512])
                        v.tensor_add(ya[:], ya[:], yb[:])
                        ins = v.tensor_add(ost[ci % 2][:], ya[:], xstg[ci % 2][:])
                        assert self.tok(ins, "vE") == Vout[ci]

                @blk.scalar
                def _(s):
                    for ci in range(CI):
                        self.wt(s, "vE", Vout[ci])
                        self.dtok(s.dma_start(
                            out=self.outT.rearrange("(c p) t -> c p t", p=128)[ci][
                                :, h0 + tt * TT:h0 + tt * TT + TT],
                            in_=ost[ci % 2][:]), "So")

            self.rel_bank(self.pA, "vE", C["vE"])
            self.rel_bank(self.pB, "vE", C["vE"])
            self.rel_bank(self.pC, "vE", C["vE"])
            self.rel_bank(self.pD, "vE", C["vE"])
            self.barrier()

    def build(self):
        nc = self.nc
        self._build_body()
        rem = nc.sbuf_bytes_remaining
        self.es.close()
        return nc, rem

    def _build_body(self):
        nc = self.nc
        self.consts_phase()

        with ExitStack() as SGP:
            self.gamma_start(SGP)
            for half in range(2):
                with ExitStack() as SL:
                    SWS = ExitStack()
                    ws = WStream(self, SWS, [(self.wk, 8), (self.wv, 8)])
                    SRx = ExitStack()
                    xn = self.usb(SRx, "xn", [128, CI * HALF], BF16, side="right")
                    self.x_phase(self.xpT_t, xn, half, ws=ws)
                    k_sb = self.usb(SL, "k_sb", [128, H * HALF], BF16, side="left")
                    v_sb = self.usb(SL, "v_sb", [128, H * HALF], BF16, side="left")
                    self.proj_phase(ws, xn, k_sb, "fk")
                    self.proj_phase(ws, xn, v_sb, "plain")
                    SRx.close()
                    SWS.close()
                    self.attn_half(half, None, k_sb, v_sb, None, full=False)
            self.gamma_tail()
        self.dbg_dump("gacc", self.gacc[:])
        self.dbg_dump("pcol", self.pcol[:])
        self.dbg_dump("rowf", self.rowf[:])

        first_ffn = True
        for half in range(2):
            SRL = ExitStack()
            SWS = ExitStack()
            ws = WStream(self, SWS,
                         [(self.wq, 8), (self.wk, 8), (self.wv, 8), (self.wo, 8)])
            SRo = ExitStack()
            with ExitStack() as SL3:
                SRx = ExitStack()
                xn = self.usb(SRx, "xn", [128, CI * HALF], BF16, side="right")
                self.x_phase(self.xoT_t, xn, half, ws=ws)
                self.dbg_dump("xn%d" % half, xn[:])
                q_sb = self.usb(SL3, "q_sb", [128, H * HALF], BF16, side="left")
                k_sb = self.usb(SL3, "k_sb", [128, H * HALF], BF16, side="left")
                v_sb = self.usb(SL3, "v_sb", [128, H * HALF], BF16, side="left")
                self.proj_phase(ws, xn, q_sb, "fq")
                self.proj_phase(ws, xn, k_sb, "fk")
                self.proj_phase(ws, xn, v_sb, "plain")
                SRx.close()
                o_sb = self.usb(SRo, "o_sb", [128, H * HALF], BF16, side="right")
                self.dbg_dump("qp%d" % half, q_sb[:])
                self.dbg_dump("kp%d" % half, k_sb[:])
                self.attn_half(half, q_sb, k_sb, v_sb, o_sb, full=True)
                self.dbg_dump("on%d" % half, o_sb[:])
            lnattn = self.usb(SRL, "lnattn", [128, CI * HALF], BF16, side="left")
            self.wo_phase(ws, o_sb, lnattn)
            SRo.close()
            SWS.close()
            self.dbg_dump("lnattn%d" % half, lnattn[:])
            with ExitStack() as SF:
                y_sb = self.usb(SF, "y_sb", [128, CI * HALF], BF16, side="left")
                hid_sb = self.usb(SF, "hid_sb", [128, HC * TT], BF16, side="left")
                f_sb = self.usb(SF, "f_sb", [128, CI * TT], BF16, side="left")
                self.y_phase(half, lnattn, y_sb)
                self.dbg_dump("yn%d" % half, y_sb[:])
                for tt in range(2):
                    q = half * 2 + tt
                    self.ffn12_q(q, y_sb, hid_sb, first_ffn)
                    self.dbg_dump("hid%d" % q, hid_sb[:])
                    self.ffn3_q(q, hid_sb, f_sb, first_ffn)
                    self.dbg_dump("f%d" % q, f_sb[:])
                    self.final_q(q, lnattn, f_sb)
                    first_ffn = False
            SRL.close()

        with self.block() as blk:
            @blk.scalar
            def _(s):
                self.wt(s, "So", self.C["So"])


def build_nc(dbg=()):
    kb = KB(dbg)
    nc, rem = kb.build()
    return nc


def _stage_weight_qk(w):
    wT = np.ascontiguousarray(np.asarray(w, np.float32).T)
    return np.ascontiguousarray(wT.reshape(CI, 128, CI, 128).transpose(2, 1, 0, 3))


def _stage_w12(w):
    wT = np.ascontiguousarray(np.asarray(w, np.float32).T)
    return np.ascontiguousarray(wT.reshape(CI, 128, HC, 128).transpose(2, 1, 0, 3))


def _stage_w3(w):
    wT = np.ascontiguousarray(np.asarray(w, np.float32).T)
    return np.ascontiguousarray(wT.reshape(HC, 128, CI, 128).transpose(2, 1, 0, 3))


def kernel(**inputs):
    x = np.asarray(inputs["L0_attn_x"], np.float32)
    B, L, _ = x.shape
    n_cores = 8

    wq = _stage_weight_qk(inputs["L0_attn_wq"])
    wk = _stage_weight_qk(inputs["L0_attn_wk"])
    wv = _stage_weight_qk(inputs["L0_attn_wv"])
    wo = _stage_weight_qk(inputs["L0_attn_wo"])
    w1s = _stage_w12(inputs["L0_ffn_w1"])
    w2s = _stage_w12(inputs["L0_ffn_w2"])
    w3s = _stage_w3(inputs["L0_ffn_w3"])
    fmw = np.ascontiguousarray(np.stack([
        np.asarray(inputs["L0_attn_fq1_w"], np.float32).T,
        np.asarray(inputs["L0_attn_fq2_w"], np.float32).T,
        np.asarray(inputs["L0_attn_fk1_w"], np.float32).T,
        np.asarray(inputs["L0_attn_fk2_w"], np.float32).T]))
    fmb = np.ascontiguousarray(np.stack([
        np.asarray(inputs["L0_attn_fq1_b"], np.float32),
        np.asarray(inputs["L0_attn_fq2_b"], np.float32),
        np.asarray(inputs["L0_attn_fk1_b"], np.float32),
        np.asarray(inputs["L0_attn_fk2_b"], np.float32),
        np.asarray(inputs["L0_attn_norm_w"], np.float32)]))
    cmask4 = np.ascontiguousarray(
        np.tile(np.triu(np.ones((128, 128), np.float32)), (1, 4)))
    cident = np.eye(128, dtype=np.float32)
    zeros_x = np.zeros((D, TOK), np.float32)

    shared = dict(wq=wq, wk=wk, wv=wv, wo=wo, w1s=w1s, w2s=w2s, w3s=w3s,
                  fmw=fmw, fmb=fmb, cmask4=cmask4, cident=cident)
    in_maps = []
    for c in range(n_cores):
        b, half = c // 2, c % 2
        xo = np.ascontiguousarray(x[b, half * TOK:(half + 1) * TOK, :].T)
        xp = np.ascontiguousarray(x[b, 0:TOK, :].T) if half == 1 else zeros_x
        m = dict(shared)
        m["xoT"] = xo
        m["xpT"] = xp
        in_maps.append(m)

    nc = build_nc()
    res = run_bass_kernel_spmd(nc, in_maps, core_ids=list(range(n_cores)))
    out = np.empty((B, L, D), np.float32)
    for c in range(n_cores):
        b, half = c // 2, c % 2
        out[b, half * TOK:(half + 1) * TOK, :] = res.results[c]["outT"].T
    return out


# version 13 (final)
# speedup vs baseline: 1.1856x; 1.0250x over previous
import numpy as np
import ml_dtypes
from contextlib import ExitStack, contextmanager

import concourse.bass as bass
import concourse.mybir as mybir
from concourse.bass_utils import run_bass_kernel_spmd

F32 = mybir.dt.float32
BF16 = mybir.dt.bfloat16
AF = mybir.ActivationFunctionType
ALU = mybir.AluOpType
AX = mybir.AxisListType.X

D = 2048
CI = 16
H = 16
CH = 128
TOK = 2048
HALF = 1024
TT = 512
NCH = 8
FH = 5632
HC = 44
INVD = 1.0 / 2048.0
INVDK = 1.0 / 128.0
EPS_LN = 1e-5
SCALE = 128.0 ** (-0.5)
NW_ELEMS = float(FH * D)
GP = 132


class WStream:

    def __init__(self, kb, S, segments):
        self.kb = kb
        self.wb = [kb.usb(S, "wsb%d" % i, [128, 4096], BF16, side="right")
                   for i in range(2)]
        self.segs = []
        for wst, npairs in segments:
            for j in range(npairs):
                self.segs.append((wst, j))
        self.cons = {}
        self.val = {}
        self.issued = 0
        self.cursor = 0

    def claim(self, npairs):
        base = self.cursor
        self.cursor += npairs
        return base

    def pump(self, g):
        kb = self.kb
        while self.issued < len(self.segs):
            p = self.issued
            gate = self.cons.get(p - 2)
            if p >= 2 and gate is None:
                return
            if gate is not None:
                kb.wt(g, *gate)
            wst, lp = self.segs[p]
            self.val[p] = kb.dtok(g.dma_start(
                out=self.wb[p % 2][:].rearrange("p (o c k) -> p o c k",
                                                o=2, k=128),
                in_=wst[2 * lp:2 * lp + 2].rearrange("o p c k -> p o c k")),
                "Ws%d" % (p % 2))
            self.issued += 1


class _GPump:

    def __init__(self, kb, S):
        self.kb = kb
        self.ring = {
            "s": [kb.usb(S, "gsa%d" % i, [128, 2048], F32, side="left")
                  for i in range(2)],
            "g": [kb.usb(S, "gsb%d" % i, [128, 2048], F32, side="left")
                  for i in range(2)],
        }
        self.junk = kb.usb(S, "gjunk", [128, 2048], BF16, side="left")
        p12 = [kb.g1[i] for i in range(44)] + [kb.g2[i] for i in range(44)]
        p3 = [kb.g3[i] for i in range(44)]
        ALL_SCALAR = True
        if ALL_SCALAR:
            self.pieces = {"s": p12 + p3, "g": []}
            self.cols = {"s": 0, "g": 88}
        else:
            self.pieces = {"s": p12, "g": p3}
            self.cols = {"s": 0, "g": 88}
        self.sem = {"s": ["GA0", "GA1"], "g": ["GB0", "GB1"]}
        self.ni = {"s": 0, "g": 0}
        self.nr = {"s": 0, "g": 0}

    def _issue(self, eng, kind):
        kb = self.kb
        i = self.ni[kind]
        if i >= len(self.pieces[kind]):
            return
        kb.dtok(eng.dma_start(out=self.ring[kind][i % 2][:],
                              in_=self.pieces[kind][i]),
                self.sem[kind][i % 2])
        self.ni[kind] += 1

    def prime(self):
        kb = self.kb
        with kb.block() as blk:
            @blk.scalar
            def _(s):
                self._issue(s, "s")
                self._issue(s, "s")

            @blk.gpsimd
            def _(g):
                self._issue(g, "g")
                self._issue(g, "g")

    def red(self, eng, kind, k=1):
        kb = self.kb
        for _ in range(k):
            i = self.nr[kind]
            if i >= len(self.pieces[kind]):
                return
            kb.wt(eng, self.sem[kind][i % 2], 16 * (i // 2 + 1))
            col = self.cols[kind] + i
            if kind == "s":
                eng.activation(self.junk[:], self.ring[kind][i % 2][:], AF.Abs,
                               accum_out=kb.gacc[:, col:col + 1])
            else:
                eng.tensor_scalar(self.junk[:], self.ring[kind][i % 2][:],
                                  0.0, 0.0, ALU.abs_max, ALU.add,
                                  accum_out=kb.gacc[:, col:col + 1])
            self.nr[kind] += 1
            self._issue(eng, kind)

    @property
    def done(self):
        return (self.nr["s"] + self.nr["g"] == GP)


class KB:
    def __init__(self, dbg=()):
        nc = bass.Bass("TRN2", target_bir_lowering=False, debug=False,
                       detect_race_conditions=False)
        self.nc = nc
        self.es = ExitStack()
        ap = lambda *a, **k: nc.dram_tensor(*a, **k).ap()
        self.xoT = ap("xoT", [D, TOK], F32, kind="ExternalInput")
        self.xpT = ap("xpT", [D, TOK], F32, kind="ExternalInput")
        self.wq = ap("wq", [CI, 128, CI, 128], F32, kind="ExternalInput")
        self.wk = ap("wk", [CI, 128, CI, 128], F32, kind="ExternalInput")
        self.wv = ap("wv", [CI, 128, CI, 128], F32, kind="ExternalInput")
        self.wo = ap("wo", [CI, 128, CI, 128], F32, kind="ExternalInput")
        self.w1s = ap("w1s", [HC, 128, CI, 128], F32, kind="ExternalInput")
        self.w2s = ap("w2s", [HC, 128, CI, 128], F32, kind="ExternalInput")
        self.w3s = ap("w3s", [CI, 128, HC, 128], F32, kind="ExternalInput")
        self.fmw = ap("fmw", [4, 128, 128], F32, kind="ExternalInput")
        self.fmb = ap("fmb", [5, 128], F32, kind="ExternalInput")
        self.cmask4 = ap("cmask4", [128, 512], F32, kind="ExternalInput")
        self.cident = ap("cident", [128, 128], F32, kind="ExternalInput")
        self.outT = ap("outT", [D, TOK], F32, kind="ExternalOutput")
        self.qw1_sp = ap("qw1_sp", [HC, 128, 2048], BF16, kind="Internal")
        self.qw2_sp = ap("qw2_sp", [HC, 128, 2048], BF16, kind="Internal")
        self.qw3_sp = ap("qw3_sp", [CI, 128, HC * 128], BF16, kind="Internal")
        self.dbg_outs = {}
        for i, (tag, shape) in enumerate(dbg):
            self.dbg_outs[tag] = ap(f"dbg{i}", list(shape), F32, kind="ExternalOutput")

        self.xoT_t = self.xoT.rearrange("(c p) t -> c p t", p=128)
        self.xpT_t = self.xpT.rearrange("(c p) t -> c p t", p=128)
        self.g1 = self.w1s.rearrange("n p a b -> n p (a b)")
        self.g2 = self.w2s.rearrange("n p a b -> n p (a b)")
        self.g3 = self.w3s.rearrange("a p c b -> (a p c b)") \
                          .rearrange("(n p c) -> n p c", p=128, c=2048)

        sbl = lambda n, sh, dt: self.es.enter_context(nc.sbuf_tensor(n, sh, dt, side="left"))
        self.ones_col = sbl("ones_col", [128, 1], BF16)
        self.ones_row = sbl("ones_row", [1, 128], BF16)
        self.ones_row_f = sbl("ones_row_f", [1, 128], F32)
        self.ones_col_f = sbl("ones_col_f", [128, 1], F32)
        self.ident = sbl("ident", [128, 128], BF16)
        self.ident_f = sbl("ident_f", [128, 128], F32)
        self.mask4 = sbl("mask4", [128, 512], BF16)
        self.fqk = sbl("fqk", [128, 4 * 128], BF16)
        self.pcol = sbl("pcol", [128, 16], F32)
        self.chalf = sbl("chalf", [128, 6], F32)
        self.rowf = sbl("rowf", [1, 5 * TT], F32)
        self.rowb = sbl("rowb", [1, 4 * TT], BF16)
        self.cst = sbl("cst", [1, 8], F32)
        self.S32 = sbl("S32", [128, H * 128], F32)
        self.Sbf = sbl("Sbf", [128, H * 128], BF16)
        self.gacc = sbl("gacc", [128, 144], F32)

        pst = lambda n, sh, dt: self.es.enter_context(nc.psum_tensor(n, sh, dt))
        self.pA = pst("pA", [128, 512], F32)
        self.pB = pst("pB", [128, 512], F32)
        self.pC = pst("pC", [128, 512], F32)
        self.pD = pst("pD", [128, 512], F32)
        self.pE = pst("pE", [128, 512], F32)
        self.pF = pst("pF", [128, 512], F32)
        self.pT0 = pst("pT0", [128, 1024], BF16)
        self.pT1 = pst("pT1", [128, 1024], BF16)

        self.SEM = {}
        self.C = {}
        self.DMA_SEMS = ["Lg", "Ws0", "Ws1", "Fa0", "Fa1", "Q0", "Q1",
                         "Lx0", "Lx1", "GA0", "GA1", "GB0", "GB1",
                         "Sp", "So", "Db"]
        for name in ["bar", "tE", "vE", "sE", "gR"] + self.DMA_SEMS:
            self.SEM[name] = self.es.enter_context(nc.semaphore(name))
            self.C[name] = 0
        self.bank_tok = {}
        self.bar_n = 0
        self.gp = None

    @contextmanager
    def block(self):
        with self.nc.Block() as b:
            yield b

    def usb(self, S, name, shape, dt, side="right"):
        self._uid = getattr(self, "_uid", 0) + 1
        return S.enter_context(self.nc.sbuf_tensor(f"{name}_{self._uid}", shape, dt, side=side))

    def tok(self, ins, sem, n=1):
        self.C[sem] += n
        ins.then_inc(self.SEM[sem], n)
        return self.C[sem]

    def dtok(self, ins, sem):
        return self.tok(ins, sem, 16)

    def wt(self, eng, sem, val):
        if val > 0:
            eng.wait_ge(self.SEM[sem], val)

    def use_bank(self, eng, bank):
        t = self.bank_tok.get(id(bank))
        if t is not None:
            self.wt(eng, t[0], t[1])

    def rel_bank(self, bank, sem, val):
        self.bank_tok[id(bank)] = (sem, val)

    def barrier(self):
        self.bar_n += 1
        snap = {s: self.C[s] for s in self.DMA_SEMS}
        with self.block() as blk:
            @blk.sync
            def _(e):
                for s, v in snap.items():
                    self.wt(e, s, v)
        self.bank_tok = {}

    def dbg_dump(self, tag, src_ap):
        if tag not in self.dbg_outs:
            return
        with self.block() as blk:
            @blk.gpsimd
            def _(e):
                self.dtok(e.dma_start(out=self.dbg_outs[tag], in_=src_ap), "Db")
                self.wt(e, "Db", self.C["Db"])
        self.barrier()

    def consts_phase(self):
        nc, C = self.nc, self.C
        with ExitStack() as S:
            fst = self.usb(S, "fst", [128, 4 * 128], F32, side="right")
            fmb_s = self.usb(S, "fmb_s", [128, 128], F32, side="right")
            with self.block() as blk:
                @blk.gpsimd
                def _(g):
                    g.wait_ge(self.SEM["vE"], 1)
                    self.dtok(g.dma_start(out=fmb_s[0:5, :], in_=self.fmb), "Lg")
                    self.dtok(g.dma_start(out=self.ident[:], in_=self.cident), "Lg")
                    self.dtok(g.dma_start(out=self.mask4[:], in_=self.cmask4), "Lg")
                    self.dtok(g.dma_start(
                        out=fst[:].rearrange("p (n c) -> p n c", c=128),
                        in_=self.fmw.rearrange("n p c -> p n c")), "Lg")

                @blk.sync
                def _(e):
                    self.dtok(e.dma_start(out=self.ident_f[:], in_=self.cident), "Lx0")

                @blk.vector
                def _(v):
                    self.tok(v.memset(fmb_s[:], 0.0), "vE")
                    v.memset(self.ones_col[:], 1.0)
                    v.memset(self.ones_row[:], 1.0)
                    v.memset(self.ones_row_f[:], 1.0)
                    v.memset(self.ones_col_f[:], 1.0)
                    v.memset(self.cst[0:1, 1:2], INVD)
                    v.memset(self.cst[0:1, 2:3], INVDK)
                    v.memset(self.cst[0:1, 3:4], -0.5)
                    v.memset(self.chalf[:, 0:1], 0.5)
                    v.memset(self.chalf[:, 1:2], -0.5)
                    v.memset(self.chalf[:, 2:3], 100000.0)
                    v.memset(self.chalf[:, 3:4], 128.0)
                    v.memset(self.S32[:], 0.0)
                    v.memset(self.Sbf[:], 0.0)
                    self.wt(v, "Lg", C["Lg"])
                    for i in range(1, 4):
                        ins = v.tensor_scalar_mul(self.fqk[:, i * 128:(i + 1) * 128],
                                                  fst[:, i * 128:(i + 1) * 128], 1.0)
                    self.tok(ins, "vE")

                vready = C["vE"]

                @blk.tensor
                def _(t):
                    self.wt(t, "Lx0", C["Lx0"])
                    self.wt(t, "vE", vready)
                    self.tok(t.transpose(self.pA[:, 0:128], fmb_s[:], self.ident_f[:]), "tE")

                ttok = C["tE"]

                @blk.scalar
                def _(s):
                    self.wt(s, "vE", vready)
                    s.activation(self.chalf[:, 4:5], self.chalf[:, 3:4], AF.Ln)
                    ins = s.activation(self.chalf[:, 5:6], self.chalf[:, 4:5],
                                       AF.Exp, scale=self.chalf[:, 1:2])
                    self.tok(ins, "sE")

                sready = C["sE"]

                @blk.vector
                def _(v):
                    self.wt(v, "tE", ttok)
                    for i in range(1, 5):
                        v.tensor_scalar_mul(self.pcol[:, i:i + 1],
                                            self.pA[:, i:i + 1], 1.0)
                    self.wt(v, "sE", sready)
                    v.scalar_tensor_tensor(self.fqk[:, 0:128], fst[:, 0:128],
                                           self.chalf[:, 5:6], fst[:, 0:128],
                                           ALU.mult, ALU.bypass)
                    v.scalar_tensor_tensor(self.pcol[:, 0:1], self.pA[:, 0:1],
                                           self.chalf[:, 5:6], self.chalf[:, 5:6],
                                           ALU.mult, ALU.bypass)
                    ins = v.reciprocal(self.pcol[:, 15:16], self.chalf[:, 2:3])
                    self.tok(ins, "vE")
            self.barrier()

    def gamma_start(self, S):
        self.gp = _GPump(self, S)
        self.gp.prime()

    def gamma_tail(self):
        assert self.gp.done, (self.gp.ni, self.gp.nr)
        self.gp = None
        nc, C = self.nc, self.C
        if True:
            with self.block() as blk:
                vred, sred = C["vE"], C["sE"]

                @blk.vector
                def _(v):
                    for m in range(3):
                        ins = v.tensor_reduce(self.gacc[:, 140 + m:141 + m],
                                              self.gacc[:, 44 * m:44 * (m + 1)],
                                              AX, ALU.add)
                    self.tok(ins, "vE")

                vcols = C["vE"]
                t_base = C["tE"]

                @blk.tensor
                def _(t):
                    self.wt(t, "vE", vcols)
                    self.tok(t.matmul(self.pB[0:1, 0:3], self.ones_col_f[:],
                                      self.gacc[:, 140:143],
                                      start=True, stop=True), "tE")

                @blk.vector
                def _(v):
                    self.wt(v, "tE", t_base + 1)
                    v.tensor_copy(self.rowf[0:1, 3 * TT:4 * TT],
                                  self.pB[0:1, 0:512])
                    v.tensor_scalar(self.rowf[0:1, 2 * TT:3 * TT],
                                    self.rowf[0:1, 3 * TT:4 * TT],
                                    0.0, NW_ELEMS, ALU.mult, ALU.add)
                    v.reciprocal(self.rowf[0:1, 0:TT],
                                 self.rowf[0:1, 2 * TT:3 * TT])
                    v.tensor_scalar_mul(self.rowf[0:1, 0:TT],
                                        self.rowf[0:1, 0:TT], 0.5)
                    v.tensor_scalar(self.rowf[0:1, 2 * TT:3 * TT],
                                    self.rowf[0:1, 3 * TT:4 * TT],
                                    0.0, 1e8, ALU.mult, ALU.add)
                    v.reciprocal(self.rowf[0:1, TT:2 * TT],
                                 self.rowf[0:1, 2 * TT:3 * TT])
                    v.tensor_scalar_mul(self.rowf[0:1, TT:2 * TT],
                                        self.rowf[0:1, TT:2 * TT], 0.5)
                    v.tensor_tensor(self.rowf[0:1, 4 * TT:5 * TT],
                                    self.rowf[0:1, 3 * TT:4 * TT],
                                    self.rowf[0:1, 0:TT], ALU.mult)
                    v.tensor_tensor(self.rowf[0:1, 4 * TT:5 * TT],
                                    self.rowf[0:1, 4 * TT:5 * TT],
                                    self.rowf[0:1, TT:2 * TT], ALU.add)
                    v.tensor_scalar_mul(self.rowf[0:1, 3 * TT:4 * TT],
                                        self.rowf[0:1, 4 * TT:5 * TT], -1.0)
                    v.reciprocal(self.rowf[0:1, 2 * TT:3 * TT],
                                 self.rowf[0:1, 4 * TT:5 * TT])
                    self.tok(v.tensor_scalar_mul(self.rowf[0:1, 2 * TT:3 * TT],
                                                 self.rowf[0:1, 2 * TT:3 * TT],
                                                 0.5), "vE")

                vthr = C["vE"]

                @blk.tensor
                def _(t):
                    self.wt(t, "vE", vthr)
                    t.matmul(self.pC[:, 0:3], self.ones_row_f[:],
                             self.rowf[0:1, 4 * TT:4 * TT + 3],
                             start=True, stop=True)
                    t.matmul(self.pC[:, 3:6], self.ones_row_f[:],
                             self.rowf[0:1, 3 * TT:3 * TT + 3],
                             start=True, stop=True)
                    mm = t.matmul(self.pC[:, 6:9], self.ones_row_f[:],
                                  self.rowf[0:1, 2 * TT:2 * TT + 3],
                                  start=True, stop=True)
                    self.tok(mm, "tE")

                trep = C["tE"]

                @blk.vector
                def _(v):
                    self.wt(v, "tE", trep)
                    self.tok(v.tensor_copy(self.pcol[:, 5:15], self.pC[:, 0:10]), "vE")
            self.barrier()

    def ln_rows_apply(self, buf, banks, stats_tok, rep_bf, apply=True):
        nc, C = self.nc, self.C
        rm = self.rowf[0:1, 0:TT]
        rs = self.rowf[0:1, TT:2 * TT]
        ra = self.rowf[0:1, 2 * TT:3 * TT]
        bm = self.rowb[0:1, 0:TT]
        bs = self.rowb[0:1, TT:2 * TT]
        toks = []
        for tti in range(2):
            s1, s2 = banks[tti]
            with self.block() as blk:
                @blk.vector
                def _(v):
                    self.wt(v, "tE", stats_tok[tti])
                    v.tensor_scalar_mul(rm, s1[0:1, :], INVD)
                    v.tensor_scalar_mul(rs, s2[0:1, :], INVD)
                    v.tensor_mul(ra, rm, rm)
                    self.tok(v.tensor_sub(rs, rs, ra), "vE")

                vvar = C["vE"]

                @blk.scalar
                def _(s):
                    self.wt(s, "vE", vvar)
                    s.activation(rs, rs, AF.Ln, bias=self.pcol[0:1, 15:16])
                    s.activation(rs, rs, AF.Exp, scale=self.cst[0:1, 3:4])
                    s.copy(bm, rm)
                    self.tok(s.copy(bs, rs), "sE")
                    if self.gp:
                        self.gp.red(s, "s", 1)

                srows = C["sE"]

                @blk.tensor
                def _(t):
                    self.wt(t, "sE", srows)
                    self.use_bank(t, self.pC)
                    self.use_bank(t, self.pD)
                    t.matmul(self.pC[:], self.ones_row[:], bm, start=True, stop=True)
                    self.tok(t.matmul(self.pD[:], self.ones_row[:], bs,
                                      start=True, stop=True), "tE")

                trep = C["tE"]

                @blk.vector
                def _(v):
                    self.wt(v, "tE", trep)
                    v.tensor_copy(rep_bf[:, 0:512], self.pC[:])
                    ins = v.tensor_copy(rep_bf[:, 512:1024], self.pD[:])
                    if apply:
                        for ci in range(CI):
                            tile = buf[:, ci * HALF + tti * TT:
                                       ci * HALF + tti * TT + TT]
                            v.tensor_sub(tile, tile, rep_bf[:, 0:512])
                            ins = v.tensor_mul(tile, tile, rep_bf[:, 512:1024])
                    self.tok(ins, "vE")
            self.rel_bank(self.pC, "vE", C["vE"])
            self.rel_bank(self.pD, "vE", C["vE"])
            toks.append(C["vE"])
        return toks

    def x_phase(self, srcT_t, buf, half, ws=None):
        nc, C = self.nc, self.C
        with ExitStack() as S:
            sq16 = self.usb(S, "sq16", [128, CI * TT], BF16, side="right")
            rep = self.usb(S, "repx", [128, 1024], BF16, side="right")
            h0 = half * HALF
            with self.block() as blk:
                @blk.gpsimd
                def _(g):
                    for ci in range(CI):
                        self.dtok(g.dma_start(out=buf[:, ci * HALF:(ci + 1) * HALF],
                                              in_=srcT_t[ci][:, h0:h0 + HALF]), "Lg")
                    if ws is not None:
                        ws.pump(g)
                    if self.gp:
                        self.gp.red(g, "g", 2)

                lg_full = C["Lg"]
                t_base = C["tE"]
                sq_toks = {}

                @blk.vector
                def _(v):
                    self.wt(v, "Lg", lg_full)
                    for tti in range(2):
                        if tti == 1:
                            self.wt(v, "tE", t_base + 1)
                        for ci in range(CI):
                            tile = buf[:, ci * HALF + tti * TT:
                                       ci * HALF + tti * TT + TT]
                            ins = v.tensor_mul(sq16[:, ci * TT:(ci + 1) * TT],
                                               tile, tile)
                            sq_toks[(tti, ci)] = self.tok(ins, "vE")

                stats_tok = {}

                @blk.tensor
                def _(t):
                    bankmap = {0: (self.pA, self.pB), 1: (self.pE, self.pF)}
                    for tti in range(2):
                        s1, s2 = bankmap[tti]
                        self.use_bank(t, s1)
                        self.use_bank(t, s2)
                        for ci in range(CI):
                            self.wt(t, "vE", sq_toks[(tti, ci)])
                            tile = buf[:, ci * HALF + tti * TT:
                                       ci * HALF + tti * TT + TT]
                            t.matmul(s1[0:1, :], self.ones_col[:], tile,
                                     start=(ci == 0), stop=(ci == CI - 1))
                            ins = t.matmul(s2[0:1, :], self.ones_col[:],
                                           sq16[:, ci * TT:(ci + 1) * TT],
                                           start=(ci == 0), stop=(ci == CI - 1))
                        stats_tok[tti] = self.tok(ins, "tE")
                assert stats_tok[0] == t_base + 1

            banks = {0: (self.pA, self.pB), 1: (self.pE, self.pF)}
            vt = self.ln_rows_apply(buf, banks, stats_tok, rep)
            self.rel_bank(self.pA, "vE", vt[0])
            self.rel_bank(self.pB, "vE", vt[0])
            self.rel_bank(self.pE, "vE", vt[1])
            self.rel_bank(self.pF, "vE", vt[1])
            self.barrier()

    def proj_phase(self, ws, xn, dest, kind):
        nc, C = self.nc, self.C
        with ExitStack() as S:
            wb = ws.wb
            fmap = kind in ("fq", "fk")
            if fmap:
                raw = self.usb(S, "rawfm", [128, 2 * TT], BF16, side="right")
                t1s = self.usb(S, "t1fm", [128, 2 * TT], BF16, side="right")
                fw1 = self.fqk[:, 0:128] if kind == "fq" else self.fqk[:, 256:384]
                fw2 = self.fqk[:, 128:256] if kind == "fq" else self.fqk[:, 384:512]
                b1 = self.pcol[:, 0:1] if kind == "fq" else self.pcol[:, 2:3]
                b2 = self.pcol[:, 1:2] if kind == "fq" else self.pcol[:, 3:4]

            t0, v0, s0 = C["tE"], C["vE"], C["sE"]
            base = ws.claim(CI // 2)
            tper = 2 if fmap else 1
            Tproj = {}
            Tf = {}
            n = 0
            for oc in range(CI):
                for tt in range(2):
                    n += 1
                    Tproj[(oc, tt)] = t0 + n
                    if fmap:
                        n += 1
                        Tf[(oc, tt)] = t0 + n
            Vr = {}
            Ve = {}
            n = 0
            for oc in range(CI):
                for tt in range(2):
                    if fmap:
                        n += 1
                        Vr[(oc, tt)] = v0 + n
                        n += 1
                        Ve[(oc, tt)] = v0 + n
                    else:
                        n += 1
                        Ve[(oc, tt)] = v0 + n
            Sf = {}
            if fmap:
                for i, key in enumerate([(oc, tt) for oc in range(CI) for tt in range(2)]):
                    Sf[key] = s0 + i + 1
            for lp in range(CI // 2):
                ws.cons[base + lp] = ("tE", Tproj[(2 * lp + 1, 1)])

            with self.block() as blk:
                @blk.gpsimd
                def _(g):
                    ws.pump(g)
                    if self.gp:
                        self.gp.red(g, "g", 16)

                @blk.tensor
                def _(t):
                    pp = [self.pA, self.pB]
                    pf1 = [self.pC, self.pE]
                    pf2 = [self.pD, self.pF]
                    for oc in range(CI):
                        gp_, osub = base + oc // 2, oc % 2
                        self.wt(t, "Ws%d" % (gp_ % 2), ws.val[gp_])
                        for tt in range(2):
                            par = (oc * 2 + tt) % 2
                            self.use_bank(t, pp[par])
                            for ci in range(CI):
                                ins = t.matmul(pp[par][:],
                                               wb[gp_ % 2][:, osub * 2048 + ci * 128:
                                                  osub * 2048 + (ci + 1) * 128],
                                               xn[:, ci * HALF + tt * TT:
                                                  ci * HALF + tt * TT + TT],
                                               start=(ci == 0), stop=(ci == CI - 1))
                            assert self.tok(ins, "tE") == Tproj[(oc, tt)]
                            self.rel_bank(pp[par], "vE",
                                          Vr[(oc, tt)] if fmap else Ve[(oc, tt)])
                            if fmap:
                                self.use_bank(t, pf1[par])
                                self.use_bank(t, pf2[par])
                                self.wt(t, "vE", Vr[(oc, tt)])
                                t.matmul(pf1[par][:], fw1,
                                         raw[:, par * TT:(par + 1) * TT],
                                         start=True, stop=True)
                                ins = t.matmul(pf2[par][:], fw2,
                                               raw[:, par * TT:(par + 1) * TT],
                                               start=True, stop=True)
                                assert self.tok(ins, "tE") == Tf[(oc, tt)]
                                self.rel_bank(pf1[par], "sE", Sf[(oc, tt)])
                                self.rel_bank(pf2[par], "vE", Ve[(oc, tt)])

                @blk.vector
                def _(v):
                    pp = [self.pA, self.pB]
                    pf2 = [self.pD, self.pF]
                    for oc in range(CI):
                        for tt in range(2):
                            par = (oc * 2 + tt) % 2
                            dst = dest[:, oc * HALF + tt * TT:
                                       oc * HALF + tt * TT + TT]
                            if fmap:
                                self.wt(v, "tE", Tproj[(oc, tt)])
                                assert self.tok(
                                    v.tensor_copy(raw[:, par * TT:(par + 1) * TT],
                                                  pp[par][:]), "vE") == Vr[(oc, tt)]
                                self.wt(v, "tE", Tf[(oc, tt)])
                                self.wt(v, "sE", Sf[(oc, tt)])
                                ins = v.scalar_tensor_tensor(
                                    dst, pf2[par][:], b2,
                                    t1s[:, par * TT:(par + 1) * TT],
                                    ALU.add, ALU.mult)
                                assert self.tok(ins, "vE") == Ve[(oc, tt)]
                            else:
                                self.wt(v, "tE", Tproj[(oc, tt)])
                                assert self.tok(v.tensor_copy(dst, pp[par][:]),
                                                "vE") == Ve[(oc, tt)]

                if not fmap and self.gp:
                    @blk.scalar
                    def _(s):
                        for _ in range(CI):
                            self.gp.red(s, "s", 2)

                if fmap:
                    @blk.scalar
                    def _(s):
                        pf1 = [self.pC, self.pE]
                        keysf = [(oc, tt) for oc in range(CI) for tt in range(2)]
                        for i, (oc, tt) in enumerate(keysf):
                            par = (oc * 2 + tt) % 2
                            self.wt(s, "tE", Tf[(oc, tt)])
                            if i >= 2:
                                self.wt(s, "vE", Ve[keysf[i - 2]])
                            ins = s.add(t1s[:, par * TT:(par + 1) * TT],
                                        pf1[par][:], b1)
                            assert self.tok(ins, "sE") == Sf[(oc, tt)]
                            if self.gp:
                                self.gp.red(s, "s", 1)
            self.barrier()

    def attn_half(self, half, q_sb, k_sb, v_sb, o_sb, full=True):
        nc, C = self.nc, self.C
        with ExitStack() as S:
            tm = self.usb(S, "tm", [128, 2048], BF16, side="right")
            sscr = self.usb(S, "sscr", [128, 1024], BF16, side="right")
            o2scr = self.usb(S, "o2scr", [128, 1024], BF16, side="right")

            t0, v0, s0 = C["tE"], C["vE"], C["sE"]
            tn, vn, sn = [t0], [v0], [s0]
            Ttp, Ts, To, Td = {}, {}, {}, {}
            Vtm, Vmask, Vo, Vs, Sc = {}, {}, {}, {}, {}
            Vo2, Tst, Sr, Trp, Vap = {}, {}, {}, {}, {}
            for c in range(NCH):
                for g in range(4):
                    tn[0] += 1; Ttp[(c, g)] = tn[0]
                    if full:
                        tn[0] += 1; Ts[(c, g)] = tn[0]
                        tn[0] += 1; To[(c, g)] = tn[0]
                    tn[0] += 1; Td[(c, g)] = tn[0]
                    vn[0] += 1; Vtm[(c, g)] = vn[0]
                    if full:
                        vn[0] += 1; Vmask[(c, g)] = vn[0]
                        vn[0] += 1; Vo[(c, g)] = vn[0]
                    vn[0] += 1; Vs[(c, g)] = vn[0]
                    sn[0] += 1; Sc[(c, g)] = sn[0]
                if full and c % 4 == 3:
                    tti = c // 4
                    for h in range(H):
                        vn[0] += 1; Vo2[(tti, h)] = vn[0]
                        tn[0] += 1; Tst[(tti, h)] = tn[0]
                        sn[0] += 1; Sr[(tti, h)] = sn[0]
                        tn[0] += 1; Trp[(tti, h)] = tn[0]
                        vn[0] += 1; Vap[(tti, h)] = vn[0]

            def prev2(c, g):
                i = c * 4 + g - 2
                return None if i < 0 else (i // 4, i % 4)

            def prevh(tti, h):
                return None if h < 2 else (tti, h - 2)

            qcol = lambda hd, c: hd * HALF + c * CH
            rfr = [self.rowf[0:1, 3 * TT:4 * TT], self.rowf[0:1, 4 * TT:5 * TT]]
            rbr = [self.rowb[0:1, 2 * TT:3 * TT], self.rowb[0:1, 3 * TT:4 * TT]]

            with self.block() as blk:
                @blk.tensor
                def _(t):
                    pT = [self.pT0, self.pT1]
                    pS = [self.pA, self.pB]
                    pO = [self.pC, self.pD]
                    pDl = [self.pE, self.pF]
                    for c in range(NCH):
                        for g in range(4):
                            par = g % 2
                            p2 = prev2(c, g)
                            if p2:
                                self.wt(t, "vE", Vtm[p2])
                            for j in range(4):
                                hd = 4 * g + j
                                t.transpose(pT[par][:, j * 128:(j + 1) * 128],
                                            v_sb[:, qcol(hd, c):qcol(hd, c) + CH],
                                            self.ident[:])
                            for j in range(4):
                                hd = 4 * g + j
                                ins = t.transpose(pT[par][:, 512 + j * 128:512 + (j + 1) * 128],
                                                  k_sb[:, qcol(hd, c):qcol(hd, c) + CH],
                                                  self.ident[:])
                            assert self.tok(ins, "tE") == Ttp[(c, g)]
                            if full:
                                if p2:
                                    self.wt(t, "vE", Vmask[p2])
                                if c == 4:
                                    self.wt(t, "sE", Sr[(0, 14 + par)])
                                for j in range(4):
                                    hd = 4 * g + j
                                    ins = t.matmul(pS[par][:, j * 128:(j + 1) * 128],
                                                   k_sb[:, qcol(hd, c):qcol(hd, c) + CH],
                                                   q_sb[:, qcol(hd, c):qcol(hd, c) + CH],
                                                   start=True, stop=True)
                                assert self.tok(ins, "tE") == Ts[(c, g)]
                                if p2:
                                    self.wt(t, "vE", Vo[p2])
                                if c == 4:
                                    self.wt(t, "vE", Vap[(0, 14 + par)])
                                if c > 0:
                                    self.wt(t, "sE", Sc[(c - 1, g)])
                                self.wt(t, "vE", Vmask[(c, g)])
                                self.wt(t, "vE", Vtm[(c, g)])
                                for j in range(4):
                                    hd = 4 * g + j
                                    t.matmul(pO[par][:, j * 128:(j + 1) * 128],
                                             self.Sbf[:, hd * 128:(hd + 1) * 128],
                                             q_sb[:, qcol(hd, c):qcol(hd, c) + CH],
                                             start=True, stop=False)
                                    ins = t.matmul(pO[par][:, j * 128:(j + 1) * 128],
                                                   tm[:, par * 1024 + j * 128:
                                                      par * 1024 + (j + 1) * 128],
                                                   sscr[:, par * 512 + j * 128:
                                                        par * 512 + (j + 1) * 128],
                                                   start=False, stop=True)
                                assert self.tok(ins, "tE") == To[(c, g)]
                            if p2:
                                self.wt(t, "vE", Vs[p2])
                            if not full:
                                self.wt(t, "vE", Vtm[(c, g)])
                            for j in range(4):
                                hd = 4 * g + j
                                ins = t.matmul(pDl[par][:, j * 128:(j + 1) * 128],
                                               tm[:, par * 1024 + 512 + j * 128:
                                                  par * 1024 + 512 + (j + 1) * 128],
                                               tm[:, par * 1024 + j * 128:
                                                  par * 1024 + (j + 1) * 128],
                                               start=True, stop=True)
                            assert self.tok(ins, "tE") == Td[(c, g)]
                        if full and c % 4 == 3:
                            tti = c // 4
                            pSt = [self.pA, self.pB]
                            pRp = [self.pC, self.pD]
                            for h in range(H):
                                par = h % 2
                                ph = prevh(tti, h)
                                self.wt(t, "vE", Vo2[(tti, h)])
                                if ph:
                                    self.wt(t, "sE", Sr[ph])
                                else:
                                    self.wt(t, "vE", Vmask[(c, 2 + par)])
                                ins = t.matmul(pSt[par][0:1, :], self.ones_col[:],
                                               o2scr[:, par * TT:(par + 1) * TT],
                                               start=True, stop=True)
                                assert self.tok(ins, "tE") == Tst[(tti, h)]
                                self.wt(t, "sE", Sr[(tti, h)])
                                if ph:
                                    self.wt(t, "vE", Vap[ph])
                                else:
                                    self.wt(t, "vE", Vo[(c, 2 + par)])
                                ins = t.matmul(pRp[par][:], self.ones_row[:],
                                               rbr[par], start=True, stop=True)
                                assert self.tok(ins, "tE") == Trp[(tti, h)]

                @blk.vector
                def _(v):
                    pT = [self.pT0, self.pT1]
                    pS = [self.pA, self.pB]
                    pO = [self.pC, self.pD]
                    pDl = [self.pE, self.pF]
                    for c in range(NCH):
                        for g in range(4):
                            par = g % 2
                            p2 = prev2(c, g)
                            self.wt(v, "tE", Ttp[(c, g)])
                            if p2:
                                self.wt(v, "tE", Td[p2])
                            ins = v.tensor_copy(tm[:, par * 1024:(par + 1) * 1024],
                                                pT[par][:])
                            assert self.tok(ins, "vE") == Vtm[(c, g)]
                            if full:
                                self.wt(v, "tE", Ts[(c, g)])
                                if p2:
                                    self.wt(v, "tE", To[p2])
                                ins = v.tensor_mul(sscr[:, par * 512:(par + 1) * 512],
                                                   pS[par][:], self.mask4[:])
                                assert self.tok(ins, "vE") == Vmask[(c, g)]
                                self.wt(v, "tE", To[(c, g)])
                                dst = o_sb[:].rearrange("p (h t) -> p h t", t=HALF)[
                                    :, 4 * g:4 * g + 4, c * CH:(c + 1) * CH]
                                src = pO[par][:].rearrange("p (h t) -> p h t", t=CH)
                                ins = v.tensor_copy(dst, src)
                                assert self.tok(ins, "vE") == Vo[(c, g)]
                            self.wt(v, "tE", Td[(c, g)])
                            if c > 0:
                                self.wt(v, "sE", Sc[(c - 1, g)])
                            sl = self.S32[:, g * 512:(g + 1) * 512]
                            ins = v.tensor_add(sl, sl, pDl[par][:])
                            assert self.tok(ins, "vE") == Vs[(c, g)]
                        if full and c % 4 == 3:
                            tti = c // 4
                            pRp = [self.pC, self.pD]
                            for h in range(H):
                                par = h % 2
                                ph = prevh(tti, h)
                                otile = o_sb[:, h * HALF + tti * TT:
                                             h * HALF + tti * TT + TT]
                                if ph:
                                    self.wt(v, "tE", Tst[ph])
                                ins = v.tensor_mul(o2scr[:, par * TT:(par + 1) * TT],
                                                   otile, otile)
                                assert self.tok(ins, "vE") == Vo2[(tti, h)]
                                self.wt(v, "tE", Trp[(tti, h)])
                                ins = v.scalar_tensor_tensor(
                                    otile, otile, self.pcol[:, 4:5], pRp[par][:],
                                    ALU.mult, ALU.mult)
                                assert self.tok(ins, "vE") == Vap[(tti, h)]

                @blk.scalar
                def _(s):
                    pSt = [self.pA, self.pB]
                    for c in range(NCH):
                        for g in range(4):
                            self.wt(s, "vE", Vs[(c, g)])
                            if full:
                                self.wt(s, "tE", To[(c, g)])
                            ins = s.copy(self.Sbf[:, g * 512:(g + 1) * 512],
                                         self.S32[:, g * 512:(g + 1) * 512])
                            assert self.tok(ins, "sE") == Sc[(c, g)]
                            if self.gp:
                                self.gp.red(s, "s", 1)
                        if full and c % 4 == 3:
                            tti = c // 4
                            for h in range(H):
                                par = h % 2
                                self.wt(s, "tE", Tst[(tti, h)])
                                s.activation(rfr[par], pSt[par][0:1, :], AF.Ln,
                                             bias=self.pcol[0:1, 15:16],
                                             scale=self.cst[0:1, 2:3])
                                s.activation(rfr[par], rfr[par], AF.Exp,
                                             scale=self.cst[0:1, 3:4])
                                ins = s.copy(rbr[par], rfr[par])
                                assert self.tok(ins, "sE") == Sr[(tti, h)]

            for b in [self.pA, self.pB]:
                self.rel_bank(b, "sE", C["sE"])
            for b in [self.pC, self.pD]:
                self.rel_bank(b, "vE", C["vE"])
            for b in [self.pE, self.pF]:
                self.rel_bank(b, "vE", C["vE"])
            self.barrier()

    def wo_phase(self, ws, o_sb, lnattn):
        nc, C = self.nc, self.C
        with ExitStack() as S:
            wb = ws.wb
            sq4 = self.usb(S, "sq4", [128, 4 * TT], BF16, side="right")
            rep = self.usb(S, "repw", [128, 1024], BF16, side="right")

            t0, v0 = C["tE"], C["vE"]
            base = ws.claim(CI // 2)
            Tproj = {}
            Tst = {}
            n = 0
            for oc in range(CI):
                for tt in range(2):
                    n += 1
                    Tproj[(oc, tt)] = t0 + n
                    n += 1
                    Tst[(oc, tt)] = t0 + n
            Vev = {}
            n = 0
            for oc in range(CI):
                for tt in range(2):
                    n += 1
                    Vev[(oc, tt)] = v0 + n
            for lp in range(CI // 2):
                ws.cons[base + lp] = ("tE", Tproj[(2 * lp + 1, 1)])

            with self.block() as blk:
                @blk.gpsimd
                def _(g):
                    ws.pump(g)

                @blk.tensor
                def _(t):
                    pp = [self.pA, self.pB]
                    stats = {0: (self.pC, self.pD), 1: (self.pE, self.pF)}
                    for b in [self.pC, self.pD, self.pE, self.pF]:
                        self.use_bank(t, b)
                    for oc in range(CI):
                        gp_, osub = base + oc // 2, oc % 2
                        self.wt(t, "Ws%d" % (gp_ % 2), ws.val[gp_])
                        for tt in range(2):
                            par = (oc * 2 + tt) % 2
                            self.use_bank(t, pp[par])
                            for ci in range(CI):
                                ins = t.matmul(pp[par][:],
                                               wb[gp_ % 2][:, osub * 2048 + ci * 128:
                                                  osub * 2048 + (ci + 1) * 128],
                                               o_sb[:, ci * HALF + tt * TT:
                                                    ci * HALF + tt * TT + TT],
                                               start=(ci == 0), stop=(ci == CI - 1))
                            assert self.tok(ins, "tE") == Tproj[(oc, tt)]
                            self.rel_bank(pp[par], "vE", Vev[(oc, tt)])
                            s1, s2 = stats[tt]
                            self.wt(t, "vE", Vev[(oc, tt)])
                            t.matmul(s1[0:1, :], self.ones_col[:],
                                     lnattn[:, oc * HALF + tt * TT:
                                            oc * HALF + tt * TT + TT],
                                     start=(oc == 0), stop=(oc == CI - 1))
                            ins = t.matmul(s2[0:1, :], self.ones_col[:],
                                           sq4[:, ((oc % 2) * 2 + tt) * TT:
                                               ((oc % 2) * 2 + tt) * TT + TT],
                                           start=(oc == 0), stop=(oc == CI - 1))
                            assert self.tok(ins, "tE") == Tst[(oc, tt)]

                @blk.vector
                def _(v):
                    pp = [self.pA, self.pB]
                    for oc in range(CI):
                        for tt in range(2):
                            par = (oc * 2 + tt) % 2
                            self.wt(v, "tE", Tproj[(oc, tt)])
                            if oc >= 2:
                                self.wt(v, "tE", Tst[(oc - 2, tt)])
                            dst = lnattn[:, oc * HALF + tt * TT:
                                         oc * HALF + tt * TT + TT]
                            v.tensor_copy(dst, pp[par][:])
                            ins = v.tensor_mul(sq4[:, ((oc % 2) * 2 + tt) * TT:
                                                   ((oc % 2) * 2 + tt) * TT + TT],
                                               dst, dst)
                            assert self.tok(ins, "vE") == Vev[(oc, tt)]

            stats_tok = {0: Tst[(CI - 1, 0)], 1: Tst[(CI - 1, 1)]}
            banks = {0: (self.pC, self.pD), 1: (self.pE, self.pF)}
            vt = self.ln_rows_apply(lnattn, banks, stats_tok, rep)
            for b in [self.pC, self.pD, self.pE, self.pF]:
                self.rel_bank(b, "vE", vt[1])
            self.barrier()

    def y_phase(self, half, lnattn, y_sb):
        nc, C = self.nc, self.C
        with ExitStack() as S:
            xstg0 = self.usb(S, "xstg0", [128, TT], F32, side="right")
            xstg1 = self.usb(S, "xstg1", [128, TT], F32, side="right")
            sq2 = self.usb(S, "sq2", [128, 2 * TT], BF16, side="right")
            rep = self.usb(S, "repy", [128, 1024], BF16, side="right")
            xstg = [xstg0, xstg1]
            h0 = half * HALF

            t0, v0 = C["tE"], C["vE"]
            lx0_0, lx1_0 = C["Lx0"], C["Lx1"]
            keys = [(tt, ci) for tt in range(2) for ci in range(CI)]
            Vy = {k: v0 + 2 * i + 1 for i, k in enumerate(keys)}
            Vsq = {k: v0 + 2 * i + 2 for i, k in enumerate(keys)}
            Tst = {k: t0 + i + 1 for i, k in enumerate(keys)}

            with self.block() as blk:
                @blk.sync
                def _(e):
                    for i, (tt, ci) in enumerate(keys):
                        if i >= 2:
                            self.wt(e, "vE", Vsq[keys[i - 2]])
                        self.dtok(e.dma_start(
                            out=xstg[i % 2][:],
                            in_=self.xoT_t[ci][:, h0 + tt * TT:h0 + tt * TT + TT]),
                            "Lx%d" % (i % 2))

                @blk.vector
                def _(v):
                    for i, (tt, ci) in enumerate(keys):
                        self.wt(v, "Lx%d" % (i % 2),
                                (lx0_0 if i % 2 == 0 else lx1_0) + 16 * (i // 2 + 1))
                        if i >= 2:
                            self.wt(v, "tE", Tst[keys[i - 2]])
                        ytile = y_sb[:, ci * HALF + tt * TT: ci * HALF + tt * TT + TT]
                        la = lnattn[:, ci * HALF + tt * TT: ci * HALF + tt * TT + TT]
                        ins = v.tensor_add(ytile, xstg[i % 2][:], la)
                        assert self.tok(ins, "vE") == Vy[(tt, ci)]
                        ins = v.tensor_mul(sq2[:, (i % 2) * TT:(i % 2 + 1) * TT],
                                           ytile, ytile)
                        assert self.tok(ins, "vE") == Vsq[(tt, ci)]

                @blk.tensor
                def _(t):
                    bankmap = {0: (self.pA, self.pB), 1: (self.pE, self.pF)}
                    for b in [self.pA, self.pB, self.pE, self.pF]:
                        self.use_bank(t, b)
                    for i, (tt, ci) in enumerate(keys):
                        s1, s2 = bankmap[tt]
                        self.wt(t, "vE", Vsq[(tt, ci)])
                        t.matmul(s1[0:1, :], self.ones_col[:],
                                 y_sb[:, ci * HALF + tt * TT: ci * HALF + tt * TT + TT],
                                 start=(ci == 0), stop=(ci == CI - 1))
                        ins = t.matmul(s2[0:1, :], self.ones_col[:],
                                       sq2[:, (i % 2) * TT:(i % 2 + 1) * TT],
                                       start=(ci == 0), stop=(ci == CI - 1))
                        assert self.tok(ins, "tE") == Tst[(tt, ci)]

            stats_tok = {0: Tst[(0, CI - 1)], 1: Tst[(1, CI - 1)]}
            banks = {0: (self.pA, self.pB), 1: (self.pE, self.pF)}
            vt = self.ln_rows_apply(y_sb, banks, stats_tok, rep)
            for b in [self.pA, self.pB, self.pE, self.pF]:
                self.rel_bank(b, "vE", vt[1])
            self.barrier()

    def ffn12_q(self, q, y_sb, hid_sb, first):
        nc, C = self.nc, self.C
        half, tt = q // 2, q % 2
        with ExitStack() as S:
            stg0 = self.usb(S, "stg0", [128, 2048], F32, side="right")
            stg1 = self.usb(S, "stg1", [128, 2048], F32, side="right")
            qb10 = self.usb(S, "qb10", [128, 2048], BF16, side="right")
            qb11 = self.usb(S, "qb11", [128, 2048], BF16, side="right")
            qb20 = self.usb(S, "qb20", [128, 2048], BF16, side="right")
            qb21 = self.usb(S, "qb21", [128, 2048], BF16, side="right")
            tsc = self.usb(S, "tsc", [128, 6144], BF16, side="right")
            sig = self.usb(S, "sig", [128, 1024], BF16, side="right")
            stg = [stg0, stg1]
            qb1 = [qb10, qb11]
            qb2 = [qb20, qb21]

            t0, v0, s0 = C["tE"], C["vE"], C["sE"]
            sp0 = C["Sp"]
            fa0_0, fa1_0 = C["Fa0"], C["Fa1"]
            q0_0, q1_0 = C["Q0"], C["Q1"]
            T1 = {hc: t0 + 2 * hc + 1 for hc in range(HC)}
            T2 = {hc: t0 + 2 * hc + 2 for hc in range(HC)}
            Ssig = {hc: s0 + hc + 1 for hc in range(HC)}
            if first:
                Vq1 = {hc: v0 + 3 * hc + 1 for hc in range(HC)}
                Vq2 = {hc: v0 + 3 * hc + 2 for hc in range(HC)}
                Vh = {hc: v0 + 3 * hc + 3 for hc in range(HC)}
            else:
                Vh = {hc: v0 + hc + 1 for hc in range(HC)}

            with self.block() as blk:
                if first:
                    Sa, Sb, Ssig = {}, {}, {}
                    n = s0
                    for hc in range(HC):
                        for which in range(2):
                            i = 2 * hc + which
                            n += 1
                            Sa[i] = n
                            n += 1
                            Sb[i] = n
                        if hc >= 1:
                            n += 1
                            Ssig[hc - 1] = n
                    n += 1
                    Ssig[HC - 1] = n

                    @blk.gpsimd
                    def _(g):
                        for i in range(2 * HC):
                            hc, which = i // 2, i % 2
                            if i >= 2:
                                self.wt(g, "sE", Sb[i - 2])
                            src = (self.w1s if which == 0 else self.w2s)[hc]
                            self.dtok(g.dma_start(
                                out=stg[i % 2][:].rearrange("p (c o) -> p c o", o=128),
                                in_=src), "Fa%d" % (i % 2))

                    @blk.scalar
                    def _(s):
                        for hc in range(HC):
                            for which in range(2):
                                i = 2 * hc + which
                                self.wt(s, "Fa%d" % (i % 2),
                                        (fa0_0 if i % 2 == 0 else fa1_0) + 16 * (i // 2 + 1))
                                inv = self.pcol[:, 11:12] if which == 0 else self.pcol[:, 12:13]
                                qb = (qb1 if which == 0 else qb2)[hc % 2]
                                if hc >= 2:
                                    self.wt(s, "tE", T2[hc - 2])
                                    self.wt(s, "Sp", sp0 + 16 * 2 * (hc - 1))
                                ins = s.activation(qb[:], stg[i % 2][:], AF.Sign,
                                                   scale=inv, bias=self.chalf[:, 0:1])
                                assert self.tok(ins, "sE") == Sa[i]
                                if i >= 2:
                                    self.wt(s, "vE", (Vq1 if i % 2 == 0 else Vq2)[(i - 2) // 2])
                                ins = s.activation(tsc[:, (i % 2) * 2048:(i % 2 + 1) * 2048],
                                                   stg[i % 2][:], AF.Sign,
                                                   scale=inv, bias=self.chalf[:, 1:2])
                                assert self.tok(ins, "sE") == Sb[i]
                            if hc >= 1:
                                h = hc - 1
                                self.wt(s, "tE", T1[h])
                                if h >= 2:
                                    self.wt(s, "vE", Vh[h - 2])
                                ins = s.activation(sig[:, (h % 2) * TT:(h % 2 + 1) * TT],
                                                   self.pA[:] if h % 2 == 0 else self.pC[:],
                                                   AF.Sigmoid, scale=self.chalf[:, 0:1])
                                assert self.tok(ins, "sE") == Ssig[h]
                        h = HC - 1
                        self.wt(s, "tE", T1[h])
                        self.wt(s, "vE", Vh[h - 2])
                        ins = s.activation(sig[:, (h % 2) * TT:(h % 2 + 1) * TT],
                                           self.pA[:] if h % 2 == 0 else self.pC[:],
                                           AF.Sigmoid, scale=self.chalf[:, 0:1])
                        assert self.tok(ins, "sE") == Ssig[h]

                    @blk.sync
                    def _(e):
                        for hc in range(HC):
                            self.wt(e, "vE", Vq1[hc])
                            self.dtok(e.dma_start(out=self.qw1_sp[hc],
                                                  in_=qb1[hc % 2][:]), "Sp")
                            self.wt(e, "vE", Vq2[hc])
                            self.dtok(e.dma_start(out=self.qw2_sp[hc],
                                                  in_=qb2[hc % 2][:]), "Sp")

                    @blk.vector
                    def _(v):
                        for hc in range(HC):
                            for which in range(2):
                                i = 2 * hc + which
                                self.wt(v, "sE", Sb[i])
                                qb = (qb1 if which == 0 else qb2)[hc % 2]
                                ins = v.tensor_add(qb[:], qb[:],
                                                   tsc[:, (i % 2) * 2048:(i % 2 + 1) * 2048])
                                got = self.tok(ins, "vE")
                                assert got == (Vq1 if which == 0 else Vq2)[hc]
                            self.wt(v, "sE", Ssig[hc])
                            self.wt(v, "tE", T2[hc])
                            ins = v.scalar_tensor_tensor(
                                hid_sb[:, hc * TT:(hc + 1) * TT],
                                self.pD[:] if hc % 2 == 0 else self.pB[:],
                                self.chalf[:, 0:1],
                                sig[:, (hc % 2) * TT:(hc % 2 + 1) * TT],
                                ALU.mult, ALU.mult)
                            assert self.tok(ins, "vE") == Vh[hc]
                else:
                    @blk.sync
                    def _(e):
                        for hc in range(HC):
                            if hc >= 2:
                                self.wt(e, "tE", T2[hc - 2])
                            self.dtok(e.dma_start(out=qb1[hc % 2][:],
                                                  in_=self.qw1_sp[hc]), "Q%d" % (hc % 2))
                            self.dtok(e.dma_start(out=qb2[hc % 2][:],
                                                  in_=self.qw2_sp[hc]), "Q%d" % (hc % 2))

                    @blk.vector
                    def _(v):
                        for hc in range(HC):
                            self.wt(v, "sE", Ssig[hc])
                            self.wt(v, "tE", T2[hc])
                            ins = v.scalar_tensor_tensor(
                                hid_sb[:, hc * TT:(hc + 1) * TT],
                                self.pD[:] if hc % 2 == 0 else self.pB[:],
                                self.chalf[:, 0:1],
                                sig[:, (hc % 2) * TT:(hc % 2 + 1) * TT],
                                ALU.mult, ALU.mult)
                            assert self.tok(ins, "vE") == Vh[hc]

                    @blk.scalar
                    def _(s):
                        for hc in range(HC):
                            self.wt(s, "tE", T1[hc])
                            if hc >= 2:
                                self.wt(s, "vE", Vh[hc - 2])
                            ins = s.activation(sig[:, (hc % 2) * TT:(hc % 2 + 1) * TT],
                                               self.pA[:] if hc % 2 == 0 else self.pC[:],
                                               AF.Sigmoid, scale=self.chalf[:, 0:1])
                            assert self.tok(ins, "sE") == Ssig[hc]

                @blk.tensor
                def _(t):
                    for hc in range(HC):
                        if first:
                            self.wt(t, "vE", Vq2[hc])
                        else:
                            self.wt(t, "Q%d" % (hc % 2),
                                    (q0_0 if hc % 2 == 0 else q1_0) + 32 * (hc // 2 + 1))
                        b1 = self.pA if hc % 2 == 0 else self.pC
                        b2 = self.pD if hc % 2 == 0 else self.pB
                        if hc >= 2:
                            self.wt(t, "sE", Ssig[hc - 2])
                            self.wt(t, "vE", Vh[hc - 2])
                        else:
                            self.use_bank(t, b1)
                            self.use_bank(t, b2)
                        for ci in range(CI):
                            ins = t.matmul(b1[:], qb1[hc % 2][:, ci * 128:(ci + 1) * 128],
                                           y_sb[:, ci * HALF + tt * TT:
                                                ci * HALF + tt * TT + TT],
                                           start=(ci == 0), stop=(ci == CI - 1))
                        assert self.tok(ins, "tE") == T1[hc]
                        for ci in range(CI):
                            ins = t.matmul(b2[:], qb2[hc % 2][:, ci * 128:(ci + 1) * 128],
                                           y_sb[:, ci * HALF + tt * TT:
                                                ci * HALF + tt * TT + TT],
                                           start=(ci == 0), stop=(ci == CI - 1))
                        assert self.tok(ins, "tE") == T2[hc]

            for b in [self.pA, self.pC]:
                self.rel_bank(b, "sE", C["sE"])
            for b in [self.pB, self.pD]:
                self.rel_bank(b, "vE", C["vE"])
            self.barrier()

    def ffn3_q(self, q, hid_sb, f_sb, first):
        nc, C = self.nc, self.C
        NP = 4
        with ExitStack() as S:
            stg0 = self.usb(S, "s3g0", [128, 1408], F32, side="right")
            stg1 = self.usb(S, "s3g1", [128, 1408], F32, side="right")
            qb0 = self.usb(S, "q3b0", [128, 1408], BF16, side="right")
            qb1_ = self.usb(S, "q3b1", [128, 1408], BF16, side="right")
            tsc = self.usb(S, "t3sc", [128, 4224], BF16, side="right")
            sqf = self.usb(S, "sqf", [128, 1024], BF16, side="right")
            stg = [stg0, stg1]
            qb = [qb0, qb1_]

            t0, v0 = C["tE"], C["vE"]
            s0_3 = C["sE"]
            sp3 = C["Sp"]
            fa30, fa31 = C["Fa0"], C["Fa1"]
            q30, q31 = C["Q0"], C["Q1"]
            Tg = {}
            n = 0
            for oc in range(CI):
                for pc in range(NP):
                    n += 1
                    Tg[(oc, pc)] = t0 + n
                n += 1
                Tg[("st", oc)] = t0 + n
            if first:
                Vq = {}
                Vf = {}
                n = 0
                for oc in range(CI):
                    for pc in range(NP):
                        n += 1
                        Vq[(oc, pc)] = v0 + n
                    n += 2
                    Vf[oc] = v0 + n
            else:
                Vf = {oc: v0 + 2 * (oc + 1) for oc in range(CI)}
            del v0

            with self.block() as blk:
                if first:
                    Sa3 = {i: s0_3 + 2 * i + 1 for i in range(CI * NP)}
                    Sb3 = {i: s0_3 + 2 * i + 2 for i in range(CI * NP)}

                    @blk.gpsimd
                    def _(g):
                        for oc in range(CI):
                            for pc in range(NP):
                                i = oc * NP + pc
                                if i >= 2:
                                    self.wt(g, "sE", Sb3[i - 2])
                                self.dtok(g.dma_start(
                                    out=stg[i % 2][:].rearrange("p (c o) -> p c o", o=128),
                                    in_=self.w3s[oc][:, pc * 11:(pc + 1) * 11, :]),
                                    "Fa%d" % (i % 2))

                    @blk.scalar
                    def _(s):
                        for oc in range(CI):
                            for pc in range(NP):
                                i = oc * NP + pc
                                self.wt(s, "Fa%d" % (i % 2),
                                        (fa30 if i % 2 == 0 else fa31) + 16 * (i // 2 + 1))
                                if i >= 2:
                                    self.wt(s, "tE", Tg[((i - 2) // NP, (i - 2) % NP)])
                                    self.wt(s, "Sp", sp3 + 16 * (i - 1))
                                ins = s.activation(qb[i % 2][:], stg[i % 2][:], AF.Sign,
                                                   scale=self.pcol[:, 13:14],
                                                   bias=self.chalf[:, 0:1])
                                assert self.tok(ins, "sE") == Sa3[i]
                                if i >= 2:
                                    self.wt(s, "vE", Vq[((i - 2) // NP, (i - 2) % NP)])
                                ins = s.activation(tsc[:, (i % 2) * 1408:(i % 2 + 1) * 1408],
                                                   stg[i % 2][:], AF.Sign,
                                                   scale=self.pcol[:, 13:14],
                                                   bias=self.chalf[:, 1:2])
                                assert self.tok(ins, "sE") == Sb3[i]

                    @blk.sync
                    def _(e):
                        for oc in range(CI):
                            for pc in range(NP):
                                i = oc * NP + pc
                                self.wt(e, "vE", Vq[(oc, pc)])
                                self.dtok(e.dma_start(
                                    out=self.qw3_sp[oc][:, pc * 1408:(pc + 1) * 1408],
                                    in_=qb[i % 2][:]), "Sp")

                    @blk.vector
                    def _(v):
                        for oc in range(CI):
                            for pc in range(NP):
                                i = oc * NP + pc
                                self.wt(v, "sE", Sb3[i])
                                ins = v.tensor_add(qb[i % 2][:], qb[i % 2][:],
                                                   tsc[:, (i % 2) * 1408:(i % 2 + 1) * 1408])
                                assert self.tok(ins, "vE") == Vq[(oc, pc)]
                            self.wt(v, "tE", Tg[(oc, NP - 1)])
                            ftile = f_sb[:, oc * TT:(oc + 1) * TT]
                            self.tok(v.tensor_copy(ftile,
                                     self.pE[:] if oc % 2 == 0 else self.pF[:]), "vE")
                            if oc >= 2:
                                self.wt(v, "tE", Tg[("st", oc - 2)])
                            ins = v.tensor_mul(sqf[:, (oc % 2) * TT:(oc % 2 + 1) * TT],
                                               ftile, ftile)
                            assert self.tok(ins, "vE") == Vf[oc]
                else:
                    @blk.sync
                    def _(e):
                        for oc in range(CI):
                            for pc in range(NP):
                                i = oc * NP + pc
                                if i >= 2:
                                    self.wt(e, "tE", Tg[((i - 2) // NP, (i - 2) % NP)])
                                self.dtok(e.dma_start(
                                    out=qb[i % 2][:],
                                    in_=self.qw3_sp[oc][:, pc * 1408:(pc + 1) * 1408]),
                                    "Q%d" % (i % 2))

                    @blk.vector
                    def _(v):
                        for oc in range(CI):
                            self.wt(v, "tE", Tg[(oc, NP - 1)])
                            ftile = f_sb[:, oc * TT:(oc + 1) * TT]
                            self.tok(v.tensor_copy(ftile,
                                     self.pE[:] if oc % 2 == 0 else self.pF[:]), "vE")
                            if oc >= 2:
                                self.wt(v, "tE", Tg[("st", oc - 2)])
                            ins = v.tensor_mul(sqf[:, (oc % 2) * TT:(oc % 2 + 1) * TT],
                                               ftile, ftile)
                            assert self.tok(ins, "vE") == Vf[oc]

                @blk.tensor
                def _(t):
                    for b in [self.pA, self.pB, self.pE, self.pF]:
                        self.use_bank(t, b)
                    for oc in range(CI):
                        fb = self.pE if oc % 2 == 0 else self.pF
                        if oc >= 2:
                            self.wt(t, "vE", Vf[oc - 2])
                        for pc in range(NP):
                            i = oc * NP + pc
                            if first:
                                self.wt(t, "vE", Vq[(oc, pc)])
                            else:
                                self.wt(t, "Q%d" % (i % 2),
                                        (q30 if i % 2 == 0 else q31) + 16 * (i // 2 + 1))
                            for j in range(11):
                                hc = pc * 11 + j
                                ins = t.matmul(fb[:],
                                               qb[i % 2][:, j * 128:(j + 1) * 128],
                                               hid_sb[:, hc * TT:(hc + 1) * TT],
                                               start=(hc == 0), stop=(hc == HC - 1))
                            assert self.tok(ins, "tE") == Tg[(oc, pc)]
                        self.wt(t, "vE", Vf[oc])
                        t.matmul(self.pA[0:1, :], self.ones_col[:],
                                 f_sb[:, oc * TT:(oc + 1) * TT],
                                 start=(oc == 0), stop=(oc == CI - 1))
                        ins = t.matmul(self.pB[0:1, :], self.ones_col[:],
                                       sqf[:, (oc % 2) * TT:(oc % 2 + 1) * TT],
                                       start=(oc == 0), stop=(oc == CI - 1))
                        assert self.tok(ins, "tE") == Tg[("st", oc)]

            self.f_stats_tok = Tg[("st", CI - 1)]
            for b in [self.pE, self.pF]:
                self.rel_bank(b, "vE", C["vE"])
            self.barrier()

    def final_q(self, q, lnattn, f_sb):
        nc, C = self.nc, self.C
        half, tt = q // 2, q % 2
        with ExitStack() as S:
            xstg0 = self.usb(S, "xf0", [128, TT], F32, side="right")
            xstg1 = self.usb(S, "xf1", [128, TT], F32, side="right")
            repf = self.usb(S, "repf", [128, 1024], F32, side="right")
            ya = self.usb(S, "ya", [128, TT], F32, side="right")
            yb = self.usb(S, "yb", [128, TT], F32, side="right")
            ost0 = self.usb(S, "ost0", [128, TT], F32, side="right")
            ost1 = self.usb(S, "ost1", [128, TT], F32, side="right")
            xstg = [xstg0, xstg1]
            ost = [ost0, ost1]
            h0 = half * HALF
            so0 = C["So"]
            lx0_0, lx1_0 = C["Lx0"], C["Lx1"]

            rm = self.rowf[0:1, 0:TT]
            rs = self.rowf[0:1, TT:2 * TT]
            ra = self.rowf[0:1, 2 * TT:3 * TT]

            with self.block() as blk:
                @blk.vector
                def _(v):
                    self.wt(v, "tE", self.f_stats_tok)
                    v.tensor_scalar_mul(rm, self.pA[0:1, :], INVD * 0.5)
                    v.tensor_scalar_mul(rs, self.pB[0:1, :], INVD * 0.25)
                    v.tensor_mul(ra, rm, rm)
                    self.tok(v.tensor_sub(rs, rs, ra), "vE")

                vvar = C["vE"]

                @blk.scalar
                def _(s):
                    self.wt(s, "vE", vvar)
                    s.activation(rs, rs, AF.Ln, bias=self.pcol[0:1, 15:16])
                    self.tok(s.activation(rs, rs, AF.Exp, scale=self.cst[0:1, 3:4]),
                             "sE")

                srow = C["sE"]

                @blk.vector
                def _(v):
                    self.wt(v, "sE", srow)
                    v.tensor_mul(ra, rm, rs)
                    v.tensor_scalar_mul(ra, ra, -1.0)
                    self.tok(v.tensor_scalar_mul(rs, rs, 0.5), "vE")

                vu3 = C["vE"]

                @blk.tensor
                def _(t):
                    self.wt(t, "vE", vu3)
                    self.use_bank(t, self.pC)
                    self.use_bank(t, self.pD)
                    t.matmul(self.pC[:], self.ones_row_f[:], rs, start=True, stop=True)
                    self.tok(t.matmul(self.pD[:], self.ones_row_f[:], ra,
                                      start=True, stop=True), "tE")

                trep = C["tE"]
                v0 = C["vE"]
                Vout = {ci: v0 + ci + 1 for ci in range(CI)}

                @blk.sync
                def _(e):
                    for ci in range(CI):
                        if ci >= 2:
                            self.wt(e, "vE", Vout[ci - 2])
                        self.dtok(e.dma_start(
                            out=xstg[ci % 2][:],
                            in_=self.xoT_t[ci][:, h0 + tt * TT:h0 + tt * TT + TT]),
                            "Lx%d" % (ci % 2))

                @blk.vector
                def _(v):
                    self.wt(v, "tE", trep)
                    v.tensor_copy(repf[:, 0:512], self.pC[:])
                    v.tensor_copy(repf[:, 512:1024], self.pD[:])
                    for ci in range(CI):
                        self.wt(v, "Lx%d" % (ci % 2),
                                (lx0_0 if ci % 2 == 0 else lx1_0) + 16 * (ci // 2 + 1))
                        if ci >= 2:
                            self.wt(v, "So", so0 + 16 * (ci - 1))
                        la = lnattn[:, ci * HALF + tt * TT: ci * HALF + tt * TT + TT]
                        ftile = f_sb[:, ci * TT:(ci + 1) * TT]
                        v.tensor_add(ya[:], la, repf[:, 512:1024])
                        v.tensor_mul(yb[:], ftile, repf[:, 0:512])
                        v.tensor_add(ya[:], ya[:], yb[:])
                        ins = v.tensor_add(ost[ci % 2][:], ya[:], xstg[ci % 2][:])
                        assert self.tok(ins, "vE") == Vout[ci]

                @blk.scalar
                def _(s):
                    for ci in range(CI):
                        self.wt(s, "vE", Vout[ci])
                        self.dtok(s.dma_start(
                            out=self.outT.rearrange("(c p) t -> c p t", p=128)[ci][
                                :, h0 + tt * TT:h0 + tt * TT + TT],
                            in_=ost[ci % 2][:]), "So")

            self.rel_bank(self.pA, "vE", C["vE"])
            self.rel_bank(self.pB, "vE", C["vE"])
            self.rel_bank(self.pC, "vE", C["vE"])
            self.rel_bank(self.pD, "vE", C["vE"])
            self.barrier()

    def build(self):
        nc = self.nc
        self._build_body()
        rem = nc.sbuf_bytes_remaining
        self.es.close()
        return nc, rem

    def _build_body(self):
        nc = self.nc
        self.consts_phase()

        with ExitStack() as SGP:
            self.gamma_start(SGP)
            for half in range(2):
                with ExitStack() as SL:
                    SWS = ExitStack()
                    ws = WStream(self, SWS, [(self.wk, 8), (self.wv, 8)])
                    SRx = ExitStack()
                    xn = self.usb(SRx, "xn", [128, CI * HALF], BF16, side="right")
                    self.x_phase(self.xpT_t, xn, half, ws=ws)
                    k_sb = self.usb(SL, "k_sb", [128, H * HALF], BF16, side="left")
                    v_sb = self.usb(SL, "v_sb", [128, H * HALF], BF16, side="left")
                    self.proj_phase(ws, xn, k_sb, "fk")
                    self.proj_phase(ws, xn, v_sb, "plain")
                    SRx.close()
                    SWS.close()
                    self.attn_half(half, None, k_sb, v_sb, None, full=False)
            self.gamma_tail()
        self.dbg_dump("gacc", self.gacc[:])
        self.dbg_dump("pcol", self.pcol[:])
        self.dbg_dump("rowf", self.rowf[:])

        first_ffn = True
        for half in range(2):
            SRL = ExitStack()
            SWS = ExitStack()
            ws = WStream(self, SWS,
                         [(self.wq, 8), (self.wk, 8), (self.wv, 8), (self.wo, 8)])
            SRo = ExitStack()
            with ExitStack() as SL3:
                SRx = ExitStack()
                xn = self.usb(SRx, "xn", [128, CI * HALF], BF16, side="right")
                self.x_phase(self.xoT_t, xn, half, ws=ws)
                self.dbg_dump("xn%d" % half, xn[:])
                q_sb = self.usb(SL3, "q_sb", [128, H * HALF], BF16, side="left")
                k_sb = self.usb(SL3, "k_sb", [128, H * HALF], BF16, side="left")
                v_sb = self.usb(SL3, "v_sb", [128, H * HALF], BF16, side="left")
                self.proj_phase(ws, xn, q_sb, "fq")
                self.proj_phase(ws, xn, k_sb, "fk")
                self.proj_phase(ws, xn, v_sb, "plain")
                SRx.close()
                o_sb = self.usb(SRo, "o_sb", [128, H * HALF], BF16, side="right")
                self.dbg_dump("qp%d" % half, q_sb[:])
                self.dbg_dump("kp%d" % half, k_sb[:])
                self.attn_half(half, q_sb, k_sb, v_sb, o_sb, full=True)
                self.dbg_dump("on%d" % half, o_sb[:])
            lnattn = self.usb(SRL, "lnattn", [128, CI * HALF], BF16, side="left")
            self.wo_phase(ws, o_sb, lnattn)
            SRo.close()
            SWS.close()
            self.dbg_dump("lnattn%d" % half, lnattn[:])
            with ExitStack() as SF:
                y_sb = self.usb(SF, "y_sb", [128, CI * HALF], BF16, side="left")
                hid_sb = self.usb(SF, "hid_sb", [128, HC * TT], BF16, side="left")
                f_sb = self.usb(SF, "f_sb", [128, CI * TT], BF16, side="left")
                self.y_phase(half, lnattn, y_sb)
                self.dbg_dump("yn%d" % half, y_sb[:])
                for tt in range(2):
                    q = half * 2 + tt
                    self.ffn12_q(q, y_sb, hid_sb, first_ffn)
                    self.dbg_dump("hid%d" % q, hid_sb[:])
                    self.ffn3_q(q, hid_sb, f_sb, first_ffn)
                    self.dbg_dump("f%d" % q, f_sb[:])
                    self.final_q(q, lnattn, f_sb)
                    first_ffn = False
            SRL.close()

        with self.block() as blk:
            @blk.scalar
            def _(s):
                self.wt(s, "So", self.C["So"])


def build_nc(dbg=()):
    kb = KB(dbg)
    nc, rem = kb.build()
    return nc


def _stage_weight_qk(w):
    wT = np.ascontiguousarray(np.asarray(w, np.float32).T)
    return np.ascontiguousarray(wT.reshape(CI, 128, CI, 128).transpose(2, 1, 0, 3))


def _stage_w12(w):
    wT = np.ascontiguousarray(np.asarray(w, np.float32).T)
    return np.ascontiguousarray(wT.reshape(CI, 128, HC, 128).transpose(2, 1, 0, 3))


def _stage_w3(w):
    wT = np.ascontiguousarray(np.asarray(w, np.float32).T)
    return np.ascontiguousarray(wT.reshape(HC, 128, CI, 128).transpose(2, 1, 0, 3))


def kernel(**inputs):
    x = np.asarray(inputs["L0_attn_x"], np.float32)
    B, L, _ = x.shape
    n_cores = 8

    wq = _stage_weight_qk(inputs["L0_attn_wq"])
    wk = _stage_weight_qk(inputs["L0_attn_wk"])
    wv = _stage_weight_qk(inputs["L0_attn_wv"])
    wo = _stage_weight_qk(inputs["L0_attn_wo"])
    w1s = _stage_w12(inputs["L0_ffn_w1"])
    w2s = _stage_w12(inputs["L0_ffn_w2"])
    w3s = _stage_w3(inputs["L0_ffn_w3"])
    fmw = np.ascontiguousarray(np.stack([
        np.asarray(inputs["L0_attn_fq1_w"], np.float32).T,
        np.asarray(inputs["L0_attn_fq2_w"], np.float32).T,
        np.asarray(inputs["L0_attn_fk1_w"], np.float32).T,
        np.asarray(inputs["L0_attn_fk2_w"], np.float32).T]))
    fmb = np.ascontiguousarray(np.stack([
        np.asarray(inputs["L0_attn_fq1_b"], np.float32),
        np.asarray(inputs["L0_attn_fq2_b"], np.float32),
        np.asarray(inputs["L0_attn_fk1_b"], np.float32),
        np.asarray(inputs["L0_attn_fk2_b"], np.float32),
        np.asarray(inputs["L0_attn_norm_w"], np.float32)]))
    cmask4 = np.ascontiguousarray(
        np.tile(np.triu(np.ones((128, 128), np.float32)), (1, 4)))
    cident = np.eye(128, dtype=np.float32)
    zeros_x = np.zeros((D, TOK), np.float32)

    shared = dict(wq=wq, wk=wk, wv=wv, wo=wo, w1s=w1s, w2s=w2s, w3s=w3s,
                  fmw=fmw, fmb=fmb, cmask4=cmask4, cident=cident)
    in_maps = []
    for c in range(n_cores):
        b, half = c // 2, c % 2
        xo = np.ascontiguousarray(x[b, half * TOK:(half + 1) * TOK, :].T)
        xp = np.ascontiguousarray(x[b, 0:TOK, :].T) if half == 1 else zeros_x
        m = dict(shared)
        m["xoT"] = xo
        m["xpT"] = xp
        in_maps.append(m)

    nc = build_nc()
    res = run_bass_kernel_spmd(nc, in_maps, core_ids=list(range(n_cores)))
    out = np.empty((B, L, D), np.float32)
    for c in range(n_cores):
        b, half = c // 2, c % 2
        out[b, half * TOK:(half + 1) * TOK, :] = res.results[c]["outT"].T
    return out
